# version 16
# speedup vs baseline: 1.0470x; 1.0019x over previous
import numpy as np
import ml_dtypes
import concourse.bass as bass
import concourse.mybir as mybir
from concourse.bass_utils import run_bass_kernel_spmd
from contextlib import ExitStack

F32 = mybir.dt.float32
BF16 = mybir.dt.bfloat16
U32 = mybir.dt.uint32
AF = mybir.ActivationFunctionType
ALU = mybir.AluOpType

NCORES = 8
D = 1024
HID = 2048
G4 = 4096
TLOC = 2048
THAL = TLOC + 3
KD = D // 128
KH = HID // 128
LR = float(np.float32(0.001))
EPS = 1e-5

A1_TILES = [(0, 512), (512, 512), (1024, 512), (1536, 512), (2048, 3)]
OWN_TILES = [(3, 512), (515, 512), (1027, 512), (1539, 512)]
T_TILES = [(0, 512), (512, 512), (1024, 512), (1536, 512)]


class Cnt:
    def __init__(self):
        self.v = 0
    def inc(self, n=1):
        self.v += n
        return self.v


def _r_pk(ap_2d, p=128):
    return ap_2d.rearrange("(k p) m -> p k m", p=p)



def build_A():
    nc = bass.Bass("TRN2", target_bir_lowering=False)
    xT = nc.dram_tensor("xT", [D, THAL], F32, kind="ExternalInput").ap()
    winT = nc.dram_tensor("winT", [D, G4], F32, kind="ExternalInput").ap()
    wgT = nc.dram_tensor("wgT", [HID, G4], F32, kind="ExternalInput").ap()
    conv_w_r = nc.dram_tensor("conv_w_r", [128, 64], F32, kind="ExternalInput").ap()
    conv_b_r = nc.dram_tensor("conv_b_r", [128, 16], F32, kind="ExternalInput").ap()
    bg_r = nc.dram_tensor("bg_r", [128, 32], F32, kind="ExternalInput").ap()
    fb_r = nc.dram_tensor("fb_r", [128, 16], F32, kind="ExternalInput").ap()
    gateT = nc.dram_tensor("gateT", [HID, TLOC], BF16, kind="ExternalOutput").ap()
    aT = nc.dram_tensor("aT", [HID, TLOC], F32, kind="ExternalOutput").ap()
    bT = nc.dram_tensor("bT", [HID, TLOC], F32, kind="ExternalOutput").ap()

    with ExitStack() as ctx:
        def sb(name, shape, dt=F32):
            return ctx.enter_context(nc.sbuf_tensor(name, shape, dt))
        def sem(name):
            return ctx.enter_context(nc.semaphore(name))

        cw_t = sb("cw_t", [128, 64])
        cb_t = sb("cb_t", [128, 16])
        bg_t = sb("bg_t", [128, 32])
        fb_t = sb("fb_t", [128, 16])
        c_col = sb("c_col", [128, 16])
        ones_bf = sb("ones_bf", [128, 1], BF16)
        ones_row = sb("ones_row", [1, 128])
        ones_col = sb("ones_col", [128, 1])
        sp1 = sb("sp1", [128, 16])
        sp2 = sb("sp2", [128, 16])
        xnT = sb("xnT", [128, KD * THAL], BF16)
        xcp = sb("xcp", [128, KH * TLOC], BF16)
        m_row = sb("m_row", [1, 512])
        q_row = sb("q_row", [1, 512])
        v_row = sb("v_row", [1, 512])
        s_row = sb("s_row", [1, 512])
        r_row = sb("r_row", [1, 512])
        pb = [ctx.enter_context(nc.psum_tensor(f"pb{i}", [128, 512], F32)) for i in range(8)]

        s_mm = sem("s_mm"); s_ev = sem("s_ev"); s_dv = sem("s_dv")
        cmm, cev, cdv = Cnt(), Cnt(), Cnt()
        s_c = sem("s_c")
        s_xf = [sem("s_xf0"), sem("s_xf1")]
        cxf = [Cnt(), Cnt()]
        s_w = [sem("s_w0"), sem("s_w1")]
        cw = [Cnt(), Cnt()]
        s_g = [sem("s_g0"), sem("s_g1")]
        cg = [Cnt(), Cnt()]
        s_ab = [sem("s_ab0"), sem("s_ab1")]
        cab = [Cnt(), Cnt()]

        blk = ctx.enter_context(nc.Block())

        @blk.sync
        def _(s):
            s.dma_start(out=cw_t[:], in_=conv_w_r).then_inc(s_c, 16)
            s.dma_start(out=cb_t[:], in_=conv_b_r).then_inc(s_c, 16)
            s.dma_start(out=bg_t[:], in_=bg_r).then_inc(s_c, 16)
            s.dma_start(out=fb_t[:], in_=fb_r).then_inc(s_c, 16)

        @blk.vector
        def _(v):
            v.memset(ones_bf[:], 1.0)
            v.memset(ones_row[:], 1.0)
            v.memset(ones_col[:], 1.0)
            v.drain().then_inc(s_dv, 1)
        memsets = cdv.inc()

        @blk.scalar
        def _(sc):
            sc.wait_ge(s_c, 64)
            sc.wait_ge(s_dv, memsets)
            sc.activation(out=sp1[:], in_=fb_t[:], func=AF.Abs)
            sc.drain()
            sc.activation(out=sp2[:], in_=sp1[:], func=AF.Exp, scale=-1.0)
            sc.drain()
            sc.activation(out=sp2[:], in_=sp2[:], func=AF.Ln, bias=ones_col[:])
            sc.drain()
            sc.activation(out=sp1[:], in_=fb_t[:], func=AF.Relu)
            sc.drain().then_inc(s_ev, 1)
        ln1p = cev.inc()

        @blk.vector
        def _(v):
            v.wait_ge(s_ev, ln1p)
            v.tensor_add(c_col[:], sp1[:], sp2[:])
            v.drain()
            v.tensor_single_scalar(c_col[:], c_col[:], -8.0, ALU.mult)
            v.drain().then_inc(s_dv, 1)
        ccol_done = cdv.inc()

        with ExitStack() as a1ctx:
            xf = [a1ctx.enter_context(nc.sbuf_tensor(f"xf{i}", [128, KD * 512], F32)) for i in range(2)]
            xbf = [a1ctx.enter_context(nc.sbuf_tensor(f"xbf{i}", [128, KD * 512], BF16)) for i in range(2)]
            xsq = [a1ctx.enter_context(nc.sbuf_tensor(f"xsq{i}", [128, KD * 512], BF16)) for i in range(2)]
            xn_done = {}
            for i, (c0, w) in enumerate(A1_TILES):
                bi = i % 2

                @blk.sync
                def _(s):
                    if i >= 2:
                        s.wait_ge(s_dv, xn_done[i - 2])
                    s.dma_start(out=xf[bi][:].rearrange("p (k t) -> p k t", k=KD)[:, :, 0:w],
                                in_=_r_pk(xT)[:, :, c0:c0 + w]).then_inc(s_xf[bi], 16)
                x_ld = cxf[bi].inc(16)

                @blk.vector
                def _(v):
                    v.wait_ge(s_xf[bi], x_ld)
                    v.tensor_copy(out=xbf[bi][:].rearrange("p (k t) -> p k t", k=KD)[:, :, 0:w],
                                  in_=xf[bi][:].rearrange("p (k t) -> p k t", k=KD)[:, :, 0:w])
                    v.drain().then_inc(s_dv, 1)
                cast_done = cdv.inc()

                @blk.scalar
                def _(sc):
                    sc.wait_ge(s_dv, cast_done)
                    sc.activation(out=xsq[bi][:].rearrange("p (k t) -> p k t", k=KD)[:, :, 0:w],
                                  in_=xbf[bi][:].rearrange("p (k t) -> p k t", k=KD)[:, :, 0:w],
                                  func=AF.Square)
                    sc.drain().then_inc(s_ev, 1)
                sq_done = cev.inc()

                @blk.tensor
                def _(t):
                    t.wait_ge(s_ev, sq_done)
                    if i >= 1:
                        t.wait_ge(s_dv, xn_done[i - 1])
                    for k in range(KD):
                        t.matmul(pb[0][0:1, 0:w], ones_bf[:], xbf[bi][:, k * 512:k * 512 + w],
                                 start=(k == 0), stop=(k == KD - 1))
                    last = None
                    for k in range(KD):
                        last = t.matmul(pb[1][0:1, 0:w], ones_bf[:], xsq[bi][:, k * 512:k * 512 + w],
                                        start=(k == 0), stop=(k == KD - 1))
                    last.then_inc(s_mm, 1)
                stats_mm = cmm.inc()

                @blk.vector
                def _(v):
                    v.wait_ge(s_mm, stats_mm)
                    v.tensor_single_scalar(m_row[:, 0:w], pb[0][0:1, 0:w], 1.0 / D, ALU.mult)
                    v.tensor_single_scalar(q_row[:, 0:w], pb[1][0:1, 0:w], 1.0 / D, ALU.mult)
                    v.drain()
                    v.tensor_mul(v_row[:, 0:w], m_row[:, 0:w], m_row[:, 0:w])
                    v.drain()
                    v.tensor_sub(v_row[:, 0:w], q_row[:, 0:w], v_row[:, 0:w])
                    v.drain()
                    v.tensor_single_scalar(v_row[:, 0:w], v_row[:, 0:w], EPS, ALU.add)
                    v.drain().then_inc(s_dv, 1)
                var_done = cdv.inc()

                @blk.scalar
                def _(sc):
                    sc.wait_ge(s_dv, var_done)
                    sc.activation(out=s_row[:, 0:w], in_=v_row[:, 0:w], func=AF.Sqrt)
                    sc.drain().then_inc(s_ev, 1)
                std_done = cev.inc()

                @blk.vector
                def _(v):
                    v.wait_ge(s_ev, std_done)
                    v.reciprocal(r_row[:, 0:w], s_row[:, 0:w])
                    v.drain().then_inc(s_dv, 1)
                rinv_done = cdv.inc()

                @blk.tensor
                def _(t):
                    t.wait_ge(s_dv, rinv_done)
                    t.matmul(pb[2][:, 0:w], ones_row[:], m_row[:, 0:w], start=True, stop=True)
                    last = t.matmul(pb[3][:, 0:w], ones_row[:], r_row[:, 0:w], start=True, stop=True)
                    last.then_inc(s_mm, 1)
                bc_mm = cmm.inc()

                @blk.vector
                def _(v):
                    v.wait_ge(s_mm, bc_mm)
                    for k in range(KD):
                        v.tensor_sub(xf[bi][:, k * 512:k * 512 + w],
                                     xf[bi][:, k * 512:k * 512 + w], pb[2][:, 0:w])
                        v.drain()
                        v.tensor_mul(xnT[:, k * THAL + c0:k * THAL + c0 + w],
                                     xf[bi][:, k * 512:k * 512 + w], pb[3][:, 0:w])
                        v.drain()
                    v.drain().then_inc(s_dv, 1)
                xn_done[i] = cdv.inc()
            a1_all_done = xn_done[len(A1_TILES) - 1]

        with ExitStack() as a2ctx:
            wsl = [a2ctx.enter_context(nc.sbuf_tensor(f"wsl{i}", [128, KD * 128], BF16)) for i in range(2)]
            xc_f32 = a2ctx.enter_context(nc.sbuf_tensor("xc_f32", [128, THAL], F32))
            acc = [a2ctx.enter_context(nc.sbuf_tensor(f"acc{i}", [128, TLOC], F32)) for i in range(2)]
            gst = [a2ctx.enter_context(nc.sbuf_tensor(f"gst{i}", [128, 512], BF16)) for i in range(2)]

            mt_order = list(range(16, 32)) + list(range(16))
            conv_done = {}
            slab_free = {}
            pb_last_ev = {0: 0, 1: 0}
            mmcnt = 0
            for oi, mt in enumerate(mt_order):
                wbi = oi % 2
                is_xc = mt >= 16
                ct = mt - 16
                tts = A1_TILES if is_xc else OWN_TILES

                @blk.gpsimd
                def _(g):
                    if oi >= 2:
                        g.wait_ge(s_mm, slab_free[oi - 2])
                    else:
                        g.wait_ge(s_dv, a1_all_done)
                    g.dma_start(out=wsl[wbi][:].rearrange("p (k m) -> p k m", k=KD),
                                in_=_r_pk(winT)[:, :, mt * 128:(mt + 1) * 128]).then_inc(s_w[wbi], 16)
                w_ld = cw[wbi].inc(16)

                for ti, (c0, w) in enumerate(tts):
                    par = mmcnt % 2
                    pbi = 4 + par
                    mmcnt += 1
                    psum_guard = pb_last_ev[par]

                    @blk.tensor
                    def _(t):
                        if ti == 0:
                            t.wait_ge(s_w[wbi], w_ld)
                            if oi == 0:
                                t.wait_ge(s_dv, a1_all_done)
                        if psum_guard:
                            t.wait_ge(s_ev, psum_guard)
                        last = None
                        for k in range(KD):
                            last = t.matmul(pb[pbi][:, 0:w], wsl[wbi][:, k * 128:(k + 1) * 128],
                                            xnT[:, k * THAL + c0:k * THAL + c0 + w],
                                            start=(k == 0), stop=(k == KD - 1))
                        last.then_inc(s_mm, 1)
                    mm_done = cmm.inc()

                    if is_xc:
                        @blk.scalar
                        def _(sc):
                            sc.wait_ge(s_mm, mm_done)
                            if ti == 0 and oi >= 1:
                                sc.wait_ge(s_dv, conv_done[ct - 1])
                            sc.copy(out=xc_f32[:, c0:c0 + w], in_=pb[pbi][:, 0:w])
                            sc.drain().then_inc(s_ev, 1)
                        pb_last_ev[par] = cev.inc()
                    else:
                        stage = ti % 2
                        st_guard = cg[stage].v

                        @blk.scalar
                        def _(sc):
                            sc.wait_ge(s_mm, mm_done)
                            if st_guard:
                                sc.wait_ge(s_g[stage], st_guard)
                            sc.copy(out=gst[stage][:], in_=pb[pbi][:, 0:w])
                            sc.drain().then_inc(s_ev, 1)
                        pb_last_ev[par] = cev.inc()
                        ev_here = pb_last_ev[par]

                        @blk.sync
                        def _(s):
                            s.wait_ge(s_ev, ev_here)
                            s.dma_start(out=gateT[mt * 128:(mt + 1) * 128, c0 - 3:c0 - 3 + w],
                                        in_=gst[stage][:]).then_inc(s_g[stage], 16)
                        cg[stage].inc(16)
                slab_free[oi] = cmm.v

                if is_xc:
                    my_ev = cev.v

                    @blk.vector
                    def _(v):
                        v.wait_ge(s_ev, my_ev)
                        v.tensor_single_scalar(acc[0][:], xc_f32[:, 0:TLOC], cw_t[:, 4 * ct + 0:4 * ct + 1], ALU.mult)
                        v.drain()
                        v.scalar_tensor_tensor(acc[1][:], xc_f32[:, 1:1 + TLOC], cw_t[:, 4 * ct + 1:4 * ct + 2],
                                               acc[0][:], ALU.mult, ALU.add)
                        v.drain()
                        v.scalar_tensor_tensor(acc[0][:], xc_f32[:, 2:2 + TLOC], cw_t[:, 4 * ct + 2:4 * ct + 3],
                                               acc[1][:], ALU.mult, ALU.add)
                        v.drain()
                        v.scalar_tensor_tensor(acc[1][:], xc_f32[:, 3:3 + TLOC], cw_t[:, 4 * ct + 3:4 * ct + 4],
                                               acc[0][:], ALU.mult, ALU.add)
                        v.drain()
                        v.tensor_single_scalar(xcp[:, ct * TLOC:(ct + 1) * TLOC], acc[1][:],
                                               cb_t[:, ct:ct + 1], ALU.add)
                        v.drain().then_inc(s_dv, 1)
                    conv_done[ct] = cdv.inc()
            a2_conv_all = cdv.v
            a2_end_mm = cmm.v
            a2_g0, a2_g1 = cg[0].v, cg[1].v

        with ExitStack() as a3ctx:
            wf_sl = [a3ctx.enter_context(nc.sbuf_tensor(f"wf{i}", [128, KH * 128], BF16)) for i in range(2)]
            wi_sl = [a3ctx.enter_context(nc.sbuf_tensor(f"wi{i}", [128, KH * 128], BF16)) for i in range(2)]
            sf_buf = a3ctx.enter_context(nc.sbuf_tensor("sf_buf", [128, 16 * 512], F32))
            si_buf = a3ctx.enter_context(nc.sbuf_tensor("si_buf", [128, 16 * 512], F32))
            tA = a3ctx.enter_context(nc.sbuf_tensor("tA", [128, 512], F32))
            tB = a3ctx.enter_context(nc.sbuf_tensor("tB", [128, 512], F32))
            tC = a3ctx.enter_context(nc.sbuf_tensor("tC", [128, 512], F32))
            a_st = [a3ctx.enter_context(nc.sbuf_tensor(f"a_st{i}", [128, 512], F32)) for i in range(2)]
            b_st = [a3ctx.enter_context(nc.sbuf_tensor(f"b_st{i}", [128, 512], F32)) for i in range(2)]

            slab_mm = {}
            sig_done = {}
            pass2_last = {}
            for tti, (c0, w) in enumerate(T_TILES):
                for ct in range(16):
                    wbi = ct % 2
                    gi = tti * 16 + ct

                    @blk.gpsimd
                    def _(g):
                        if gi >= 2:
                            g.wait_ge(s_mm, slab_mm[gi - 2])
                        else:
                            g.wait_ge(s_mm, a2_end_mm)
                            g.wait_ge(s_dv, a2_conv_all)
                            g.wait_ge(s_g[0], a2_g0)
                            g.wait_ge(s_g[1], a2_g1)
                        g.dma_start(out=wf_sl[wbi][:].rearrange("p (k m) -> p k m", k=KH),
                                    in_=_r_pk(wgT)[:, :, ct * 128:(ct + 1) * 128]).then_inc(s_w[wbi], 16)
                        g.dma_start(out=wi_sl[wbi][:].rearrange("p (k m) -> p k m", k=KH),
                                    in_=_r_pk(wgT)[:, :, HID + ct * 128:HID + (ct + 1) * 128]).then_inc(s_w[wbi], 16)
                    w_ld = cw[wbi].inc(32)

                    @blk.tensor
                    def _(t):
                        t.wait_ge(s_w[wbi], w_ld)
                        if gi >= 2:
                            t.wait_ge(s_ev, sig_done[gi - 2])
                        p_f = pb[4 + (gi % 2)]
                        p_i = pb[6 + (gi % 2)]
                        for k in range(KH):
                            t.matmul(p_f[:, 0:w], wf_sl[wbi][:, k * 128:(k + 1) * 128],
                                     xcp[:, k * TLOC + c0:k * TLOC + c0 + w],
                                     start=(k == 0), stop=(k == KH - 1))
                        last = None
                        for k in range(KH):
                            last = t.matmul(p_i[:, 0:w], wi_sl[wbi][:, k * 128:(k + 1) * 128],
                                            xcp[:, k * TLOC + c0:k * TLOC + c0 + w],
                                            start=(k == 0), stop=(k == KH - 1))
                        last.then_inc(s_mm, 1)
                    slab_mm[gi] = cmm.inc()

                    @blk.scalar
                    def _(sc):
                        sc.wait_ge(s_mm, slab_mm[gi])
                        if tti >= 1 and ct == 0:
                            sc.wait_ge(s_dv, pass2_last[tti - 1])
                        p_f = pb[4 + (gi % 2)]
                        p_i = pb[6 + (gi % 2)]
                        sc.activation(out=sf_buf[:, ct * 512:ct * 512 + w], in_=p_f[:, 0:w],
                                      func=AF.Sigmoid, bias=bg_t[:, ct:ct + 1])
                        sc.drain()
                        sc.activation(out=si_buf[:, ct * 512:ct * 512 + w], in_=p_i[:, 0:w],
                                      func=AF.Sigmoid, bias=bg_t[:, 16 + ct:17 + ct])
                        sc.drain().then_inc(s_ev, 1)
                    sig_done[gi] = cev.inc()

                for ct in range(16):
                    sbi = ct % 2
                    gi = tti * 16 + ct
                    ab_guard = cab[sbi].v

                    @blk.vector
                    def _(v):
                        v.wait_ge(s_ev, sig_done[tti * 16 + 15])
                        v.tensor_single_scalar(tA[:, 0:w], sf_buf[:, ct * 512:ct * 512 + w],
                                               c_col[:, ct:ct + 1], ALU.mult)
                        v.drain().then_inc(s_dv, 1)
                    t2_done = cdv.inc()

                    @blk.scalar
                    def _(sc):
                        sc.wait_ge(s_dv, t2_done)
                        if ab_guard:
                            sc.wait_ge(s_ab[sbi], ab_guard)
                        sc.activation(out=a_st[sbi][:, 0:w], in_=tA[:, 0:w], func=AF.Exp)
                        sc.drain().then_inc(s_ev, 1)
                    alpha_done = cev.inc()

                    @blk.vector
                    def _(v):
                        v.wait_ge(s_ev, alpha_done)
                        v.tensor_mul(tB[:, 0:w], a_st[sbi][:, 0:w], a_st[sbi][:, 0:w])
                        v.drain()
                        v.tensor_scalar(out=tB[:, 0:w], in0=tB[:, 0:w], scalar1=-1.0,
                                        scalar2=float(np.float32(1.0) + np.float32(1e-6)),
                                        op0=ALU.mult, op1=ALU.add)
                        v.drain().then_inc(s_dv, 1)
                    t4_done = cdv.inc()

                    @blk.scalar
                    def _(sc):
                        sc.wait_ge(s_dv, t4_done)
                        sc.activation(out=tC[:, 0:w], in_=tB[:, 0:w], func=AF.Ln)
                        sc.drain()
                        sc.activation(out=tC[:, 0:w], in_=tC[:, 0:w], func=AF.Exp, scale=0.5)
                        sc.drain().then_inc(s_ev, 1)
                    beta_done = cev.inc()

                    @blk.vector
                    def _(v):
                        v.wait_ge(s_ev, beta_done)
                        v.tensor_mul(tB[:, 0:w], tC[:, 0:w], si_buf[:, ct * 512:ct * 512 + w])
                        v.drain()
                        v.tensor_mul(b_st[sbi][:, 0:w], tB[:, 0:w],
                                     xcp[:, ct * TLOC + c0:ct * TLOC + c0 + w])
                        v.drain().then_inc(s_dv, 1)
                    bterm_done = cdv.inc()

                    @blk.sync
                    def _(s):
                        s.wait_ge(s_ev, alpha_done)
                        s.wait_ge(s_dv, bterm_done)
                        s.dma_start(out=aT[ct * 128:(ct + 1) * 128, c0:c0 + w],
                                    in_=a_st[sbi][:, 0:w]).then_inc(s_ab[sbi], 16)
                        s.dma_start(out=bT[ct * 128:(ct + 1) * 128, c0:c0 + w],
                                    in_=b_st[sbi][:, 0:w]).then_inc(s_ab[sbi], 16)
                    cab[sbi].inc(32)
                pass2_last[tti] = cdv.v

            @blk.sync
            def _(s):
                s.wait_ge(s_g[0], cg[0].v)
                s.wait_ge(s_g[1], cg[1].v)
                s.wait_ge(s_ab[0], cab[0].v)
                s.wait_ge(s_ab[1], cab[1].v)

    return nc



def _part_cols(vec, ncols):
    return np.ascontiguousarray(vec.reshape(ncols, 128).T)


def prep_A_inmaps(inputs):
    x = np.asarray(inputs["L0_attn_x"])
    W_in = np.asarray(inputs["L0_attn_W_in"])
    W_g = np.asarray(inputs["L0_attn_W_g"])
    conv_w = np.asarray(inputs["L0_attn_conv_w"])
    conv_b = np.asarray(inputs["L0_attn_conv_b"])
    b_g = np.asarray(inputs["L0_attn_b_g"])
    fb = np.asarray(inputs["L0_attn_forget_base"])

    winT = np.ascontiguousarray(W_in.T)
    wgT = np.ascontiguousarray(W_g.T)
    cwr = np.ascontiguousarray(conv_w.reshape(2048, 4).reshape(16, 128, 4).transpose(1, 0, 2).reshape(128, 64))
    cbr = _part_cols(conv_b, 16)
    bgr = _part_cols(b_g, 32)
    fbr = _part_cols(fb, 16)

    in_maps = []
    for c in range(NCORES):
        b, q = c // 4, c % 4
        t0 = q * TLOC
        xh = np.zeros((THAL, D), dtype=np.float32)
        lo = max(t0 - 3, 0)
        xh[3 - (t0 - lo):] = x[b, lo:t0 + TLOC]
        in_maps.append({
            "xT": np.ascontiguousarray(xh.T),
            "winT": winT, "wgT": wgT,
            "conv_w_r": cwr, "conv_b_r": cbr, "bg_r": bgr, "fb_r": fbr,
        })
    return in_maps


def kernel(**inputs):
    raise NotImplementedError("under construction")


def build_B():
    nc = bass.Bass("TRN2", target_bir_lowering=False)
    a_cm = nc.dram_tensor("a_cm", [512, 8192], F32, kind="ExternalInput").ap()
    b_cm = nc.dram_tensor("b_cm", [512, 8192], F32, kind="ExternalInput").ap()
    h_cm = nc.dram_tensor("h_cm", [512, 8192], BF16, kind="ExternalOutput").ap()

    with ExitStack() as ctx:
        at = ctx.enter_context(nc.sbuf_tensor("at", [128, 8192], F32))
        bt = ctx.enter_context(nc.sbuf_tensor("bt", [128, 8192], F32))
        hf = ctx.enter_context(nc.sbuf_tensor("hf", [128, 8192], F32))
        hb = ctx.enter_context(nc.sbuf_tensor("hb", [128, 8192], BF16))
        zinit = ctx.enter_context(nc.sbuf_tensor("zinit", [128, 1], F32))
        s_ld = ctx.enter_context(nc.semaphore("s_ld"))
        s_dv = ctx.enter_context(nc.semaphore("s_dv"))
        s_st = ctx.enter_context(nc.semaphore("s_st"))
        cld, cdv, cst = Cnt(), Cnt(), Cnt()
        blk = ctx.enter_context(nc.Block())

        @blk.vector
        def _(v):
            v.memset(zinit[:], 0.0)
            v.drain().then_inc(s_dv, 1)
        cdv.inc()

        for rt in range(4):
            @blk.sync
            def _(s):
                if rt >= 1:
                    s.wait_ge(s_dv, scan_done)
                s.dma_start(out=at[:], in_=a_cm[rt * 128:(rt + 1) * 128, :]).then_inc(s_ld, 16)
                s.dma_start(out=bt[:], in_=b_cm[rt * 128:(rt + 1) * 128, :]).then_inc(s_ld, 16)
            ld_done = cld.inc(32)

            @blk.vector
            def _(v):
                v.wait_ge(s_ld, ld_done)
                if rt >= 1:
                    v.wait_ge(s_st, st_done)
                v.tensor_tensor_scan(out=hf[:], data0=at[:], data1=bt[:], initial=zinit[:],
                                     op0=ALU.mult, op1=ALU.add)
                v.drain()
                v.tensor_copy(out=hb[:], in_=hf[:])
                v.drain().then_inc(s_dv, 1)
            scan_done = cdv.inc()

            @blk.sync
            def _(s):
                s.wait_ge(s_dv, scan_done)
                s.dma_start(out=h_cm[rt * 128:(rt + 1) * 128, :], in_=hb[:]).then_inc(s_st, 16)
            st_done = cst.inc(16)

        @blk.sync
        def _(s):
            s.wait_ge(s_st, cst.v)

    return nc


def reshard_AtoB(resA):
    in_maps = []
    for c2 in range(NCORES):
        b2 = c2 // 2
        ch0 = 512 * (c2 % 2) if False else None
        r0 = 512 * c2
        b = r0 // 2048
        ch = r0 % 2048
        a_sh = np.concatenate([np.asarray(resA[b * 4 + tq]["aT"])[ch:ch + 512, :] for tq in range(4)], axis=1)
        b_sh = np.concatenate([np.asarray(resA[b * 4 + tq]["bT"])[ch:ch + 512, :] for tq in range(4)], axis=1)
        in_maps.append({"a_cm": np.ascontiguousarray(a_sh), "b_cm": np.ascontiguousarray(b_sh)})
    return in_maps


def reshard_BtoC(resB):
    h_full = np.concatenate([np.asarray(resB[c2]["h_cm"]) for c2 in range(NCORES)], axis=0)
    out = []
    for c in range(NCORES):
        b, q = c // 4, c % 4
        out.append(np.ascontiguousarray(h_full[b * 2048:(b + 1) * 2048, q * 2048:(q + 1) * 2048]))
    return out


def build_C():
    nc = bass.Bass("TRN2", target_bir_lowering=False)
    xT = nc.dram_tensor("xT", [D, TLOC], F32, kind="ExternalInput").ap()
    gateT_i = nc.dram_tensor("gateT_i", [HID, TLOC], BF16, kind="ExternalInput").ap()
    hT_i = nc.dram_tensor("hT_i", [HID, TLOC], BF16, kind="ExternalInput").ap()
    woutT = nc.dram_tensor("woutT", [HID, D], F32, kind="ExternalInput").ap()
    wg0_i = nc.dram_tensor("wg0_i", [D, 32], F32, kind="ExternalInput").ap()
    ident_i = nc.dram_tensor("ident_i", [128, 128], F32, kind="ExternalInput").ap()
    x1T = nc.dram_tensor("x1T", [D, TLOC], F32, kind="ExternalOutput").ap()
    ln2T = nc.dram_tensor("ln2T", [D, TLOC], BF16, kind="ExternalOutput").ap()
    I_o = nc.dram_tensor("I_o", [32, 4 * TLOC], BF16, kind="ExternalOutput").ap()
    WI_o = nc.dram_tensor("WI_o", [32, 4 * TLOC], BF16, kind="ExternalOutput").ap()
    cnt_o = nc.dram_tensor("cnt_o", [32, 1], F32, kind="ExternalOutput").ap()

    with ExitStack() as ctx:
        def sb(name, shape, dt=F32):
            return ctx.enter_context(nc.sbuf_tensor(name, shape, dt))
        def sem(name):
            return ctx.enter_context(nc.semaphore(name))

        ident = sb("ident", [128, 128])
        ident_bf = sb("ident_bf", [128, 128], BF16)
        wg0 = sb("wg0", [128, KD * 32])
        ones_bf = sb("ones_bf", [128, 1], BF16)
        ones_row = sb("ones_row", [1, 128])
        it32 = sb("it32", [128, 32])
        m_row = sb("m_row", [1, 512]); q_row = sb("q_row", [1, 512])
        v_row = sb("v_row", [1, 512]); s_row = sb("s_row", [1, 512]); r_row = sb("r_row", [1, 512])
        ln2b = sb("ln2b", [128, KD * TLOC], BF16)
        sTf = sb("sTf", [32, 4 * 512])
        pbA = ctx.enter_context(nc.psum_tensor("pbA", [128, 512], F32))
        pbB = ctx.enter_context(nc.psum_tensor("pbB", [128, 512], F32))
        pbS = ctx.enter_context(nc.psum_tensor("pbS", [128, 512], F32))
        pbQ = ctx.enter_context(nc.psum_tensor("pbQ", [128, 512], F32))
        pbM = ctx.enter_context(nc.psum_tensor("pbM", [128, 512], F32))
        pbR = ctx.enter_context(nc.psum_tensor("pbR", [128, 512], F32))
        ptA = ctx.enter_context(nc.psum_tensor("ptA", [32, 128], BF16))
        ptB = ctx.enter_context(nc.psum_tensor("ptB", [32, 128], BF16))

        s_c = sem("s_c"); s_mm = sem("s_mm"); s_ev = sem("s_ev"); s_dv = sem("s_dv")
        s_w = [sem("s_w0"), sem("s_w1")]
        s_st = sem("s_st"); s_gp = sem("s_gp")
        s_gh = sem("s_gh"); s_xl = sem("s_xl")
        cmm, cev, cdv, cst, cgp = Cnt(), Cnt(), Cnt(), Cnt(), Cnt()
        cw = [Cnt(), Cnt()]

        blk = ctx.enter_context(nc.Block())

        @blk.sync
        def _(s):
            s.dma_start(out=ident[:], in_=ident_i).then_inc(s_c, 16)
            s.dma_start(out=wg0[:].rearrange("p (k m) -> p k m", k=KD),
                        in_=_r_pk(wg0_i)).then_inc(s_c, 16)
        CONSTS = 32

        @blk.vector
        def _(v):
            v.wait_ge(s_c, CONSTS)
            v.memset(ones_bf[:], 1.0)
            v.memset(ones_row[:], 1.0)
            v.tensor_copy(out=ident_bf[:], in_=ident[:])
            v.drain().then_inc(s_dv, 1)
        consts_done = cdv.inc()

        @blk.gpsimd
        def _(g):
            g.iota(it32[:], pattern=[[1, 32]], base=0, channel_multiplier=0,
                   allow_small_or_imprecise_dtypes=True)
            g.engine_nop().then_inc(s_gp, 1)
        iota_done = cgp.inc()

        c12ctx = ExitStack()
        mh = c12ctx.enter_context(nc.sbuf_tensor("mh", [128, KH * TLOC], BF16, side="right"))
        with ExitStack() as c1ctx:
            hbuf = c1ctx.enter_context(nc.sbuf_tensor("hbuf", [128, KH * TLOC], BF16, side="right"))
            gg = c1ctx.enter_context(nc.sbuf_tensor("gg", [128, TLOC], BF16, side="right"))

            @blk.sync
            def _(s):
                s.dma_start(out=mh[:].rearrange("p (k t) -> p k t", k=KH),
                            in_=_r_pk(gateT_i)).then_inc(s_gh, 16)
                s.dma_start(out=hbuf[:].rearrange("p (k t) -> p k t", k=KH),
                            in_=_r_pk(hT_i)).then_inc(s_gh, 16)
            GH = 32

            gelu_done = {}
            for ctt in range(KH):
                @blk.scalar
                def _(sc):
                    if ctt == 0:
                        sc.wait_ge(s_gh, GH)
                    if ctt >= 1:
                        sc.wait_ge(s_dv, mul_done)
                    sc.activation(out=gg[:], in_=mh[:, ctt * TLOC:(ctt + 1) * TLOC], func=AF.Gelu)
                    sc.drain().then_inc(s_ev, 1)
                gelu_done = cev.inc()

                @blk.vector
                def _(v):
                    v.wait_ge(s_ev, gelu_done)
                    v.tensor_mul(mh[:, ctt * TLOC:(ctt + 1) * TLOC], gg[:],
                                 hbuf[:, ctt * TLOC:(ctt + 1) * TLOC])
                    v.drain().then_inc(s_dv, 1)
                mul_done = cdv.inc()
            c1_done = cdv.v
            c1_ev = cev.v

        xbuf = ctx.enter_context(nc.sbuf_tensor("xbuf", [128, KD * TLOC], F32))
        with ExitStack() as c2ctx:
            wsl = [c2ctx.enter_context(nc.sbuf_tensor(f"wsl{i}", [128, KH * 128], BF16, side="right")) for i in range(2)]

            @blk.sync
            def _(s):
                s.wait_ge(s_dv, c1_done)
                s.dma_start(out=xbuf[:].rearrange("p (k t) -> p k t", k=KD),
                            in_=_r_pk(xT)).then_inc(s_xl, 16)
            x_full = 16

            slab_mm = {}
            add_done = {}
            pb_guard = {0: 0, 1: 0}
            for dt in range(KD):
                wbi = dt % 2
                slab_guard = slab_mm.get(dt - 2, 0)

                @blk.gpsimd
                def _(g):
                    if dt >= 2:
                        g.wait_ge(s_mm, slab_guard)
                    else:
                        g.wait_ge(s_dv, c1_done)
                    g.dma_start(out=wsl[wbi][:].rearrange("p (k m) -> p k m", k=KH),
                                in_=_r_pk(woutT)[:, :, dt * 128:(dt + 1) * 128]).then_inc(s_w[wbi], 16)
                w_ld = cw[wbi].inc(16)

                for ti, (c0, w) in enumerate(T_TILES):
                    par = (dt * 4 + ti) % 2
                    pbx = pbA if par == 0 else pbB
                    guard = pb_guard[par]

                    @blk.tensor
                    def _(t):
                        if ti == 0:
                            t.wait_ge(s_w[wbi], w_ld)
                        if guard:
                            t.wait_ge(s_dv, guard)
                        last = None
                        for k in range(KH):
                            last = t.matmul(pbx[:, 0:w], wsl[wbi][:, k * 128:(k + 1) * 128],
                                            mh[:, k * TLOC + c0:c0 + k * TLOC + w],
                                            start=(k == 0), stop=(k == KH - 1))
                        last.then_inc(s_mm, 1)
                    mm_done = cmm.inc()

                    @blk.vector
                    def _(v):
                        v.wait_ge(s_mm, mm_done)
                        if dt == 0 and ti == 0:
                            v.wait_ge(s_xl, x_full)
                        v.tensor_add(xbuf[:, dt * TLOC + c0:dt * TLOC + c0 + w],
                                     xbuf[:, dt * TLOC + c0:dt * TLOC + c0 + w], pbx[:, 0:w])
                        v.drain().then_inc(s_dv, 1)
                    pb_guard[par] = cdv.inc()
                    add_here = pb_guard[par]

                    @blk.sync
                    def _(s):
                        s.wait_ge(s_dv, add_here)
                        s.dma_start(out=x1T[dt * 128:(dt + 1) * 128, c0:c0 + w],
                                    in_=xbuf[:, dt * TLOC + c0:dt * TLOC + c0 + w]).then_inc(s_st, 16)
                    cst.inc(16)
                slab_mm[dt] = cmm.v
            c2_done = cdv.v
            c2_mm = cmm.v
        c12ctx.close()

        with ExitStack() as c3ctx:
            lf = [c3ctx.enter_context(nc.sbuf_tensor(f"lf{i}", [128, KD * 512], F32, side="right")) for i in range(2)]
            xb = c3ctx.enter_context(nc.sbuf_tensor("xb", [128, KD * 512], BF16, side="right"))
            xq = c3ctx.enter_context(nc.sbuf_tensor("xq", [128, KD * 512], BF16, side="right"))
            ln_step = {}
            for ti, (c0, w) in enumerate(T_TILES):
                bi = ti % 2

                @blk.vector
                def _(v):
                    if ti == 0:
                        v.wait_ge(s_dv, c2_done)
                        v.wait_ge(s_mm, c2_mm)
                    for k in range(KD):
                        v.tensor_copy(out=xb[:, k * 512:k * 512 + w],
                                      in_=xbuf[:, k * TLOC + c0:k * TLOC + c0 + w])
                    v.drain().then_inc(s_dv, 1)
                cast_done = cdv.inc()

                @blk.scalar
                def _(sc):
                    sc.wait_ge(s_dv, cast_done)
                    sc.activation(out=xq[:].rearrange("p (k t) -> p k t", k=KD)[:, :, 0:w],
                                  in_=xb[:].rearrange("p (k t) -> p k t", k=KD)[:, :, 0:w], func=AF.Square)
                    sc.drain().then_inc(s_ev, 1)
                sq_done = cev.inc()

                @blk.tensor
                def _(t):
                    t.wait_ge(s_ev, sq_done)
                    if ti >= 1:
                        t.wait_ge(s_dv, ln_step[ti - 1])
                    for k in range(KD):
                        t.matmul(pbS[0:1, 0:w], ones_bf[:], xb[:, k * 512:k * 512 + w],
                                 start=(k == 0), stop=(k == KD - 1))
                    last = None
                    for k in range(KD):
                        last = t.matmul(pbQ[0:1, 0:w], ones_bf[:], xq[:, k * 512:k * 512 + w],
                                        start=(k == 0), stop=(k == KD - 1))
                    last.then_inc(s_mm, 1)
                stats_mm = cmm.inc()

                @blk.vector
                def _(v):
                    v.wait_ge(s_mm, stats_mm)
                    v.tensor_single_scalar(m_row[:, 0:w], pbS[0:1, 0:w], 1.0 / D, ALU.mult)
                    v.tensor_single_scalar(q_row[:, 0:w], pbQ[0:1, 0:w], 1.0 / D, ALU.mult)
                    v.drain()
                    v.tensor_mul(v_row[:, 0:w], m_row[:, 0:w], m_row[:, 0:w])
                    v.drain()
                    v.tensor_sub(v_row[:, 0:w], q_row[:, 0:w], v_row[:, 0:w])
                    v.drain()
                    v.tensor_single_scalar(v_row[:, 0:w], v_row[:, 0:w], EPS, ALU.add)
                    v.drain().then_inc(s_dv, 1)
                var_done = cdv.inc()

                @blk.scalar
                def _(sc):
                    sc.wait_ge(s_dv, var_done)
                    sc.activation(out=s_row[:, 0:w], in_=v_row[:, 0:w], func=AF.Sqrt)
                    sc.drain().then_inc(s_ev, 1)
                std_done = cev.inc()

                @blk.vector
                def _(v):
                    v.wait_ge(s_ev, std_done)
                    v.reciprocal(r_row[:, 0:w], s_row[:, 0:w])
                    v.drain().then_inc(s_dv, 1)
                rinv_done = cdv.inc()

                @blk.tensor
                def _(t):
                    t.wait_ge(s_dv, rinv_done)
                    t.matmul(pbM[:, 0:w], ones_row[:], m_row[:, 0:w], start=True, stop=True)
                    last = t.matmul(pbR[:, 0:w], ones_row[:], r_row[:, 0:w], start=True, stop=True)
                    last.then_inc(s_mm, 1)
                bc_mm = cmm.inc()

                @blk.vector
                def _(v):
                    v.wait_ge(s_mm, bc_mm)
                    for k in range(KD):
                        v.tensor_sub(lf[bi][:, k * 512:k * 512 + w],
                                     xbuf[:, k * TLOC + c0:k * TLOC + c0 + w], pbM[:, 0:w])
                        v.drain()
                        v.tensor_mul(lf[bi][:, k * 512:k * 512 + w],
                                     lf[bi][:, k * 512:k * 512 + w], pbR[:, 0:w])
                        v.drain()
                        v.tensor_copy(out=ln2b[:, k * TLOC + c0:k * TLOC + c0 + w],
                                      in_=lf[bi][:, k * 512:k * 512 + w])
                        v.drain()
                    v.drain().then_inc(s_dv, 1)
                ln_f32 = cdv.inc()

                @blk.tensor
                def _(t):
                    t.wait_ge(s_dv, ln_f32)
                    if ti >= 1:
                        t.wait_ge(s_ev, prev_sT)
                    last = None
                    for k in range(KD):
                        last = t.matmul(pbA[0:32, 0:w], wg0[:, k * 32:(k + 1) * 32],
                                        lf[bi][:, k * 512:k * 512 + w],
                                        start=(k == 0), stop=(k == KD - 1))
                    last.then_inc(s_mm, 1)
                sc_mm = cmm.inc()

                @blk.scalar
                def _(sc):
                    sc.wait_ge(s_mm, sc_mm)
                    sc.copy(out=sTf[:, ti * 512:ti * 512 + w], in_=pbA[0:32, 0:w])
                    sc.drain().then_inc(s_ev, 1)
                ln_step[ti] = cdv.v
                prev_sT = cev.inc()

                @blk.sync
                def _(s):
                    s.wait_ge(s_dv, ln_f32)
                    s.dma_start(out=ln2T[:, c0:c0 + w].rearrange("(k p) t -> p k t", p=128),
                                in_=ln2b[:].rearrange("p (k t) -> p k t", k=KD)[:, :, c0:c0 + w]).then_inc(s_st, 16)
                cst.inc(16)
            c3_ev = cev.v
            c3_dv = cdv.v
            del prev_sT

        with ExitStack() as c5ctx:
            Ibuf = c5ctx.enter_context(nc.sbuf_tensor("Ibuf", [32, 4 * TLOC], BF16, side="right"))
            WIbuf = c5ctx.enter_context(nc.sbuf_tensor("WIbuf", [32, 4 * TLOC], BF16, side="right"))
            lg = c5ctx.enter_context(nc.sbuf_tensor("lg", [128, 32], F32, side="right"))
            srow = c5ctx.enter_context(nc.sbuf_tensor("srow", [128, 1], F32, side="right"))
            top8 = c5ctx.enter_context(nc.sbuf_tensor("top8", [128, 8], F32, side="right"))
            idxu = c5ctx.enter_context(nc.sbuf_tensor("idxu", [128, 8], U32, side="right"))
            idxf = c5ctx.enter_context(nc.sbuf_tensor("idxf", [128, 8], F32, side="right"))
            oh = c5ctx.enter_context(nc.sbuf_tensor("oh", [128, 32], BF16, side="right"))
            ohw = c5ctx.enter_context(nc.sbuf_tensor("ohw", [128, 32], BF16, side="right"))
            wcol = c5ctx.enter_context(nc.sbuf_tensor("wcol", [128, 4], F32, side="right"))
            tmp32 = c5ctx.enter_context(nc.sbuf_tensor("tmp32", [128, 32], F32, side="right"))
            cnt_t = c5ctx.enter_context(nc.sbuf_tensor("cnt_t", [32, 1], F32, side="right"))

            for tk in range(16):
                ti = tk // 4
                off = (tk % 4) * 128

                @blk.tensor
                def _(t):
                    if tk == 0:
                        t.wait_ge(s_ev, c3_ev)
                        t.wait_ge(s_gp, iota_done)
                    else:
                        t.wait_ge(s_dv, blk_done)
                    t.transpose(pbB[:, 0:32], sTf[0:32, ti * 512 + off:ti * 512 + off + 128],
                                ident[0:32, 0:32]).then_inc(s_mm, 1)
                tr_mm = cmm.inc()

                @blk.scalar
                def _(sc):
                    sc.wait_ge(s_mm, tr_mm)
                    sc.copy(out=lg[:], in_=pbB[:, 0:32])
                    sc.drain().then_inc(s_ev, 1)
                lg_done = cev.inc()

                @blk.vector
                def _(v):
                    v.wait_ge(s_ev, lg_done)
                    v.reduce_max(srow[:], lg[:], axis=mybir.AxisListType.X)
                    v.drain()
                    v.tensor_single_scalar(srow[:], srow[:], -1.0, ALU.mult)
                    v.drain().then_inc(s_dv, 1)
                mx_done = cdv.inc()

                @blk.scalar
                def _(sc):
                    sc.wait_ge(s_dv, mx_done)
                    sc.activation(out=lg[:], in_=lg[:], func=AF.Exp, bias=srow[:])
                    sc.drain().then_inc(s_ev, 1)
                ex_done = cev.inc()

                @blk.vector
                def _(v):
                    v.wait_ge(s_ev, ex_done)
                    v.reduce_sum(srow[:], lg[:], axis=mybir.AxisListType.X)
                    v.drain()
                    v.reciprocal(srow[:], srow[:])
                    v.drain()
                    v.tensor_single_scalar(lg[:], lg[:], srow[:], ALU.mult)
                    v.drain()
                    v.max(out=top8[:], in_=lg[:])
                    v.drain()
                    v.max_index(out=idxu[:], in_max=top8[:], in_values=lg[:])
                    v.drain()
                    v.tensor_copy(out=idxf[:], in_=idxu[:])
                    v.drain().then_inc(s_dv, 1)
                top_done = cdv.inc()

                for k in range(4):
                    @blk.vector
                    def _(v):
                        if k == 0:
                            v.wait_ge(s_dv, top_done)
                        else:
                            v.wait_ge(s_mm, ohtr_mm)
                        v.tensor_single_scalar(oh[:], it32[:], idxf[:, k:k + 1], ALU.is_equal)
                        v.drain()
                        v.tensor_mul(tmp32[:], lg[:], oh[:])
                        v.drain()
                        v.reduce_sum(wcol[:, k:k + 1], tmp32[:], axis=mybir.AxisListType.X)
                        v.drain()
                        v.tensor_single_scalar(ohw[:], oh[:], wcol[:, k:k + 1], ALU.mult)
                        v.drain().then_inc(s_dv, 1)
                    oh_done = cdv.inc()

                    @blk.tensor
                    def _(t):
                        t.wait_ge(s_dv, oh_done)
                        if tk > 0 or k > 0:
                            t.wait_ge(s_ev, tr_ev)
                        t.transpose(ptA[:, :], oh[:], ident_bf[:])
                        last = t.transpose(ptB[:, :], ohw[:], ident_bf[:])
                        last.then_inc(s_mm, 1)
                    ohtr_mm = cmm.inc()

                    @blk.scalar
                    def _(sc):
                        sc.wait_ge(s_mm, ohtr_mm)
                        sc.copy(out=Ibuf[:, tk * 512 + k:tk * 512 + k + 509:4], in_=ptA[:, :])
                        sc.drain()
                        sc.copy(out=WIbuf[:, tk * 512 + k:tk * 512 + k + 509:4], in_=ptB[:, :])
                        sc.drain().then_inc(s_ev, 1)
                    tr_ev = cev.inc()
                blk_done = cdv.v

            @blk.vector
            def _(v):
                v.wait_ge(s_ev, cev.v)
                v.reduce_sum(cnt_t[:], Ibuf[:], axis=mybir.AxisListType.X)
                v.drain().then_inc(s_dv, 1)
            cnt_done = cdv.inc()

            @blk.sync
            def _(s):
                s.wait_ge(s_dv, cnt_done)
                s.dma_start(out=I_o, in_=Ibuf[:]).then_inc(s_st, 16)
                s.dma_start(out=WI_o, in_=WIbuf[:]).then_inc(s_st, 16)
                s.dma_start(out=cnt_o, in_=cnt_t[:]).then_inc(s_st, 16)
            cst.inc(48)

        @blk.sync
        def _(s):
            s.wait_ge(s_st, cst.v)

    return nc


def build_D():
    nc = bass.Bass("TRN2", target_bir_lowering=False)
    ln2_i = nc.dram_tensor("ln2_i", [D, TLOC], BF16, kind="ExternalInput").ap()
    x1_i = nc.dram_tensor("x1_i", [D, TLOC], F32, kind="ExternalInput").ap()
    I_i = nc.dram_tensor("I_i", [32, 4 * TLOC], BF16, kind="ExternalInput").ap()
    WI_i = nc.dram_tensor("WI_i", [32, 4 * TLOC], BF16, kind="ExternalInput").ap()
    cnts_i = nc.dram_tensor("cnts_i", [8, 32], F32, kind="ExternalInput").ap()
    mask_i = nc.dram_tensor("mask_i", [8, 1], F32, kind="ExternalInput").ap()
    W1_i = nc.dram_tensor("W1_i", [32, D, 512], F32, kind="ExternalInput").ap()
    W3_i = nc.dram_tensor("W3_i", [32, D, 512], F32, kind="ExternalInput").ap()
    W2_i = nc.dram_tensor("W2_i", [32, 512, D], F32, kind="ExternalInput").ap()
    Ws1_i = nc.dram_tensor("Ws1_i", [D, D], F32, kind="ExternalInput").ap()
    Ws3_i = nc.dram_tensor("Ws3_i", [D, D], F32, kind="ExternalInput").ap()
    Ws2_i = nc.dram_tensor("Ws2_i", [D, D], F32, kind="ExternalInput").ap()
    ident_i = nc.dram_tensor("ident_i", [128, 128], F32, kind="ExternalInput").ap()
    x2T = nc.dram_tensor("x2T", [D, TLOC], F32, kind="ExternalOutput").ap()
    ln3T = nc.dram_tensor("ln3T", [D, TLOC], BF16, kind="ExternalOutput").ap()
    pref_d = nc.dram_tensor("pref_d", [1, 32], F32)
    cw_d = nc.dram_tensor("cw_d", [32, TLOC], F32)

    CAPC = 8192.0
    DT8 = [(i * 256, 256) for i in range(8)]

    with ExitStack() as ctx:
        def sb(name, shape, dt=F32, side="left"):
            return ctx.enter_context(nc.sbuf_tensor(name, shape, dt, side=side))
        def sem(name):
            return ctx.enter_context(nc.semaphore(name))

        ln2b = sb("ln2b", [128, KD * TLOC], BF16)
        ysb = sb("ysb", [128, KD * TLOC], BF16)
        cwrow = [sb(f"cwrow{i}", [1, TLOC], BF16) for i in range(2)]
        ident_bf = sb("ident_bf", [128, 128], BF16)
        ones_bf = sb("ones_bf", [128, 1], BF16)
        ones_row_bf = sb("ones_row_bf", [1, 128], BF16)
        ones5 = sb("ones5", [1, 512], BF16)
        ones_row = sb("ones_row", [1, 128])
        prefc = sb("prefc", [32, 1])
        m_row = sb("m_row", [1, 512]); q_row = sb("q_row", [1, 512])
        v_row = sb("v_row", [1, 512]); sd_row = sb("sd_row", [1, 512]); r_row = sb("r_row", [1, 512])
        pbA = ctx.enter_context(nc.psum_tensor("pbA", [128, 512], F32))
        pbB = ctx.enter_context(nc.psum_tensor("pbB", [128, 512], F32))
        pbC = ctx.enter_context(nc.psum_tensor("pbC", [128, 512], F32))
        pbD = ctx.enter_context(nc.psum_tensor("pbD", [128, 512], F32))
        pbY = [ctx.enter_context(nc.psum_tensor(f"pbY{i}", [128, 512], F32)) for i in range(4)]

        s_c = sem("s_c"); s_mm = sem("s_mm"); s_ev = sem("s_ev"); s_dv = sem("s_dv")
        s_w = [sem("s_w0"), sem("s_w1")]
        s_st = sem("s_st")
        s_x = [sem("s_x0"), sem("s_x1")]
        s_sx = [sem("s_sx0"), sem("s_sx1")]
        s_sl = sem("s_sl")
        cmm, cev, cdv, cst = Cnt(), Cnt(), Cnt(), Cnt()
        cw_c = [Cnt(), Cnt()]
        cx_c = [Cnt(), Cnt()]
        csx = [Cnt(), Cnt()]
        csl = Cnt()

        blk = ctx.enter_context(nc.Block())

        with ExitStack() as d1ctx:
            cnts = d1ctx.enter_context(nc.sbuf_tensor("cnts", [8, 32], F32, side="right"))
            maskb = d1ctx.enter_context(nc.sbuf_tensor("maskb", [8, 1], F32, side="right"))
            prow = d1ctx.enter_context(nc.sbuf_tensor("prow", [1, 32], F32, side="right"))
            Ib = d1ctx.enter_context(nc.sbuf_tensor("Ib", [32, 4 * TLOC], BF16, side="right"))
            WIb = d1ctx.enter_context(nc.sbuf_tensor("WIb", [32, 4 * TLOC], BF16, side="right"))
            If = d1ctx.enter_context(nc.sbuf_tensor("If", [32, 2048], F32, side="right"))
            zf = d1ctx.enter_context(nc.sbuf_tensor("zf", [32, 2048], F32, side="right"))
            rk = d1ctx.enter_context(nc.sbuf_tensor("rk", [32, 2048], F32, side="right"))
            tmpw = d1ctx.enter_context(nc.sbuf_tensor("tmpw", [32, 512], F32, side="right"))
            carry = d1ctx.enter_context(nc.sbuf_tensor("carry", [32, 1], F32, side="right"))
            cwf = d1ctx.enter_context(nc.sbuf_tensor("cwf", [32, TLOC], F32, side="right"))

            @blk.gpsimd
            def _(g):
                g.dma_start(out=ident_bf[:], in_=ident_i).then_inc(s_w[0], 16)
            ID_LD = cw_c[0].inc(16)

            @blk.sync
            def _(s):
                s.dma_start(out=cnts[:], in_=cnts_i).then_inc(s_c, 16)
                s.dma_start(out=maskb[:], in_=mask_i).then_inc(s_c, 16)
                s.dma_start(out=Ib[:], in_=I_i).then_inc(s_c, 16)
                s.dma_start(out=WIb[:], in_=WI_i).then_inc(s_c, 16)
                s.dma_start(out=ln2b[:].rearrange("p (k t) -> p k t", k=KD),
                            in_=_r_pk(ln2_i)).then_inc(s_c, 16)
            D1_LD = 80

            @blk.tensor
            def _(t):
                t.wait_ge(s_c, D1_LD)
                t.matmul(pbA[0:1, 0:32], maskb[:], cnts[:], start=True, stop=True).then_inc(s_mm, 1)
            pref_mm = cmm.inc()

            @blk.scalar
            def _(sc):
                sc.wait_ge(s_mm, pref_mm)
                sc.copy(out=prow[:], in_=pbA[0:1, 0:32])
                sc.drain().then_inc(s_ev, 1)
            prow_done = cev.inc()

            @blk.sync
            def _(s):
                s.wait_ge(s_ev, prow_done)
                s.dma_start(out=pref_d[:, :], in_=prow[:]).then_inc(s_c, 16)
                s.wait_ge(s_c, D1_LD + 16)
                s.dma_start(out=prefc[:], in_=pref_d.rearrange("a b -> b a")).then_inc(s_c, 16)
            PREF_LD = D1_LD + 32

            @blk.vector
            def _(v):
                v.memset(ones_bf[:], 1.0)
                v.memset(ones_row_bf[:], 1.0)
                v.memset(ones_row[:], 1.0)
                v.memset(zf[:], 0.0)
                v.wait_ge(s_c, PREF_LD)
                v.tensor_copy(out=carry[:], in_=prefc[:])
                v.drain()
                for q in range(4):
                    p0 = q * 2048
                    v.tensor_copy(out=If[:], in_=Ib[:, p0:p0 + 2048])
                    v.drain()
                    v.tensor_tensor_scan(out=rk[:], data0=zf[:], data1=If[:], initial=carry[:],
                                         op0=ALU.add, op1=ALU.add)
                    v.drain()
                    v.tensor_copy(out=carry[:], in_=rk[:, 2047:2048])
                    v.drain()
                    v.tensor_single_scalar(rk[:], rk[:], CAPC, ALU.is_le)
                    v.drain()
                    for k in range(4):
                        if k == 0:
                            v.tensor_mul(cwf[:, q * 512:(q + 1) * 512],
                                         WIb[:, p0 + k:p0 + 2045 + k:4], rk[:, k:2045 + k:4])
                        else:
                            v.tensor_mul(tmpw[:], WIb[:, p0 + k:p0 + 2045 + k:4], rk[:, k:2045 + k:4])
                            v.drain()
                            v.tensor_add(cwf[:, q * 512:(q + 1) * 512],
                                         cwf[:, q * 512:(q + 1) * 512], tmpw[:])
                        v.drain()
                v.drain().then_inc(s_dv, 1)
            cw_pre = cdv.inc()

            @blk.sync
            def _(s):
                s.wait_ge(s_dv, cw_pre)
                s.dma_start(out=cw_d[:, :], in_=cwf[:]).then_inc(s_c, 16)
            CW_ST = PREF_LD + 16

            @blk.vector
            def _(v):
                v.wait_ge(s_c, CW_ST)
                v.engine_nop().then_inc(s_dv, 1)
            cw_done = cdv.inc()

        with ExitStack() as d2ctx:
            ws1 = d2ctx.enter_context(nc.sbuf_tensor("ws1", [128, KD * D], BF16, side="right"))
            ws3 = d2ctx.enter_context(nc.sbuf_tensor("ws3", [128, KD * D], BF16, side="right"))
            ws2 = d2ctx.enter_context(nc.sbuf_tensor("ws2", [128, KD * D], BF16, side="right"))
            h3b = d2ctx.enter_context(nc.sbuf_tensor("h3b", [128, KD * 256], BF16, side="right"))
            asb = d2ctx.enter_context(nc.sbuf_tensor("asb", [128, KD * 256], BF16, side="right"))

            @blk.gpsimd
            def _(g):
                g.wait_ge(s_dv, cw_done)
                g.dma_start(out=ws1[:].rearrange("p (k m) -> p k m", k=KD),
                            in_=_r_pk(Ws1_i)).then_inc(s_w[0], 16)
                g.dma_start(out=ws3[:].rearrange("p (k m) -> p k m", k=KD),
                            in_=_r_pk(Ws3_i)).then_inc(s_w[0], 16)
                g.dma_start(out=ws2[:].rearrange("p (k m) -> p k m", k=KD),
                            in_=_r_pk(Ws2_i)).then_inc(s_w[0], 16)
            ws_ld = cw_c[0].inc(48)

            pb_guard = {0: None, 1: None}
            prev_ymm = 0
            prev_yev = None
            for ti, (c0, w) in enumerate(DT8):
                for phase in range(2):
                    for m in range(KD):
                        par = m % 2
                        pbx = pbA if par == 0 else pbB
                        guard = pb_guard[par]

                        @blk.tensor
                        def _(t):
                            if ti == 0 and phase == 0 and m == 0:
                                t.wait_ge(s_w[0], ws_ld)
                                t.wait_ge(s_dv, cw_done)
                            if phase == 0 and m == 0 and ti > 0:
                                t.wait_ge(s_mm, prev_ymm)
                            if guard:
                                t.wait_ge(s_dv if guard[0] == "dv" else s_ev, guard[1])
                            wsl = ws3 if phase == 0 else ws1
                            last = None
                            for k in range(KD):
                                last = t.matmul(pbx[:, 0:w], wsl[:, k * D + m * 128:k * D + (m + 1) * 128],
                                                ln2b[:, k * TLOC + c0:k * TLOC + c0 + w],
                                                start=(k == 0), stop=(k == KD - 1))
                            last.then_inc(s_mm, 1)
                        mm_done = cmm.inc()

                        if phase == 0:
                            @blk.vector
                            def _(v):
                                v.wait_ge(s_mm, mm_done)
                                v.tensor_copy(out=h3b[:, m * 256:m * 256 + w], in_=pbx[:, 0:w])
                                v.drain().then_inc(s_dv, 1)
                            pb_guard[par] = ("dv", cdv.inc())
                        else:
                            @blk.scalar
                            def _(sc):
                                sc.wait_ge(s_mm, mm_done)
                                sc.activation(out=asb[:, m * 256:m * 256 + w], in_=pbx[:, 0:w],
                                              func=AF.Silu)
                                sc.drain().then_inc(s_ev, 1)
                            pb_guard[par] = ("ev", cev.inc())
                last_he_ev = cev.v

                @blk.vector
                def _(v):
                    v.wait_ge(s_ev, last_he_ev)
                    v.tensor_mul(asb[:], asb[:], h3b[:])
                    v.drain().then_inc(s_dv, 1)
                as_done = cdv.inc()

                @blk.tensor
                def _(t):
                    t.wait_ge(s_dv, as_done)
                    if prev_yev:
                        t.wait_ge(s_dv, prev_yev[1])
                    last = None
                    for mo in range(KD):
                        pby = pbY[mo // 2]
                        of = (mo % 2) * 256
                        for k in range(KD):
                            last = t.matmul(pby[:, of:of + w], ws2[:, k * D + mo * 128:k * D + (mo + 1) * 128],
                                            asb[:, k * 256:k * 256 + w],
                                            start=(k == 0), stop=(k == KD - 1))
                    last.then_inc(s_mm, 1)
                prev_ymm = cmm.inc()

                @blk.vector
                def _(v):
                    v.wait_ge(s_mm, prev_ymm)
                    for mo in range(KD):
                        pby = pbY[mo // 2]
                        of = (mo % 2) * 256
                        v.tensor_copy(out=ysb[:, mo * TLOC + c0:mo * TLOC + c0 + w], in_=pby[:, of:of + w])
                    v.drain().then_inc(s_dv, 1)
                prev_yev = ("dv", cdv.inc())
            d2_ev = cev.v
            d2_mm = cmm.v
            d2_dv = cdv.v

        with ExitStack() as d3ctx:
            e1sl = [d3ctx.enter_context(nc.sbuf_tensor(f"e1sl{i}", [128, KD * 512], BF16, side="right")) for i in range(2)]
            e3sl = [d3ctx.enter_context(nc.sbuf_tensor(f"e3sl{i}", [128, KD * 512], BF16, side="right")) for i in range(2)]
            e2sl = [d3ctx.enter_context(nc.sbuf_tensor(f"e2sl{i}", [128, 4 * D], BF16, side="right")) for i in range(2)]
            h3e = d3ctx.enter_context(nc.sbuf_tensor("h3e", [128, 4 * 512], BF16, side="right"))
            aeb = d3ctx.enter_context(nc.sbuf_tensor("aeb", [128, 4 * 512], BF16, side="right"))
            cwe = d3ctx.enter_context(nc.sbuf_tensor("cwe", [128, 512], BF16, side="right"))

            slab_mm = {}
            sil_guard = {0: 0, 1: 0, 2: 0, 3: 0}
            y_guard = {0: 0, 1: 0}
            cwb_dv = 0
            prev_iter_mm = 0
            for e in range(32):
                wbi = e % 2

                @blk.gpsimd
                def _(g):
                    if e >= 2:
                        g.wait_ge(s_mm, slab_mm[e - 2])
                    elif e == 0:
                        g.wait_ge(s_ev, d2_ev)
                        g.wait_ge(s_mm, d2_mm)
                        g.wait_ge(s_dv, d2_dv)
                        g.wait_ge(s_w[0], ID_LD)
                    g.dma_start(out=e1sl[wbi][:].rearrange("p (k m) -> p k m", k=KD),
                                in_=_r_pk(W1_i[e])).then_inc(s_w[wbi], 16)
                    g.dma_start(out=e3sl[wbi][:].rearrange("p (k m) -> p k m", k=KD),
                                in_=_r_pk(W3_i[e])).then_inc(s_w[wbi], 16)
                    g.dma_start(out=e2sl[wbi][:].rearrange("p (k m) -> p k m", k=4),
                                in_=_r_pk(W2_i[e])).then_inc(s_w[wbi], 16)
                    g.dma_start(out=cwrow[wbi][:], in_=cw_d[e:e + 1, :]).then_inc(s_w[wbi], 16)
                w_ld = cw_c[wbi].inc(64)

                for ti, (c0, w) in enumerate(T_TILES):
                    PBS = [pbA, pbB, pbC, pbD]
                    my_cwb_dv = cwb_dv

                    @blk.tensor
                    def _(t):
                        if ti == 0:
                            t.wait_ge(s_w[wbi], w_ld)
                            if e == 0:
                                t.wait_ge(s_mm, d2_mm)
                        if my_cwb_dv:
                            t.wait_ge(s_dv, my_cwb_dv)
                        t.matmul(pbY[2][:, 0:w], ones_row_bf[:], cwrow[wbi][0:1, c0:c0 + w],
                                 start=True, stop=True).then_inc(s_mm, 1)
                    cwb_mm = cmm.inc()

                    hemm0 = {}
                    for m in range(4):
                        pb0 = PBS[m]
                        g0 = sil_guard[m]

                        @blk.tensor
                        def _(t):
                            if g0:
                                t.wait_ge(s_dv, g0)
                            last = None
                            for k in range(KD):
                                last = t.matmul(pb0[:, 0:w],
                                                e3sl[wbi][:, k * 512 + m * 128:k * 512 + (m + 1) * 128],
                                                ln2b[:, k * TLOC + c0:k * TLOC + c0 + w],
                                                start=(k == 0), stop=(k == KD - 1))
                            last.then_inc(s_mm, 1)
                        hemm0[m] = cmm.inc()

                    @blk.vector
                    def _(v):
                        v.wait_ge(s_mm, cwb_mm)
                        v.tensor_copy(out=cwe[:, 0:w], in_=pbY[2][:, 0:w])
                        v.wait_ge(s_mm, hemm0[0])
                        v.tensor_copy(out=h3e[:, 0 * 512:0 * 512 + w], in_=pbA[:, 0:w])
                        v.wait_ge(s_mm, hemm0[1])
                        v.tensor_copy(out=h3e[:, 1 * 512:1 * 512 + w], in_=pbB[:, 0:w])
                        v.drain().then_inc(s_dv, 2)
                        v.wait_ge(s_mm, hemm0[2])
                        v.tensor_copy(out=h3e[:, 2 * 512:2 * 512 + w], in_=pbC[:, 0:w])
                        v.wait_ge(s_mm, hemm0[3])
                        v.tensor_copy(out=h3e[:, 3 * 512:3 * 512 + w], in_=pbD[:, 0:w])
                        v.drain().then_inc(s_dv, 2)
                    pair1 = cdv.inc(2)
                    pair2 = cdv.inc(2)
                    cwb_dv = pair1

                    hemm1 = {}
                    e_base = cev.v
                    for m in range(4):
                        pb1 = PBS[m]
                        g1 = pair1 if m < 2 else pair2

                        @blk.tensor
                        def _(t):
                            if g1:
                                t.wait_ge(s_dv, g1)
                            last = None
                            for k in range(KD):
                                last = t.matmul(pb1[:, 0:w],
                                                e1sl[wbi][:, k * 512 + m * 128:k * 512 + (m + 1) * 128],
                                                ln2b[:, k * TLOC + c0:k * TLOC + c0 + w],
                                                start=(k == 0), stop=(k == KD - 1))
                            last.then_inc(s_mm, 1)
                        hemm1[m] = cmm.inc()

                    silu_done = e_base + 4
                    sil_guard[0] = e_base + 2; sil_guard[1] = e_base + 2
                    sil_guard[2] = e_base + 4; sil_guard[3] = e_base + 4

                    @blk.vector
                    def _(v):
                        v.wait_ge(s_ev, silu_done)
                        v.tensor_mul(aeb[:, 0:4 * 512], aeb[:, 0:4 * 512], h3e[:, 0:4 * 512])
                        for m in range(4):
                            v.tensor_mul(aeb[:, m * 512:m * 512 + w], aeb[:, m * 512:m * 512 + w],
                                         cwe[:, 0:w])
                        v.drain().then_inc(s_dv, 1)
                    ae_dv = cdv.inc()

                    ydv_base = cdv.v
                    ymm = {}
                    for mo in range(KD):
                        yslot = mo % 2
                        pby = pbY[yslot]
                        if mo < 2:
                            yg = y_guard[yslot]
                        else:
                            yg = ydv_base + 2 * (mo // 2)

                        @blk.tensor
                        def _(t):
                            if mo == 0:
                                t.wait_ge(s_dv, ae_dv)
                            if yg:
                                t.wait_ge(s_dv, yg)
                            last = None
                            for k in range(4):
                                last = t.matmul(pby[:, 0:w],
                                                e2sl[wbi][:, k * D + mo * 128:k * D + (mo + 1) * 128],
                                                aeb[:, k * 512:k * 512 + w],
                                                start=(k == 0), stop=(k == 3))
                            last.then_inc(s_mm, 1)
                        ymm[mo] = cmm.inc()

                    @blk.vector
                    def _(v):
                        for mo in range(KD):
                            v.wait_ge(s_mm, ymm[mo])
                            v.tensor_add(ysb[:, mo * TLOC + c0:mo * TLOC + c0 + w],
                                         ysb[:, mo * TLOC + c0:mo * TLOC + c0 + w],
                                         pbY[mo % 2][:, 0:w])
                            if mo % 2 == 1:
                                v.drain().then_inc(s_dv, 2)
                    cdv.inc(8)
                    y_guard[0] = cdv.v
                    y_guard[1] = cdv.v

                    my_prev_mm = prev_iter_mm

                    @blk.scalar
                    def _(sc):
                        if my_prev_mm:
                            sc.wait_ge(s_mm, my_prev_mm)
                        sc.wait_ge(s_mm, hemm1[0])
                        sc.activation(out=aeb[:, 0 * 512:0 * 512 + w], in_=pbA[:, 0:w], func=AF.Silu)
                        sc.wait_ge(s_mm, hemm1[1])
                        sc.activation(out=aeb[:, 1 * 512:1 * 512 + w], in_=pbB[:, 0:w], func=AF.Silu)
                        sc.drain().then_inc(s_ev, 2)
                        sc.wait_ge(s_mm, hemm1[2])
                        sc.activation(out=aeb[:, 2 * 512:2 * 512 + w], in_=pbC[:, 0:w], func=AF.Silu)
                        sc.wait_ge(s_mm, hemm1[3])
                        sc.activation(out=aeb[:, 3 * 512:3 * 512 + w], in_=pbD[:, 0:w], func=AF.Silu)
                        sc.drain().then_inc(s_ev, 2)
                    cev.inc(4)
                    prev_iter_mm = cmm.v
                slab_mm[e] = cmm.v
            d3_dv = cdv.v
            d3_mm = cmm.v
            d3_ev = cev.v

        with ExitStack() as d4ctx:
            xb = [d4ctx.enter_context(nc.sbuf_tensor(f"xb{i}", [128, KD * 512], F32, side="right")) for i in range(2)]
            yb = d4ctx.enter_context(nc.sbuf_tensor("yb", [128, KD * 512], F32, side="right"))
            lnb = d4ctx.enter_context(nc.sbuf_tensor("lnb", [128, KD * 512], BF16, side="right"))
            lnq = d4ctx.enter_context(nc.sbuf_tensor("lnq", [128, KD * 512], BF16, side="right"))
            ln_prev = {}
            for ti, (c0, w) in enumerate(T_TILES):
                bi = ti % 2
                sx_guard = csx[bi].v

                @blk.sync
                def _(s):
                    if ti == 0:
                        s.wait_ge(s_dv, d3_dv)
                        s.wait_ge(s_ev, d3_ev)
                    if ti >= 2:
                        s.wait_ge(s_dv, ln_prev[ti - 2])
                        s.wait_ge(s_sx[bi], sx_guard)
                    s.dma_start(out=xb[bi][:].rearrange("p (k t) -> p k t", k=KD),
                                in_=_r_pk(x1_i)[:, :, c0:c0 + w]).then_inc(s_x[bi], 16)
                x_ld = cx_c[bi].inc(16)

                @blk.vector
                def _(v):
                    v.wait_ge(s_x[bi], x_ld)
                    for k in range(KD):
                        v.tensor_copy(out=yb[:, k * 512:k * 512 + w],
                                      in_=ysb[:, k * TLOC + c0:k * TLOC + c0 + w])
                        v.drain()
                        v.tensor_add(xb[bi][:, k * 512:k * 512 + w],
                                     xb[bi][:, k * 512:k * 512 + w], yb[:, k * 512:k * 512 + w])
                        v.drain()
                        v.tensor_copy(out=lnb[:, k * 512:k * 512 + w],
                                      in_=xb[bi][:, k * 512:k * 512 + w])
                        v.drain()
                    v.engine_nop().then_inc(s_dv, 1)
                x2_done = cdv.inc()

                @blk.sync
                def _(s):
                    s.wait_ge(s_dv, x2_done)
                    s.dma_start(out=x2T[:, c0:c0 + w].rearrange("(k p) t -> p k t", p=128),
                                in_=xb[bi][:].rearrange("p (k t) -> p k t", k=KD)[:, :, 0:w]
                                ).then_inc(s_sx[bi], 16)
                csx[bi].inc(16)

                @blk.scalar
                def _(sc):
                    sc.wait_ge(s_dv, x2_done)
                    if csl.v:
                        sc.wait_ge(s_sl, csl.v)
                    sc.activation(out=lnq[:].rearrange("p (k t) -> p k t", k=KD)[:, :, 0:w],
                                  in_=lnb[:].rearrange("p (k t) -> p k t", k=KD)[:, :, 0:w], func=AF.Square)
                    sc.drain().then_inc(s_ev, 1)
                sq_done = cev.inc()

                @blk.tensor
                def _(t):
                    t.wait_ge(s_ev, sq_done)
                    if ti >= 1:
                        t.wait_ge(s_dv, ln_prev[ti - 1])
                    for k in range(KD):
                        t.matmul(pbA[0:1, 0:w], ones_bf[:], lnb[:, k * 512:k * 512 + w],
                                 start=(k == 0), stop=(k == KD - 1))
                    last = None
                    for k in range(KD):
                        last = t.matmul(pbB[0:1, 0:w], ones_bf[:], lnq[:, k * 512:k * 512 + w],
                                        start=(k == 0), stop=(k == KD - 1))
                    last.then_inc(s_mm, 1)
                stats_mm = cmm.inc()

                @blk.vector
                def _(v):
                    v.wait_ge(s_mm, stats_mm)
                    v.tensor_single_scalar(m_row[:, 0:w], pbA[0:1, 0:w], 1.0 / D, ALU.mult)
                    v.tensor_single_scalar(q_row[:, 0:w], pbB[0:1, 0:w], 1.0 / D, ALU.mult)
                    v.drain()
                    v.tensor_mul(v_row[:, 0:w], m_row[:, 0:w], m_row[:, 0:w])
                    v.drain()
                    v.tensor_sub(v_row[:, 0:w], q_row[:, 0:w], v_row[:, 0:w])
                    v.drain()
                    v.tensor_single_scalar(v_row[:, 0:w], v_row[:, 0:w], EPS, ALU.add)
                    v.drain().then_inc(s_dv, 1)
                var_done = cdv.inc()

                @blk.scalar
                def _(sc):
                    sc.wait_ge(s_dv, var_done)
                    sc.activation(out=sd_row[:, 0:w], in_=v_row[:, 0:w], func=AF.Sqrt)
                    sc.drain().then_inc(s_ev, 1)
                std_done = cev.inc()

                @blk.vector
                def _(v):
                    v.wait_ge(s_ev, std_done)
                    v.reciprocal(r_row[:, 0:w], sd_row[:, 0:w])
                    v.drain().then_inc(s_dv, 1)
                rinv_done = cdv.inc()

                @blk.tensor
                def _(t):
                    t.wait_ge(s_dv, rinv_done)
                    t.matmul(pbC[:, 0:w], ones_row[:], m_row[:, 0:w], start=True, stop=True)
                    last = t.matmul(pbD[:, 0:w], ones_row[:], r_row[:, 0:w], start=True, stop=True)
                    last.then_inc(s_mm, 1)
                bc_mm = cmm.inc()

                @blk.vector
                def _(v):
                    v.wait_ge(s_mm, bc_mm)
                    for k in range(KD):
                        v.tensor_sub(yb[:, k * 512:k * 512 + w],
                                     xb[bi][:, k * 512:k * 512 + w], pbC[:, 0:w])
                        v.drain()
                        v.tensor_mul(lnq[:, k * 512:k * 512 + w],
                                     yb[:, k * 512:k * 512 + w], pbD[:, 0:w])
                        v.drain()
                    v.engine_nop().then_inc(s_dv, 1)
                ln_done = cdv.inc()

                @blk.sync
                def _(s):
                    s.wait_ge(s_dv, ln_done)
                    s.dma_start(out=ln3T[:, c0:c0 + w].rearrange("(k p) t -> p k t", p=128),
                                in_=lnq[:].rearrange("p (k t) -> p k t", k=KD)[:, :, 0:w]
                                ).then_inc(s_sl, 16)
                csl.inc(16)
                ln_prev[ti] = cdv.v

        @blk.sync
        def _(s):
            s.wait_ge(s_sx[0], csx[0].v)
            s.wait_ge(s_sx[1], csx[1].v)
            s.wait_ge(s_sl, csl.v)

    return nc


def build_E():
    nc = bass.Bass("TRN2", target_bir_lowering=False)
    ln3s_i = nc.dram_tensor("ln3s_i", [D, TLOC], BF16, kind="ExternalInput").ap()
    w0_i = nc.dram_tensor("w0_i", [D, G4], F32, kind="ExternalInput").ap()
    w2_i = nc.dram_tensor("w2_i", [D, G4], F32, kind="ExternalInput").ap()
    uT = nc.dram_tensor("uT", [G4, TLOC], BF16, kind="ExternalOutput").ap()
    vT = nc.dram_tensor("vT", [G4, TLOC], BF16, kind="ExternalOutput").ap()

    with ExitStack() as ctx:
        ln3b = ctx.enter_context(nc.sbuf_tensor("ln3b", [128, KD * TLOC], BF16))
        wsl = [ctx.enter_context(nc.sbuf_tensor(f"wsl{i}", [128, KD * 128], BF16)) for i in range(4)]
        stg = [ctx.enter_context(nc.sbuf_tensor(f"stg{i}", [128, 512], BF16)) for i in range(4)]
        pb = [ctx.enter_context(nc.psum_tensor(f"pb{i}", [128, 512], F32)) for i in range(4)]
        s_c = ctx.enter_context(nc.semaphore("s_c"))
        s_w = [ctx.enter_context(nc.semaphore(f"s_w{i}")) for i in range(4)]
        s_g = [ctx.enter_context(nc.semaphore(f"s_g{i}")) for i in range(4)]
        s_mm = ctx.enter_context(nc.semaphore("s_mm"))
        s_ev = ctx.enter_context(nc.semaphore("s_ev"))
        cmm, cev = Cnt(), Cnt()
        cw_c = [Cnt() for _ in range(4)]
        cg_c = [Cnt() for _ in range(4)]
        blk = ctx.enter_context(nc.Block())

        @blk.sync
        def _(s):
            s.dma_start(out=ln3b[:].rearrange("p (k t) -> p k t", k=KD),
                        in_=_r_pk(ln3s_i)).then_inc(s_c, 16)

        slab_mm = {}
        ev_guard = {}
        for mt in range(32):
            for ui, (w_in, out_d) in enumerate([(w0_i, uT), (w2_i, vT)]):
                sl = (mt * 2 + ui) % 4

                @blk.gpsimd
                def _(g):
                    if mt * 2 + ui >= 4:
                        g.wait_ge(s_mm, slab_mm[mt * 2 + ui - 4])
                    g.dma_start(out=wsl[sl][:].rearrange("p (k m) -> p k m", k=KD),
                                in_=_r_pk(w_in)[:, :, mt * 128:(mt + 1) * 128]).then_inc(s_w[sl], 16)
                w_ld = cw_c[sl].inc(16)

                for ti, (c0, w) in enumerate(T_TILES):
                    pbi = (mt * 2 + ui) % 2 * 2 + ti % 2
                    st_guard = cg_c[pbi].v

                    @blk.tensor
                    def _(t):
                        if ti == 0:
                            t.wait_ge(s_w[sl], w_ld)
                            if mt == 0 and ui == 0:
                                t.wait_ge(s_c, 16)
                        if ev_guard.get(pbi):
                            t.wait_ge(s_ev, ev_guard[pbi])
                        last = None
                        for k in range(KD):
                            last = t.matmul(pb[pbi][:, 0:w], wsl[sl][:, k * 128:(k + 1) * 128],
                                            ln3b[:, k * TLOC + c0:k * TLOC + c0 + w],
                                            start=(k == 0), stop=(k == KD - 1))
                        last.then_inc(s_mm, 1)
                    mm_done = cmm.inc()

                    @blk.scalar
                    def _(sc):
                        sc.wait_ge(s_mm, mm_done)
                        if st_guard:
                            sc.wait_ge(s_g[pbi], st_guard)
                        sc.copy(out=stg[pbi][:, 0:w], in_=pb[pbi][:, 0:w])
                        sc.drain().then_inc(s_ev, 1)
                    ev_guard[pbi] = cev.inc()
                    ev_here = ev_guard[pbi]

                    @blk.sync
                    def _(s):
                        s.wait_ge(s_ev, ev_here)
                        s.dma_start(out=out_d[mt * 128:(mt + 1) * 128, c0:c0 + w],
                                    in_=stg[pbi][:, 0:w]).then_inc(s_g[pbi], 16)
                    cg_c[pbi].inc(16)
                slab_mm[mt * 2 + ui] = cmm.v

        @blk.sync
        def _(s):
            for i in range(4):
                s.wait_ge(s_g[i], cg_c[i].v)

    return nc


def build_F():
    nc = bass.Bass("TRN2", target_bir_lowering=False)
    uT_i = nc.dram_tensor("uT_i", [G4, TLOC], BF16, kind="ExternalInput").ap()
    vT_i = nc.dram_tensor("vT_i", [G4, TLOC], BF16, kind="ExternalInput").ap()
    ln3s_i = nc.dram_tensor("ln3s_i", [D, TLOC], BF16, kind="ExternalInput").ap()
    w1_i = nc.dram_tensor("w1_i", [G4, D], F32, kind="ExternalInput").ap()
    outT = nc.dram_tensor("outT", [D, TLOC], F32, kind="ExternalOutput").ap()
    ag_in = nc.dram_tensor("ag_in", [1, 1], F32)
    ag_out = nc.dram_tensor("ag_out", [8, 1], F32)

    KF = G4 // 128

    with ExitStack() as ctx:
        w1b = ctx.enter_context(nc.sbuf_tensor("w1b", [128, KF * D], BF16))
        ln3b = ctx.enter_context(nc.sbuf_tensor("ln3b", [128, KD * TLOC], BF16))
        usl = [ctx.enter_context(nc.sbuf_tensor(f"usl{i}", [128, KF * 128], BF16)) for i in range(2)]
        vsl = [ctx.enter_context(nc.sbuf_tensor(f"vsl{i}", [128, KF * 128], BF16)) for i in range(2)]
        sub = ctx.enter_context(nc.sbuf_tensor("sub", [128, KF * 128], BF16))
        hhb = ctx.enter_context(nc.sbuf_tensor("hhb", [128, KF * 128], BF16))
        o_f = ctx.enter_context(nc.sbuf_tensor("o_f", [128, KD * 128], F32))
        xi_f = ctx.enter_context(nc.sbuf_tensor("xi_f", [128, KD * 128], F32))
        dcol = ctx.enter_context(nc.sbuf_tensor("dcol", [128, 1], F32))
        d2col = ctx.enter_context(nc.sbuf_tensor("d2col", [128, 1], F32))
        ones_f = ctx.enter_context(nc.sbuf_tensor("ones_f", [128, 1], F32))
        ones_row = ctx.enter_context(nc.sbuf_tensor("ones_row", [1, 128], F32))
        rs_col = ctx.enter_context(nc.sbuf_tensor("rs_col", [128, 1], F32))
        ls_s = ctx.enter_context(nc.sbuf_tensor("ls_s", [1, 1], F32))
        ls_all = ctx.enter_context(nc.sbuf_tensor("ls_all", [1, 8], F32))
        lt_col = ctx.enter_context(nc.sbuf_tensor("lt_col", [128, 1], F32))
        pbO = [ctx.enter_context(nc.psum_tensor(f"pbO{i}", [128, 512], F32)) for i in range(2)]
        pbL = ctx.enter_context(nc.psum_tensor("pbL", [128, 512], F32))
        s_c = ctx.enter_context(nc.semaphore("s_c"))
        s_u = [ctx.enter_context(nc.semaphore(f"s_u{i}")) for i in range(2)]
        s_mm = ctx.enter_context(nc.semaphore("s_mm"))
        s_ev = ctx.enter_context(nc.semaphore("s_ev"))
        s_dv = ctx.enter_context(nc.semaphore("s_dv"))
        s_cc = ctx.enter_context(nc.semaphore("s_cc"))
        s_st = ctx.enter_context(nc.semaphore("s_st"))
        cmm, cev, cdv, ccc, cst = Cnt(), Cnt(), Cnt(), Cnt(), Cnt()
        cu = [Cnt(), Cnt()]
        blk = ctx.enter_context(nc.Block())

        @blk.sync
        def _(s):
            s.dma_start(out=ln3b[:].rearrange("p (k t) -> p k t", k=KD),
                        in_=_r_pk(ln3s_i)).then_inc(s_c, 16)

        @blk.gpsimd
        def _(g):
            g.dma_start(out=w1b[:].rearrange("p (k m) -> p k m", k=KF),
                        in_=_r_pk(w1_i)).then_inc(s_c, 16)

        @blk.vector
        def _(v):
            v.memset(dcol[:], 1.0)
            v.memset(ones_f[:], 1.0)
            v.memset(ones_row[:], 1.0)
            v.drain().then_inc(s_dv, 1)
        init_dv = cdv.inc()

        for i in range(16):
            sl = i % 2

            @blk.gpsimd
            def _(g):
                if i >= 2:
                    g.wait_ge(s_dv, hh_done[i - 2])
                g.dma_start(out=usl[sl][:].rearrange("p (k m) -> p k m", k=KF),
                            in_=_r_pk(uT_i)[:, :, i * 128:(i + 1) * 128]).then_inc(s_u[sl], 16)
                g.dma_start(out=vsl[sl][:].rearrange("p (k m) -> p k m", k=KF),
                            in_=_r_pk(vT_i)[:, :, i * 128:(i + 1) * 128]).then_inc(s_u[sl], 16)
            u_ld = cu[sl].inc(32)
            if i == 0:
                hh_done = {}

            @blk.vector
            def _(v):
                if i == 0:
                    v.wait_ge(s_dv, init_dv)
                v.tensor_mul(d2col[:], dcol[:], dcol[:])
                v.drain().then_inc(s_dv, 1)
            d2_done = cdv.inc()

            @blk.scalar
            def _(sc):
                sc.wait_ge(s_u[sl], u_ld)
                sc.wait_ge(s_dv, d2_done)
                sc.activation(out=sub[:], in_=usl[sl][:], func=AF.Silu, scale=dcol[:])
                sc.drain().then_inc(s_ev, 1)
            silu_done = cev.inc()

            @blk.vector
            def _(v):
                v.wait_ge(s_ev, silu_done)
                v.tensor_mul(hhb[:], sub[:], vsl[sl][:])
                v.drain().then_inc(s_dv, 1)
            hh_done[i] = cdv.inc()

            @blk.tensor
            def _(t):
                t.wait_ge(s_dv, hh_done[i])
                if i >= 1:
                    t.wait_ge(s_ev, o_ev)
                last = None
                for mo in range(KD):
                    pby = pbO[mo // 4]
                    of = (mo % 4) * 128
                    for k in range(KF):
                        last = t.matmul(pby[:, of:of + 128],
                                        w1b[:, k * D + mo * 128:k * D + (mo + 1) * 128],
                                        hhb[:, k * 128:(k + 1) * 128],
                                        start=(k == 0), stop=(k == KF - 1))
                last.then_inc(s_mm, 1)
            o_mm = cmm.inc()

            @blk.scalar
            def _(sc):
                sc.wait_ge(s_mm, o_mm)
                if i >= 1:
                    sc.wait_ge(s_st, ost)
                for mo in range(KD):
                    pby = pbO[mo // 4]
                    of = (mo % 4) * 128
                    sc.activation(out=o_f[:, mo * 128:(mo + 1) * 128], in_=pby[:, of:of + 128],
                                  func=AF.Copy, scale=d2col[:])
                    sc.drain()
                sc.copy(out=xi_f[:], in_=ln3b[:].rearrange("p (k t) -> p k t", k=KD)[:, :, i * 128:(i + 1) * 128])
                sc.drain().then_inc(s_ev, 1)
            o_ev = cev.inc()

            @blk.sync
            def _(s):
                s.wait_ge(s_ev, o_ev)
                s.dma_start(out=_r_pk(outT)[:, :, i * 128:(i + 1) * 128],
                            in_=o_f[:].rearrange("p (k t) -> p k t", k=KD)).then_inc(s_st, 16)
            ost = cst.inc(16)

            @blk.vector
            def _(v):
                v.wait_ge(s_ev, o_ev)
                v.tensor_sub(xi_f[:], o_f[:], xi_f[:])
                v.drain()
                v.tensor_mul(xi_f[:], xi_f[:], xi_f[:])
                v.drain()
                v.reduce_sum(rs_col[:], xi_f[:], axis=mybir.AxisListType.X)
                v.drain().then_inc(s_dv, 1)
            rs_done = cdv.inc()

            @blk.tensor
            def _(t):
                t.wait_ge(s_dv, rs_done)
                if i >= 1:
                    t.wait_ge(s_ev, ls_ev)
                t.matmul(pbL[0:1, 0:1], ones_f[:], rs_col[:], start=True, stop=True).then_inc(s_mm, 1)
            ls_mm = cmm.inc()

            @blk.scalar
            def _(sc):
                sc.wait_ge(s_mm, ls_mm)
                sc.copy(out=ls_s[:], in_=pbL[0:1, 0:1])
                sc.drain().then_inc(s_ev, 1)
            ls_ev = cev.inc()

            @blk.gpsimd
            def _(g):
                g.wait_ge(s_ev, ls_ev)
                g.dma_start(out=ag_in[:, :], in_=ls_s[:]).then_inc(s_cc, 16)
                g.wait_ge(s_cc, ccc.v + 16)
                g.collective_compute(
                    "AllGather", ALU.bypass,
                    replica_groups=[list(range(8))],
                    ins=[ag_in.ap().opt()], outs=[ag_out.ap().opt()],
                ).then_inc(s_cc, 1)
                g.wait_ge(s_cc, ccc.v + 17)
                g.dma_start(out=ls_all[:], in_=ag_out.rearrange("a b -> b a")).then_inc(s_cc, 16)
                g.wait_ge(s_cc, ccc.v + 33)
                g.engine_nop().then_inc(s_cc, 1)
            ccc.inc(34)
            ag_done = ccc.v

            @blk.vector
            def _(v):
                v.wait_ge(s_cc, ag_done)
                v.reduce_sum(ls_s[:], ls_all[:], axis=mybir.AxisListType.X)
                v.drain().then_inc(s_dv, 1)
            tot_done = cdv.inc()

            @blk.tensor
            def _(t):
                t.wait_ge(s_dv, tot_done)
                t.matmul(pbL[:, 0:1], ones_row[:], ls_s[:], start=True, stop=True).then_inc(s_mm, 1)
            bc_mm = cmm.inc()

            @blk.vector
            def _(v):
                v.wait_ge(s_mm, bc_mm)
                v.tensor_single_scalar(lt_col[:], pbL[:, 0:1], 1.0 / (2 * 512 * 1024), ALU.mult)
                v.drain()
                v.tensor_scalar(out=lt_col[:], in0=lt_col[:], scalar1=-LR, scalar2=1.0,
                                op0=ALU.mult, op1=ALU.add)
                v.drain()
                v.tensor_mul(dcol[:], dcol[:], lt_col[:])
                v.drain().then_inc(s_dv, 1)
            cdv.inc()

        @blk.sync
        def _(s):
            s.wait_ge(s_st, cst.v)

    return nc


def build_G():
    nc = bass.Bass("TRN2", target_bir_lowering=False)
    x2_i = nc.dram_tensor("x2_i", [D, TLOC], F32, kind="ExternalInput").ap()
    at_i = nc.dram_tensor("at_i", [D, TLOC], F32, kind="ExternalInput").ap()
    gwT_i = nc.dram_tensor("gwT_i", [D, 8], F32, kind="ExternalInput").ap()
    gb_i = nc.dram_tensor("gb_i", [1, 8], F32, kind="ExternalInput").ap()
    w1p_i = nc.dram_tensor("w1p_i", [8, D, 2048], F32, kind="ExternalInput").ap()
    b1p_i = nc.dram_tensor("b1p_i", [8, 2048], F32, kind="ExternalInput").ap()
    w3p_i = nc.dram_tensor("w3p_i", [8, D, 2048], F32, kind="ExternalInput").ap()
    b3p_i = nc.dram_tensor("b3p_i", [8, 2048], F32, kind="ExternalInput").ap()
    w2t_i = nc.dram_tensor("w2t_i", [8, D, D], F32, kind="ExternalInput").ap()
    b2_i = nc.dram_tensor("b2_i", [8, D], F32, kind="ExternalInput").ap()
    sw1p_i = nc.dram_tensor("sw1p_i", [D, 2048], F32, kind="ExternalInput").ap()
    sb1p_i = nc.dram_tensor("sb1p_i", [1, 2048], F32, kind="ExternalInput").ap()
    sw3p_i = nc.dram_tensor("sw3p_i", [D, 2048], F32, kind="ExternalInput").ap()
    sb3p_i = nc.dram_tensor("sb3p_i", [1, 2048], F32, kind="ExternalInput").ap()
    sw2t_i = nc.dram_tensor("sw2t_i", [D, D], F32, kind="ExternalInput").ap()
    sb2_i = nc.dram_tensor("sb2_i", [1, D], F32, kind="ExternalInput").ap()
    ident_i = nc.dram_tensor("ident_i", [128, 128], F32, kind="ExternalInput").ap()
    fT = nc.dram_tensor("fT", [D, TLOC], F32, kind="ExternalOutput").ap()
    x3_d = nc.dram_tensor("x3_d", [D, TLOC], F32)
    cw8_d = nc.dram_tensor("cw8_d", [8, TLOC], F32)

    DT8 = [(i * 256, 256) for i in range(8)]
    LIM = 7.0

    with ExitStack() as ctx:
        def sb(name, shape, dt=F32, side="left"):
            return ctx.enter_context(nc.sbuf_tensor(name, shape, dt, side=side))
        def sem(name):
            return ctx.enter_context(nc.semaphore(name))

        ln4b = sb("ln4b", [128, KD * TLOC], BF16)
        ysb = sb("ysb", [128, KD * TLOC], BF16)
        ident = sb("ident", [128, 128])
        gwb = sb("gwb", [128, KD * 8])
        gb_b = sb("gb_b", [128, 8])
        b2sl = sb("b2sl", [8, KD * 128], BF16)
        cw8sb = sb("cw8sb", [8, TLOC], BF16)
        ones_bf = sb("ones_bf", [128, 1], BF16)
        ones_row = sb("ones_row", [1, 128])
        ones_row_bf = sb("ones_row_bf", [1, 128], BF16)
        ones5 = sb("ones5", [1, 512], BF16)
        it8 = sb("it8", [128, 8])
        pbA = ctx.enter_context(nc.psum_tensor("pbA", [128, 512], F32))
        pbB = ctx.enter_context(nc.psum_tensor("pbB", [128, 512], F32))
        pbC = ctx.enter_context(nc.psum_tensor("pbC", [128, 512], F32))
        pbD = ctx.enter_context(nc.psum_tensor("pbD", [128, 512], F32))
        pbY = [ctx.enter_context(nc.psum_tensor(f"pbY{i}", [128, 512], F32)) for i in range(4)]

        s_c = sem("s_c"); s_mm = sem("s_mm"); s_ev = sem("s_ev"); s_dv = sem("s_dv")
        s_w = [sem("s_w0"), sem("s_w1")]
        s_x = [sem("s_x0"), sem("s_x1")]
        s_sf = [sem("s_sf0"), sem("s_sf1")]
        s_gp = sem("s_gp")
        cmm, cev, cdv, cgp = Cnt(), Cnt(), Cnt(), Cnt()
        cw_c = [Cnt(), Cnt()]
        cx_c = [Cnt(), Cnt()]
        csf = [Cnt(), Cnt()]

        blk = ctx.enter_context(nc.Block())

        @blk.sync
        def _(s):
            s.dma_start(out=ident[:], in_=ident_i).then_inc(s_c, 16)
            s.dma_start(out=gwb[:].rearrange("p (k m) -> p k m", k=KD), in_=_r_pk(gwT_i)).then_inc(s_c, 16)
        CON1 = 32

        @blk.gpsimd
        def _(g):
            g.dma_start(out=b2sl[:].rearrange("p (k m) -> p k m", k=KD),
                        in_=b2_i.rearrange("e (k m) -> e k m", k=KD)).then_inc(s_c, 16)
            g.iota(it8[:], pattern=[[1, 8]], base=0, channel_multiplier=0,
                   allow_small_or_imprecise_dtypes=True)
            g.engine_nop().then_inc(s_gp, 1)
        CON2 = CON1 + 16
        iota_done = cgp.inc()

        @blk.vector
        def _(v):
            v.memset(ones_bf[:], 1.0)
            v.memset(ones_row[:], 1.0)
            v.memset(ones_row_bf[:], 1.0)
            v.memset(ones5[:], 1.0)
            v.drain().then_inc(s_dv, 1)
        memset_dv = cdv.inc()

        gbrow = sb("gbrow", [1, 8])

        @blk.sync
        def _(s):
            s.dma_start(out=gbrow[:], in_=gb_i).then_inc(s_c, 16)
        CON3 = CON2 + 16

        @blk.tensor
        def _(t):
            t.wait_ge(s_c, CON3)
            t.matmul(pbC[:, 0:8], ones_row[:], gbrow[:], start=True, stop=True).then_inc(s_mm, 1)
        gb_mm = cmm.inc()

        @blk.scalar
        def _(sc):
            sc.wait_ge(s_mm, gb_mm)
            sc.copy(out=gb_b[:], in_=pbC[:, 0:8])
            sc.drain().then_inc(s_ev, 1)
        gb_done = cev.inc()

        g12ctx = ExitStack()
        s8f = g12ctx.enter_context(nc.sbuf_tensor("s8f", [8, 4 * 512], F32, side="right"))
        m_row = g12ctx.enter_context(nc.sbuf_tensor("m_row", [1, 512], F32, side="right"))
        q_row = g12ctx.enter_context(nc.sbuf_tensor("q_row", [1, 512], F32, side="right"))
        v_row = g12ctx.enter_context(nc.sbuf_tensor("v_row", [1, 512], F32, side="right"))
        sd_row = g12ctx.enter_context(nc.sbuf_tensor("sd_row", [1, 512], F32, side="right"))
        r_row = g12ctx.enter_context(nc.sbuf_tensor("r_row", [1, 512], F32, side="right"))
        with ExitStack() as g1ctx:
            xb = [g1ctx.enter_context(nc.sbuf_tensor(f"xb{i}", [128, KD * 512], F32, side="right")) for i in range(2)]
            ab = g1ctx.enter_context(nc.sbuf_tensor("ab", [128, KD * 512], F32, side="right"))
            lnq = g1ctx.enter_context(nc.sbuf_tensor("lnq", [128, KD * 512], BF16, side="right"))
            lsq = g1ctx.enter_context(nc.sbuf_tensor("lsq", [128, KD * 512], BF16, side="right"))
            lnf = g1ctx.enter_context(nc.sbuf_tensor("lnf", [128, KD * 512], F32, side="right"))
            ln_prev = {}
            for ti, (c0, w) in enumerate(T_TILES):
                bi = ti % 2
                sx_guard = csf[bi].v

                @blk.sync
                def _(s):
                    if ti >= 2:
                        s.wait_ge(s_dv, ln_prev[ti - 2])
                        s.wait_ge(s_sf[bi], sx_guard)
                    s.dma_start(out=xb[bi][:].rearrange("p (k t) -> p k t", k=KD),
                                in_=_r_pk(x2_i)[:, :, c0:c0 + w]).then_inc(s_x[bi], 16)
                    s.dma_start(out=ab[:].rearrange("p (k t) -> p k t", k=KD),
                                in_=_r_pk(at_i)[:, :, c0:c0 + w]).then_inc(s_x[bi], 16)
                x_ld = cx_c[bi].inc(32)

                @blk.vector
                def _(v):
                    v.wait_ge(s_x[bi], x_ld)
                    if ti == 0:
                        v.wait_ge(s_dv, memset_dv)
                    if ti >= 1:
                        v.wait_ge(s_dv, ln_prev[ti - 1])
                    for k in range(KD):
                        v.tensor_add(xb[bi][:, k * 512:k * 512 + w],
                                     xb[bi][:, k * 512:k * 512 + w], ab[:, k * 512:k * 512 + w])
                        v.drain()
                        v.tensor_copy(out=lnq[:, k * 512:k * 512 + w],
                                      in_=xb[bi][:, k * 512:k * 512 + w])
                        v.drain()
                    v.engine_nop().then_inc(s_dv, 1)
                x3_done = cdv.inc()

                @blk.sync
                def _(s):
                    s.wait_ge(s_dv, x3_done)
                    s.dma_start(out=_r_pk(x3_d)[:, :, c0:c0 + w],
                                in_=xb[bi][:].rearrange("p (k t) -> p k t", k=KD)[:, :, 0:w]
                                ).then_inc(s_sf[bi], 16)
                csf[bi].inc(16)

                @blk.scalar
                def _(sc):
                    sc.wait_ge(s_dv, x3_done)
                    sc.activation(out=lsq[:].rearrange("p (k t) -> p k t", k=KD)[:, :, 0:w],
                                  in_=lnq[:].rearrange("p (k t) -> p k t", k=KD)[:, :, 0:w], func=AF.Square)
                    sc.drain().then_inc(s_ev, 1)
                sq_done = cev.inc()

                @blk.tensor
                def _(t):
                    t.wait_ge(s_ev, sq_done)
                    if ti >= 1:
                        t.wait_ge(s_dv, ln_prev[ti - 1])
                    for k in range(KD):
                        t.matmul(pbA[0:1, 0:w], ones_bf[:], lnq[:, k * 512:k * 512 + w],
                                 start=(k == 0), stop=(k == KD - 1))
                    last = None
                    for k in range(KD):
                        last = t.matmul(pbB[0:1, 0:w], ones_bf[:], lsq[:, k * 512:k * 512 + w],
                                        start=(k == 0), stop=(k == KD - 1))
                    last.then_inc(s_mm, 1)
                stats_mm = cmm.inc()

                @blk.vector
                def _(v):
                    v.wait_ge(s_mm, stats_mm)
                    v.tensor_single_scalar(m_row[:, 0:w], pbA[0:1, 0:w], 1.0 / D, ALU.mult)
                    v.tensor_single_scalar(q_row[:, 0:w], pbB[0:1, 0:w], 1.0 / D, ALU.mult)
                    v.drain()
                    v.tensor_mul(v_row[:, 0:w], m_row[:, 0:w], m_row[:, 0:w])
                    v.drain()
                    v.tensor_sub(v_row[:, 0:w], q_row[:, 0:w], v_row[:, 0:w])
                    v.drain()
                    v.tensor_single_scalar(v_row[:, 0:w], v_row[:, 0:w], EPS, ALU.add)
                    v.drain().then_inc(s_dv, 1)
                var_done = cdv.inc()

                @blk.scalar
                def _(sc):
                    sc.wait_ge(s_dv, var_done)
                    sc.activation(out=sd_row[:, 0:w], in_=v_row[:, 0:w], func=AF.Sqrt)
                    sc.drain().then_inc(s_ev, 1)
                std_done = cev.inc()

                @blk.vector
                def _(v):
                    v.wait_ge(s_ev, std_done)
                    v.reciprocal(r_row[:, 0:w], sd_row[:, 0:w])
                    v.drain().then_inc(s_dv, 1)
                rinv_done = cdv.inc()

                @blk.tensor
                def _(t):
                    t.wait_ge(s_dv, rinv_done)
                    t.matmul(pbC[:, 0:w], ones_row[:], m_row[:, 0:w], start=True, stop=True)
                    last = t.matmul(pbD[:, 0:w], ones_row[:], r_row[:, 0:w], start=True, stop=True)
                    last.then_inc(s_mm, 1)
                bc_mm = cmm.inc()

                @blk.vector
                def _(v):
                    v.wait_ge(s_mm, bc_mm)
                    for k in range(KD):
                        v.tensor_sub(lnf[:, k * 512:k * 512 + w],
                                     xb[bi][:, k * 512:k * 512 + w], pbC[:, 0:w])
                        v.drain()
                        v.tensor_mul(lnf[:, k * 512:k * 512 + w],
                                     lnf[:, k * 512:k * 512 + w], pbD[:, 0:w])
                        v.drain()
                        v.tensor_copy(out=ln4b[:, k * TLOC + c0:k * TLOC + c0 + w],
                                      in_=lnf[:, k * 512:k * 512 + w])
                        v.drain()
                    v.engine_nop().then_inc(s_dv, 1)
                lnf_done = cdv.inc()

                @blk.tensor
                def _(t):
                    t.wait_ge(s_dv, lnf_done)
                    if ti >= 1:
                        t.wait_ge(s_ev, lg_ev)
                    last = None
                    for k in range(KD):
                        last = t.matmul(pbA[0:8, 0:w], gwb[:, k * 8:(k + 1) * 8],
                                        lnf[:, k * 512:k * 512 + w],
                                        start=(k == 0), stop=(k == KD - 1))
                    last.then_inc(s_mm, 1)
                lg_mm = cmm.inc()

                @blk.scalar
                def _(sc):
                    sc.wait_ge(s_mm, lg_mm)
                    sc.copy(out=s8f[:, ti * 512:ti * 512 + w], in_=pbA[0:8, 0:w])
                    sc.drain().then_inc(s_ev, 1)
                lg_ev = cev.inc()
                ln_prev[ti] = cdv.v
            g1_dv = cdv.v
            g1_ev = cev.v
            g1_mm = cmm.v

        with ExitStack() as g2ctx:
            lg = g2ctx.enter_context(nc.sbuf_tensor("lg", [128, 8], F32, side="right"))
            bs = g2ctx.enter_context(nc.sbuf_tensor("bs", [128, 8], F32, side="right"))
            srow = g2ctx.enter_context(nc.sbuf_tensor("srow", [128, 1], F32, side="right"))
            top8 = g2ctx.enter_context(nc.sbuf_tensor("top8", [128, 8], F32, side="right"))
            idxu = g2ctx.enter_context(nc.sbuf_tensor("idxu", [128, 8], U32, side="right"))
            idxf = g2ctx.enter_context(nc.sbuf_tensor("idxf", [128, 8], F32, side="right"))
            oh = g2ctx.enter_context(nc.sbuf_tensor("oh", [128, 8], F32, side="right"))
            wcol = g2ctx.enter_context(nc.sbuf_tensor("wcol", [128, 2], F32, side="right"))
            cwtm = g2ctx.enter_context(nc.sbuf_tensor("cwtm", [128, 8], F32, side="right"))
            cw8f = g2ctx.enter_context(nc.sbuf_tensor("cw8f", [8, TLOC], F32, side="right"))
            pt8 = pbD

            for tk in range(16):
                @blk.tensor
                def _(t):
                    if tk == 0:
                        t.wait_ge(s_ev, g1_ev)
                        t.wait_ge(s_gp, iota_done)
                    else:
                        t.wait_ge(s_dv, cw_dv)
                    t.transpose(pbB[:, 0:8], s8f[0:8, (tk // 4) * 512 + (tk % 4) * 128:
                                                    (tk // 4) * 512 + (tk % 4) * 128 + 128],
                                ident[0:8, 0:8]).then_inc(s_mm, 1)
                tr_mm = cmm.inc()

                @blk.scalar
                def _(sc):
                    sc.wait_ge(s_mm, tr_mm)
                    sc.wait_ge(s_ev, gb_done)
                    sc.copy(out=lg[:], in_=pbB[:, 0:8])
                    sc.drain().then_inc(s_ev, 1)
                lg_done = cev.inc()

                @blk.vector
                def _(v):
                    v.wait_ge(s_ev, lg_done)
                    v.reduce_max(srow[:], lg[:], axis=mybir.AxisListType.X)
                    v.drain()
                    v.tensor_single_scalar(srow[:], srow[:], -1.0, ALU.mult)
                    v.drain().then_inc(s_dv, 1)
                mx_done = cdv.inc()

                @blk.scalar
                def _(sc):
                    sc.wait_ge(s_dv, mx_done)
                    sc.activation(out=lg[:], in_=lg[:], func=AF.Exp, bias=srow[:])
                    sc.drain().then_inc(s_ev, 1)
                ex_done = cev.inc()

                @blk.vector
                def _(v):
                    v.wait_ge(s_ev, ex_done)
                    v.reduce_sum(srow[:], lg[:], axis=mybir.AxisListType.X)
                    v.drain()
                    v.reciprocal(srow[:], srow[:])
                    v.drain()
                    v.tensor_single_scalar(lg[:], lg[:], srow[:], ALU.mult)
                    v.drain()
                    v.tensor_add(bs[:], lg[:], gb_b[:])
                    v.drain()
                    v.max(out=top8[:], in_=bs[:])
                    v.drain()
                    v.max_index(out=idxu[:], in_max=top8[:], in_values=bs[:])
                    v.drain()
                    v.tensor_copy(out=idxf[:], in_=idxu[:])
                    v.drain()
                    v.memset(cwtm[:], 0.0)
                    v.drain()
                    for k in range(2):
                        v.tensor_single_scalar(oh[:], it8[:], idxf[:, k:k + 1], ALU.is_equal)
                        v.drain()
                        v.tensor_mul(oh[:], oh[:], lg[:])
                        v.drain()
                        v.reduce_sum(wcol[:, k:k + 1], oh[:], axis=mybir.AxisListType.X)
                        v.drain()
                        v.tensor_single_scalar(oh[:], it8[:], idxf[:, k:k + 1], ALU.is_equal)
                        v.drain()
                        v.scalar_tensor_tensor(cwtm[:], oh[:], wcol[:, k:k + 1], cwtm[:],
                                               ALU.mult, ALU.add)
                        v.drain()
                    v.engine_nop().then_inc(s_dv, 1)
                cw_done = cdv.inc()

                @blk.tensor
                def _(t):
                    t.wait_ge(s_dv, cw_done)
                    if tk > 0:
                        t.wait_ge(s_ev, cwT_ev)
                    t.transpose(pt8[0:8, 0:128], cwtm[:], ident[:]).then_inc(s_mm, 1)
                cwT_mm = cmm.inc()

                @blk.scalar
                def _(sc):
                    sc.wait_ge(s_mm, cwT_mm)
                    sc.copy(out=cw8f[:, tk * 128:(tk + 1) * 128], in_=pt8[0:8, 0:128])
                    sc.drain().then_inc(s_ev, 1)
                cwT_ev = cev.inc()
                cw_dv = cdv.v

            @blk.sync
            def _(s):
                s.wait_ge(s_ev, cwT_ev)
                s.dma_start(out=cw8_d[:, :], in_=cw8f[:]).then_inc(s_c, 16)
            CW8 = CON3 + 16

            @blk.gpsimd
            def _(g):
                g.wait_ge(s_c, CW8)
                g.dma_start(out=cw8sb[:], in_=cw8_d[:, :]).then_inc(s_gp, 16)
            cw8_ld = cgp.inc(16)
        g2_ev = cev.v
        g12ctx.close()

        with ExitStack() as g34ctx:
            wsl = [g34ctx.enter_context(nc.sbuf_tensor(f"wsl{i}", [128, KD * 1024], BF16, side="right")) for i in range(4)]
            w2s = g34ctx.enter_context(nc.sbuf_tensor("w2s", [128, KD * D], BF16, side="right"))
            h1e = g34ctx.enter_context(nc.sbuf_tensor("h1e", [128, 16 * 512], BF16, side="right"))
            h3e = g34ctx.enter_context(nc.sbuf_tensor("h3e", [128, 16 * 512], BF16, side="right"))
            cwe = g34ctx.enter_context(nc.sbuf_tensor("cwe", [128, 512], BF16, side="right"))
            cwrow = g34ctx.enter_context(nc.sbuf_tensor("cwrow", [1, TLOC], BF16, side="right"))
            b1r = g34ctx.enter_context(nc.sbuf_tensor("b1r", [1, 2048], BF16, side="right"))
            b3r = g34ctx.enter_context(nc.sbuf_tensor("b3r", [1, 2048], BF16, side="right"))
            sb2c = g34ctx.enter_context(nc.sbuf_tensor("sb2c", [1, D], BF16, side="right"))
            glv = h3e[:, 0:8 * 512]
            asv = h3e[:, 8 * 512:16 * 512]

            sil_guard = {0: 0, 1: 0, 2: 0, 3: 0}
            y_guard = {0: None, 1: None}
            cwb_dv = 0
            prev_iter_mm = 0
            slab_free = {}
            w2_free = {}
            EXP = [-1] + list(range(8))
            for ei, e in enumerate(EXP):
                @blk.gpsimd
                def _(g):
                    if ei >= 1:
                        g.wait_ge(s_mm, slab_free[ei - 1])
                    else:
                        g.wait_ge(s_dv, g1_dv)
                        g.wait_ge(s_ev, g2_ev)
                    for half in range(4):
                        if e < 0:
                            src = _r_pk(sw3p_i if half < 2 else sw1p_i)[:, :, (half % 2) * 1024:(half % 2) * 1024 + 1024]
                        else:
                            src = _r_pk(w3p_i[e] if half < 2 else w1p_i[e])[:, :, (half % 2) * 1024:(half % 2) * 1024 + 1024]
                        g.dma_start(out=wsl[half][:].rearrange("p (k m) -> p k m", k=KD),
                                    in_=src).then_inc(s_w[0], 16)
                    if ei >= 1:
                        g.wait_ge(s_mm, w2_free[ei - 1])
                    g.dma_start(out=w2s[:].rearrange("p (k m) -> p k m", k=KD),
                                in_=_r_pk(sw2t_i if e < 0 else w2t_i[e])).then_inc(s_w[1], 16)
                    if e < 0:
                        g.dma_start(out=b3r[:], in_=sb3p_i).then_inc(s_w[1], 16)
                        g.dma_start(out=b1r[:], in_=sb1p_i).then_inc(s_w[1], 16)
                        g.dma_start(out=sb2c[:], in_=sb2_i).then_inc(s_w[1], 16)
                    else:
                        g.dma_start(out=b3r[:], in_=b3p_i[e:e + 1, :]).then_inc(s_w[1], 16)
                        g.dma_start(out=b1r[:], in_=b1p_i[e:e + 1, :]).then_inc(s_w[1], 16)
                        g.dma_start(out=cwrow[:], in_=cw8_d[e:e + 1, :]).then_inc(s_w[1], 16)
                wsl_ld = cw_c[0].inc(64)
                w2_ld = cw_c[1].inc(64)

                for ti, (c0, w) in enumerate(T_TILES):
                    PBS = [pbA, pbB, pbC, pbD]
                    my_cwb_dv = cwb_dv
                    if e >= 0:
                        @blk.tensor
                        def _(t):
                            if ti == 0:
                                t.wait_ge(s_w[1], w2_ld)
                            if my_cwb_dv:
                                t.wait_ge(s_dv, my_cwb_dv)
                            t.matmul(pbY[2][:, 0:w], ones_row_bf[:], cwrow[0:1, c0:c0 + w],
                                     start=True, stop=True).then_inc(s_mm, 1)
                        cwb_mm = cmm.inc()

                    hemm0 = {}
                    for m in range(16):
                        ci = m // 8
                        mm_ = m % 8
                        g0 = sil_guard[m % 4]

                        @blk.tensor
                        def _(t):
                            if ti == 0 and m == 0:
                                t.wait_ge(s_w[0], wsl_ld)
                                t.wait_ge(s_w[1], w2_ld)
                            if g0:
                                t.wait_ge(s_ev, g0)
                            for k in range(KD):
                                t.matmul(PBS[m % 4][:, 0:w],
                                         wsl[ci][:, k * 1024 + mm_ * 128:k * 1024 + (mm_ + 1) * 128],
                                         ln4b[:, k * TLOC + c0:k * TLOC + c0 + w],
                                         start=(k == 0), stop=False)
                            last = t.matmul(PBS[m % 4][:, 0:w], b3r[0:1, m * 128:(m + 1) * 128],
                                            ones5[0:1, 0:w], start=False, stop=True)
                            last.then_inc(s_mm, 1)
                        hemm0[m] = cmm.inc()

                    @blk.vector
                    def _(v):
                        if e >= 0:
                            v.wait_ge(s_mm, cwb_mm)
                            v.tensor_copy(out=cwe[:, 0:w], in_=pbY[2][:, 0:w])
                        for m in range(16):
                            v.wait_ge(s_mm, hemm0[m])
                            v.tensor_copy(out=h3e[:, m * 512:m * 512 + w], in_=PBS[m % 4][:, 0:w])
                            if m % 4 == 3:
                                v.drain().then_inc(s_dv, 4)
                    quads = [cdv.inc(4) for _ in range(4)]
                    cwb_dv = quads[0]
                    copies_all = quads[3]

                    hemm1 = {}
                    e_base = cev.v
                    for m in range(16):
                        ci = 2 + m // 8
                        mm_ = m % 8
                        g1 = (e_base + 4 * (m // 4)) if m >= 4 else 0

                        @blk.tensor
                        def _(t):
                            if m == 0:
                                t.wait_ge(s_dv, copies_all)
                            if g1:
                                t.wait_ge(s_ev, g1)
                            for k in range(KD):
                                t.matmul(PBS[m % 4][:, 0:w],
                                         wsl[ci][:, k * 1024 + mm_ * 128:k * 1024 + (mm_ + 1) * 128],
                                         ln4b[:, k * TLOC + c0:k * TLOC + c0 + w],
                                         start=(k == 0), stop=False)
                            last = t.matmul(PBS[m % 4][:, 0:w], b1r[0:1, m * 128:(m + 1) * 128],
                                            ones5[0:1, 0:w], start=False, stop=True)
                            last.then_inc(s_mm, 1)
                        hemm1[m] = cmm.inc()
                    sil_guard[0] = e_base + 4; sil_guard[1] = e_base + 8
                    sil_guard[2] = e_base + 12; sil_guard[3] = e_base + 16
                    h1_copies_all = e_base + 16
                    sg_done = e_base + 17

                    @blk.vector
                    def _(v):
                        v.wait_ge(s_ev, h1_copies_all)
                        v.tensor_mul(h1e[:], h1e[:], h3e[:])
                        v.drain().then_inc(s_dv, 1)
                    hh_dv = cdv.inc()

                    @blk.vector
                    def _(v):
                        v.wait_ge(s_dv, hh_dv)
                        v.tensor_single_scalar(glv, h1e[:, 0:8 * 512], LIM, ALU.min)
                        v.drain().then_inc(s_dv, 1)
                    gm_done = cdv.inc()

                    @blk.vector
                    def _(v):
                        v.wait_ge(s_ev, sg_done)
                        v.tensor_mul(asv, asv, glv)
                        v.tensor_scalar(out=glv, in0=h1e[:, 8 * 512:16 * 512], scalar1=LIM,
                                        scalar2=-LIM, op0=ALU.min, op1=ALU.max)
                        v.drain()
                        v.tensor_single_scalar(glv, glv, 1.0, ALU.add)
                        v.drain()
                        v.tensor_mul(asv, asv, glv)
                        if e >= 0:
                            for m in range(KD):
                                v.tensor_mul(asv[:, m * 512:m * 512 + w], asv[:, m * 512:m * 512 + w],
                                             cwe[:, 0:w])
                        v.drain().then_inc(s_dv, 1)
                    ae_dv = cdv.inc()

                    ydv_base = cdv.v
                    ymm = {}
                    for mo in range(KD):
                        yslot = mo % 2
                        pby = pbY[yslot]
                        if mo < 2:
                            yg = y_guard[yslot]
                        elif e >= 0:
                            yg = (s_dv, ydv_base + 2 * (mo // 2))
                        else:
                            yg = (s_ev, e_base + 17 + 2 * (mo // 2))

                        @blk.tensor
                        def _(t):
                            if mo == 0:
                                t.wait_ge(s_dv, ae_dv)
                            if yg:
                                t.wait_ge(yg[0], yg[1])
                            last = None
                            for k in range(KD):
                                last = t.matmul(pby[:, 0:w],
                                                w2s[:, k * D + mo * 128:k * D + (mo + 1) * 128],
                                                asv[:, k * 512:k * 512 + w],
                                                start=(k == 0), stop=(e >= 0 and k == KD - 1))
                            if e < 0:
                                last = t.matmul(pby[:, 0:w], sb2c[0:1, mo * 128:(mo + 1) * 128],
                                                ones5[0:1, 0:w], start=False, stop=True)
                            last.then_inc(s_mm, 1)
                        ymm[mo] = cmm.inc()

                    if e < 0:
                        pass
                    else:
                        @blk.vector
                        def _(v):
                            for mo in range(KD):
                                v.wait_ge(s_mm, ymm[mo])
                                v.tensor_add(ysb[:, mo * TLOC + c0:mo * TLOC + c0 + w],
                                             ysb[:, mo * TLOC + c0:mo * TLOC + c0 + w],
                                             pbY[mo % 2][:, 0:w])
                                if mo % 2 == 1:
                                    v.drain().then_inc(s_dv, 2)
                        cdv.inc(8)
                        y_guard[0] = (s_dv, cdv.v)
                        y_guard[1] = (s_dv, cdv.v)

                    my_prev_mm = prev_iter_mm

                    @blk.scalar
                    def _(sc):
                        if my_prev_mm:
                            sc.wait_ge(s_mm, my_prev_mm)
                        for m in range(16):
                            sc.wait_ge(s_mm, hemm1[m])
                            sc.copy(out=h1e[:, m * 512:m * 512 + w], in_=PBS[m % 4][:, 0:w])
                            if m % 4 == 3:
                                sc.drain().then_inc(s_ev, 4)
                        sc.wait_ge(s_dv, gm_done)
                        sc.activation(out=asv, in_=glv, func=AF.Sigmoid, scale=1.702)
                        sc.drain().then_inc(s_ev, 1)
                        if e < 0:
                            for mo in range(KD):
                                sc.wait_ge(s_mm, ymm[mo])
                                sc.copy(out=ysb[:, mo * TLOC + c0:mo * TLOC + c0 + w],
                                        in_=pbY[mo % 2][:, 0:w])
                                if mo % 2 == 1:
                                    sc.drain().then_inc(s_ev, 2)
                    cev.inc(17)
                    if e < 0:
                        cev.inc(8)
                        y_guard[0] = (s_ev, cev.v)
                        y_guard[1] = (s_ev, cev.v)
                    prev_iter_mm = cmm.v
                    if ti == len(T_TILES) - 1:
                        slab_free[ei] = cmm.v
                w2_free[ei] = cmm.v

                if e < 0:
                    for ti, (c0, w) in enumerate(T_TILES):
                        for mo in range(KD):
                            yslot = mo % 2
                            pby = pbY[yslot]
                            guard = y_guard[yslot]

                            @blk.tensor
                            def _(t):
                                if ti == 0 and mo == 0:
                                    t.wait_ge(s_gp, cw8_ld)
                                if guard:
                                    t.wait_ge(guard[0], guard[1])
                                last = t.matmul(pby[:, 0:w], b2sl[:, mo * 128:(mo + 1) * 128],
                                                cw8sb[:, c0:c0 + w], start=True, stop=True)
                                last.then_inc(s_mm, 1)
                            bt_mm = cmm.inc()

                            @blk.vector
                            def _(v):
                                v.wait_ge(s_mm, bt_mm)
                                v.tensor_add(ysb[:, mo * TLOC + c0:mo * TLOC + c0 + w],
                                             ysb[:, mo * TLOC + c0:mo * TLOC + c0 + w], pby[:, 0:w])
                                v.drain().then_inc(s_dv, 1)
                            y_guard[yslot] = (s_dv, cdv.inc())
            g4_dv = cdv.v
            g4_mm = cmm.v

        with ExitStack() as g5ctx:
            xb = [g5ctx.enter_context(nc.sbuf_tensor(f"xb5{i}", [128, KD * 512], F32, side="right")) for i in range(2)]
            yb = g5ctx.enter_context(nc.sbuf_tensor("yb5", [128, KD * 512], F32, side="right"))
            fin_prev = {}
            for ti, (c0, w) in enumerate(T_TILES):
                bi = ti % 2
                st_guard = csf[bi].v

                @blk.sync
                def _(s):
                    if ti == 0:
                        s.wait_ge(s_dv, g4_dv)
                        s.wait_ge(s_mm, g4_mm)
                    s.wait_ge(s_sf[bi], st_guard)
                    if ti >= 2:
                        s.wait_ge(s_dv, fin_prev[ti - 2])
                    s.dma_start(out=xb[bi][:].rearrange("p (k t) -> p k t", k=KD),
                                in_=_r_pk(x3_d)[:, :, c0:c0 + w]).then_inc(s_x[bi], 16)
                x_ld = cx_c[bi].inc(16)

                @blk.vector
                def _(v):
                    v.wait_ge(s_x[bi], x_ld)
                    if ti >= 1:
                        v.wait_ge(s_dv, fin_prev[ti - 1])
                    for k in range(KD):
                        v.tensor_copy(out=yb[:, k * 512:k * 512 + w],
                                      in_=ysb[:, k * TLOC + c0:k * TLOC + c0 + w])
                        v.drain()
                        v.tensor_add(xb[bi][:, k * 512:k * 512 + w],
                                     xb[bi][:, k * 512:k * 512 + w], yb[:, k * 512:k * 512 + w])
                        v.drain()
                    v.engine_nop().then_inc(s_dv, 1)
                fin_done = cdv.inc()

                @blk.sync
                def _(s):
                    s.wait_ge(s_dv, fin_done)
                    s.dma_start(out=fT[:, c0:c0 + w].rearrange("(k p) t -> p k t", p=128),
                                in_=xb[bi][:].rearrange("p (k t) -> p k t", k=KD)[:, :, 0:w]
                                ).then_inc(s_sf[bi], 16)
                csf[bi].inc(16)
                fin_prev[ti] = cdv.v

        @blk.sync
        def _(s):
            s.wait_ge(s_sf[0], csf[0].v)
            s.wait_ge(s_sf[1], csf[1].v)

    return nc



def _attn_reshard_idx():
    src_core = np.empty((NCORES, TLOC), dtype=np.int64)
    src_col = np.empty((NCORES, TLOC), dtype=np.int64)
    for c2 in range(NCORES):
        b, q = c2 // 4, c2 % 4
        tl = np.arange(TLOC)
        t = q * TLOC + tl
        i = t // 512
        r = 512 * b + (t % 512)
        src_core[c2] = r // 128
        src_col[c2] = 128 * i + (r % 128)
    return src_core, src_col


def kernel_unfused(**inputs):
    inputs = {k: np.asarray(v) for k, v in inputs.items()}
    x = inputs["L0_attn_x"]

    resA = run_bass_kernel_spmd(build_A(), prep_A_inmaps(inputs),
                                core_ids=list(range(NCORES))).results
    in_mapsB = []
    for c2 in range(NCORES):
        r0 = 512 * c2
        b = r0 // HID
        ch = r0 % HID
        a_sh = np.concatenate([np.asarray(resA[b * 4 + tq]["aT"])[ch:ch + 512, :] for tq in range(4)], axis=1)
        b_sh = np.concatenate([np.asarray(resA[b * 4 + tq]["bT"])[ch:ch + 512, :] for tq in range(4)], axis=1)
        in_mapsB.append({"a_cm": np.ascontiguousarray(a_sh), "b_cm": np.ascontiguousarray(b_sh)})
    resB = run_bass_kernel_spmd(build_B(), in_mapsB, core_ids=list(range(NCORES))).results
    h_full = np.concatenate([np.asarray(resB[c2]["h_cm"]) for c2 in range(NCORES)], axis=0)

    woutT = np.ascontiguousarray(inputs["L0_attn_W_out"].T)
    wg0 = np.ascontiguousarray(inputs["L0_ffn_Wg"])
    ident = np.eye(128, dtype=np.float32)
    in_mapsC = []
    for c in range(NCORES):
        b, q = c // 4, c % 4
        t0 = q * TLOC
        in_mapsC.append({
            "xT": np.ascontiguousarray(x[b, t0:t0 + TLOC].T),
            "gateT_i": np.asarray(resA[c]["gateT"]),
            "hT_i": np.ascontiguousarray(h_full[b * HID:(b + 1) * HID, t0:t0 + TLOC]),
            "woutT": woutT, "wg0_i": wg0, "ident_i": ident,
        })
    resC = run_bass_kernel_spmd(build_C(), in_mapsC, core_ids=list(range(NCORES))).results

    cnts_all = np.ascontiguousarray(
        np.stack([np.asarray(resC[r]["cnt_o"]).ravel() for r in range(NCORES)]).astype(np.float32))
    in_mapsD = []
    for c in range(NCORES):
        mask = np.zeros((8, 1), dtype=np.float32)
        mask[:c] = 1.0
        in_mapsD.append({
            "ln2_i": np.asarray(resC[c]["ln2T"]), "x1_i": np.asarray(resC[c]["x1T"]),
            "I_i": np.asarray(resC[c]["I_o"]), "WI_i": np.asarray(resC[c]["WI_o"]),
            "cnts_i": cnts_all, "mask_i": mask,
            "W1_i": inputs["L0_ffn_W1"], "W3_i": inputs["L0_ffn_W3"], "W2_i": inputs["L0_ffn_W2"],
            "Ws1_i": inputs["L0_ffn_Ws1"], "Ws3_i": inputs["L0_ffn_Ws3"], "Ws2_i": inputs["L0_ffn_Ws2"],
            "ident_i": ident,
        })
    resD = run_bass_kernel_spmd(build_D(), in_mapsD, core_ids=list(range(NCORES))).results

    ln3_full = {}
    for c in range(NCORES):
        b, q = c // 4, c % 4
        ln3_full[(b, q)] = np.asarray(resD[c]["ln3T"])
    def chunk_slice(c):
        b = c // 4
        cols = []
        for i in range(16):
            t0 = 512 * i + 128 * (c % 4)
            q, off = divmod(t0, TLOC)
            cols.append(ln3_full[(b, q)][:, off:off + 128])
        return np.ascontiguousarray(np.concatenate(cols, axis=1))
    ln3s = [chunk_slice(c) for c in range(NCORES)]
    w0 = np.ascontiguousarray(inputs["L1_attn_w0"])
    w2a = np.ascontiguousarray(inputs["L1_attn_w2"])
    w1a = np.ascontiguousarray(inputs["L1_attn_w1"])
    in_mapsE = [{"ln3s_i": ln3s[c], "w0_i": w0, "w2_i": w2a} for c in range(NCORES)]
    resE = run_bass_kernel_spmd(build_E(), in_mapsE, core_ids=list(range(NCORES))).results

    in_mapsF = [{"uT_i": np.asarray(resE[c]["uT"]), "vT_i": np.asarray(resE[c]["vT"]),
                 "ln3s_i": ln3s[c], "w1_i": w1a} for c in range(NCORES)]
    resF = run_bass_kernel_spmd(build_F(), in_mapsF, core_ids=list(range(NCORES))).results

    outF = np.stack([np.asarray(resF[c]["outT"]) for c in range(NCORES)])
    s_core, s_col = _attn_reshard_idx()
    perm = np.concatenate([np.arange(0, 2048, 2), np.arange(1, 2048, 2)])
    w1 = inputs["L1_ffn_w1"]; w3 = inputs["L1_ffn_w3"]; w2 = inputs["L1_ffn_w2"]
    w1pT = np.ascontiguousarray(np.stack([w1[e].T[:, perm] for e in range(8)]))
    w3pT = np.ascontiguousarray(np.stack([w3[e].T[:, perm] for e in range(8)]))
    w2T = np.ascontiguousarray(np.stack([w2[e].T for e in range(8)]))
    b1p = np.ascontiguousarray(inputs["L1_ffn_b1"][:, perm])
    b3p = np.ascontiguousarray(inputs["L1_ffn_b3"][:, perm])
    sw1pT = np.ascontiguousarray(inputs["L1_ffn_sw1"].T[:, perm])
    sw3pT = np.ascontiguousarray(inputs["L1_ffn_sw3"].T[:, perm])
    sw2T = np.ascontiguousarray(inputs["L1_ffn_sw2"].T)
    sb1p = np.ascontiguousarray(inputs["L1_ffn_sb1"][perm][None, :])
    sb3p = np.ascontiguousarray(inputs["L1_ffn_sb3"][perm][None, :])
    sb2r = np.ascontiguousarray(inputs["L1_ffn_sb2"][None, :])
    gwT = np.ascontiguousarray(inputs["L1_ffn_gate_w"].T)
    gbr = np.ascontiguousarray(inputs["L1_ffn_gate_b"][None, :])
    in_mapsG = []
    for c in range(NCORES):
        at_c = np.ascontiguousarray(outF[s_core[c], :, s_col[c]].T)
        in_mapsG.append({
            "x2_i": np.asarray(resD[c]["x2T"]), "at_i": at_c,
            "gwT_i": gwT, "gb_i": gbr,
            "w1p_i": w1pT, "b1p_i": b1p, "w3p_i": w3pT, "b3p_i": b3p,
            "w2t_i": w2T, "b2_i": inputs["L1_ffn_b2"],
            "sw1p_i": sw1pT, "sb1p_i": sb1p, "sw3p_i": sw3pT, "sb3p_i": sb3p,
            "sw2t_i": sw2T, "sb2_i": sb2r, "ident_i": ident,
        })
    resG = run_bass_kernel_spmd(build_G(), in_mapsG, core_ids=list(range(NCORES))).results

    out = np.empty((2, 8192, D), dtype=np.float32)
    for c in range(NCORES):
        b, q = c // 4, c % 4
        out[b, q * TLOC:(q + 1) * TLOC, :] = np.asarray(resG[c]["fT"]).T
    return out





_scope_state = {}

def _scope_next(nc, name):
    st = _scope_state.setdefault(id(nc), {})
    if st.get("open"):
        nc.leave_named_scope(st["name"], st["sid"], False)
        st["open"] = False
    if name is not None:
        sid, _ = nc.enter_named_scope(name, False)
        st.update(open=True, name=name, sid=sid)

def build_FUSED(debug=False):
    nc = bass.Bass("TRN2", target_bir_lowering=False)
    xT = nc.dram_tensor("xT", [D, THAL], F32, kind="ExternalInput").ap()
    winT = nc.dram_tensor("winT", [D, G4], F32, kind="ExternalInput").ap()
    wgT = nc.dram_tensor("wgT", [HID, G4], F32, kind="ExternalInput").ap()
    conv_w_r = nc.dram_tensor("conv_w_r", [128, 64], F32, kind="ExternalInput").ap()
    conv_b_r = nc.dram_tensor("conv_b_r", [128, 16], F32, kind="ExternalInput").ap()
    bg_r = nc.dram_tensor("bg_r", [128, 32], F32, kind="ExternalInput").ap()
    fb_r = nc.dram_tensor("fb_r", [128, 16], F32, kind="ExternalInput").ap()
    woutT = nc.dram_tensor("woutT", [HID, D], F32, kind="ExternalInput").ap()
    wg0_i = nc.dram_tensor("wg0_i", [D, 32], F32, kind="ExternalInput").ap()
    ident_i = nc.dram_tensor("ident_i", [128, 128], F32, kind="ExternalInput").ap()
    mask_i = nc.dram_tensor("mask_i", [8, 1], F32, kind="ExternalInput").ap()
    mskp_i = nc.dram_tensor("mskp_i", [128, 8], F32, kind="ExternalInput").ap()
    W1_i = nc.dram_tensor("W1_i", [32, D, 512], F32, kind="ExternalInput").ap()
    W3_i = nc.dram_tensor("W3_i", [32, D, 512], F32, kind="ExternalInput").ap()
    W2_i = nc.dram_tensor("W2_i", [32, 512, D], F32, kind="ExternalInput").ap()
    Ws1_i = nc.dram_tensor("Ws1_i", [D, D], F32, kind="ExternalInput").ap()
    Ws3_i = nc.dram_tensor("Ws3_i", [D, D], F32, kind="ExternalInput").ap()
    Ws2_i = nc.dram_tensor("Ws2_i", [D, D], F32, kind="ExternalInput").ap()
    w0_i = nc.dram_tensor("w0_i", [D, G4], F32, kind="ExternalInput").ap()
    w2a_i = nc.dram_tensor("w2a_i", [D, G4], F32, kind="ExternalInput").ap()
    w1a_i = nc.dram_tensor("w1a_i", [G4, D], F32, kind="ExternalInput").ap()
    gwT_i = nc.dram_tensor("gwT_i", [D, 8], F32, kind="ExternalInput").ap()
    gb_i = nc.dram_tensor("gb_i", [1, 8], F32, kind="ExternalInput").ap()
    w1p_i = nc.dram_tensor("w1p_i", [8, D, 2048], F32, kind="ExternalInput").ap()
    b1p_i = nc.dram_tensor("b1p_i", [8, 2048], F32, kind="ExternalInput").ap()
    w3p_i = nc.dram_tensor("w3p_i", [8, D, 2048], F32, kind="ExternalInput").ap()
    b3p_i = nc.dram_tensor("b3p_i", [8, 2048], F32, kind="ExternalInput").ap()
    w2t_i = nc.dram_tensor("w2t_i", [8, D, D], F32, kind="ExternalInput").ap()
    b2_i = nc.dram_tensor("b2_i", [8, D], F32, kind="ExternalInput").ap()
    sw1p_i = nc.dram_tensor("sw1p_i", [D, 2048], F32, kind="ExternalInput").ap()
    sb1p_i = nc.dram_tensor("sb1p_i", [1, 2048], F32, kind="ExternalInput").ap()
    sw3p_i = nc.dram_tensor("sw3p_i", [D, 2048], F32, kind="ExternalInput").ap()
    sb3p_i = nc.dram_tensor("sb3p_i", [1, 2048], F32, kind="ExternalInput").ap()
    sw2t_i = nc.dram_tensor("sw2t_i", [D, D], F32, kind="ExternalInput").ap()
    sb2_i = nc.dram_tensor("sb2_i", [1, D], F32, kind="ExternalInput").ap()
    eoff_i = nc.dram_tensor("eoff_i", [1, 4], mybir.dt.int32, kind="ExternalInput").ap()
    goff_i = nc.dram_tensor("goff_i", [1, 4], mybir.dt.int32, kind="ExternalInput").ap()
    fT = nc.dram_tensor("fT", [D, TLOC], F32, kind="ExternalOutput").ap()
    dbg_kind = {"kind": "ExternalOutput"} if debug else {}
    gate_d = nc.dram_tensor("gate_d", [HID, TLOC], BF16, **dbg_kind)
    a_d = nc.dram_tensor("a_d", [HID, TLOC], F32, **dbg_kind)
    b_d = nc.dram_tensor("b_d", [HID, TLOC], F32, **dbg_kind)
    sum_in = nc.dram_tensor("sum_in", [128, 32], F32)
    sum_out = nc.dram_tensor("sum_out", [8 * 128, 32], F32, addr_space="Shared")
    cnt_in = nc.dram_tensor("cnt_in", [1, 32], F32)
    cnt_out = nc.dram_tensor("cnt_out", [8, 32], F32, addr_space="Shared")
    pref_d = nc.dram_tensor("pref_d", [1, 32], F32)
    cw_d = nc.dram_tensor("cw_d", [32, TLOC], F32)
    ln3_in = nc.dram_tensor("ln3_in", [D, TLOC], BF16)
    ln3_out = nc.dram_tensor("ln3_out", [8 * D, TLOC], BF16, addr_space="Shared")
    uT_d = nc.dram_tensor("uT_d", [G4, TLOC], BF16)
    vT_d = nc.dram_tensor("vT_d", [G4, TLOC], BF16)
    agl_in = nc.dram_tensor("agl_in", [1, 1], F32)
    agl_out = nc.dram_tensor("agl_out", [8, 1], F32, addr_space="Shared")
    ato_in = nc.dram_tensor("ato_in", [D, TLOC], BF16)
    ato_out = nc.dram_tensor("ato_out", [8 * D, TLOC], BF16, addr_space="Shared")
    cw8_d = nc.dram_tensor("cw8_d", [8, TLOC], F32)
    x3_d = nc.dram_tensor("x3_d", [D, TLOC], F32)
    ln3s_d = nc.dram_tensor("ln3s_d", [D, TLOC], BF16)
    lnrb_d = nc.dram_tensor("lnrb_d", [4 * 1024, 1664], BF16)
    atrb_d = nc.dram_tensor("atrb_d", [4 * 1024, 512], BF16)

    DT8 = [(i * 512, 512) for i in range(4)]
    LIM = 7.0
    KF = G4 // 128

    with ExitStack() as ctx:
        def sb(name, shape, dt=F32, side="left"):
            return ctx.enter_context(nc.sbuf_tensor(name, shape, dt, side=side))
        def sem(name):
            return ctx.enter_context(nc.semaphore(name))
        blk = ctx.enter_context(nc.Block())

        ones_bf = sb("ones_bf", [128, 1], BF16)
        ones_row = sb("ones_row", [1, 128])
        ones_col = sb("ones_col", [128, 1])
        ones_row_bf = sb("ones_row_bf", [1, 128], BF16)
        ones5 = sb("ones5", [1, 512], BF16)
        ident = sb("ident", [128, 128])
        ident_bf = sb("ident_bf", [128, 128], BF16)
        it32 = sb("it32", [128, 32])
        it8 = sb("it8", [128, 8])
        cw_t = sb("cw_t", [128, 64])
        cb_t = sb("cb_t", [128, 16])
        bg_t = sb("bg_t", [128, 32])
        fb_t = sb("fb_t", [128, 16])
        c_col = sb("c_col", [128, 16])
        sp1 = sb("sp1", [128, 16])
        sp2 = sb("sp2", [128, 16])
        wg0 = sb("wg0", [128, KD * 32])
        gwb = sb("gwb", [128, KD * 8])
        gb_b = sb("gb_b", [128, 8])
        gbrow = sb("gbrow", [1, 8])
        maskb = sb("maskb", [8, 1])
        mskp = sb("mskp", [128, 8])
        eoff = sb("eoff", [1, 4], mybir.dt.int32)
        goff = sb("goff", [1, 4], mybir.dt.int32)
        neg_bg = sb("neg_bg", [128, 32])
        carA = sb("carA", [128, 16])
        carB = sb("carB", [128, 16])
        hin = sb("hin", [128, 16])
        zcol = sb("zcol", [128, 1])
        prefc = sb("prefc", [32, 1])

        xres_ctx = ExitStack()
        xres = xres_ctx.enter_context(nc.sbuf_tensor("xres", [128, KD * THAL], F32))
        m_row = xres_ctx.enter_context(nc.sbuf_tensor("m_row", [1, 512], F32))
        q_row = xres_ctx.enter_context(nc.sbuf_tensor("q_row", [1, 512], F32))
        v_row = xres_ctx.enter_context(nc.sbuf_tensor("v_row", [1, 512], F32))
        s_row = xres_ctx.enter_context(nc.sbuf_tensor("s_row", [1, 512], F32))
        r_row = xres_ctx.enter_context(nc.sbuf_tensor("r_row", [1, 512], F32))

        s_c = sem("s_c"); CC = Cnt()
        s_cc = sem("s_cc"); CCC = Cnt()

        @blk.sync
        def _(s):
            s.dma_start(out=cw_t[:], in_=conv_w_r).then_inc(s_c, 16)
            s.dma_start(out=cb_t[:], in_=conv_b_r).then_inc(s_c, 16)
            s.dma_start(out=bg_t[:], in_=bg_r).then_inc(s_c, 16)
            s.dma_start(out=fb_t[:], in_=fb_r).then_inc(s_c, 16)
            s.dma_start(out=ident[:], in_=ident_i).then_inc(s_c, 16)
            s.dma_start(out=wg0[:].rearrange("p (k m) -> p k m", k=KD), in_=_r_pk(wg0_i)).then_inc(s_c, 16)
            s.dma_start(out=gwb[:].rearrange("p (k m) -> p k m", k=KD), in_=_r_pk(gwT_i)).then_inc(s_c, 16)
            s.dma_start(out=gbrow[:], in_=gb_i).then_inc(s_c, 16)
            s.dma_start(out=maskb[:], in_=mask_i).then_inc(s_c, 16)
            s.dma_start(out=mskp[:], in_=mskp_i).then_inc(s_c, 16)
            s.dma_start(out=xres[:].rearrange("p (k t) -> p k t", k=KD), in_=_r_pk(xT)).then_inc(s_c, 16)
        CC.inc(16 * 11)
        CONSTS_LD = CC.v

        @blk.gpsimd
        def _(g):
            g.iota(it32[:], pattern=[[1, 32]], base=0, channel_multiplier=0,
                   allow_small_or_imprecise_dtypes=True)
            g.iota(it8[:], pattern=[[1, 8]], base=0, channel_multiplier=0,
                   allow_small_or_imprecise_dtypes=True)
            g.engine_nop().then_inc(s_cc, 1)
        IOTA_DONE = CCC.inc()

        s_dv0 = sem("s_dv0"); DV0 = Cnt()
        s_ev0 = sem("s_ev0"); EV0 = Cnt()
        s_mm0 = sem("s_mm0"); MM0 = Cnt()

        @blk.vector
        def _(v):
            v.memset(ones_bf[:], 1.0)
            v.memset(ones_row[:], 1.0)
            v.memset(ones_col[:], 1.0)
            v.memset(ones_row_bf[:], 1.0)
            v.memset(ones5[:], 1.0)
            v.memset(zcol[:], 0.0)
            v.memset(carA[:], 0.0)
            v.memset(carB[:], 0.0)
            v.wait_ge(s_c, CONSTS_LD)
            v.tensor_copy(out=ident_bf[:], in_=ident[:])
            v.drain().then_inc(s_dv0, 1)
        MEMSETS = DV0.inc()

        @blk.scalar
        def _(sc):
            sc.wait_ge(s_c, CONSTS_LD)
            sc.wait_ge(s_dv0, MEMSETS)
            sc.activation(out=sp1[:], in_=fb_t[:], func=AF.Abs)
            sc.drain()
            sc.activation(out=sp2[:], in_=sp1[:], func=AF.Exp, scale=-1.0)
            sc.drain()
            sc.activation(out=sp2[:], in_=sp2[:], func=AF.Ln, bias=ones_col[:])
            sc.drain()
            sc.activation(out=sp1[:], in_=fb_t[:], func=AF.Relu)
            sc.drain().then_inc(s_ev0, 1)
        LN1P = EV0.inc()

        @blk.vector
        def _(v):
            v.wait_ge(s_ev0, LN1P)
            v.tensor_add(c_col[:], sp1[:], sp2[:])
            v.drain()
            v.tensor_single_scalar(c_col[:], c_col[:], -8.0, ALU.mult)
            v.drain()
            v.tensor_single_scalar(neg_bg[:], bg_t[:], -1.0, ALU.mult)
            v.drain().then_inc(s_dv0, 1)
        CCOL = DV0.inc()

        def barrier(waits):
            for engf in [blk.sync, blk.gpsimd, blk.vector, blk.scalar, blk.tensor]:
                @engf
                def _(e):
                    for sm, vv in waits:
                        if vv:
                            e.wait_ge(sm, vv)

        _scope_next(nc, "phA")
        pA = ExitStack()
        z512 = pA.enter_context(nc.sbuf_tensor("z512", [128, 512], F32, side="right"))
        sumsb = pA.enter_context(nc.sbuf_tensor("sumsb", [128, 8 * 32], F32, side="right"))
        xcp = pA.enter_context(nc.sbuf_tensor("xcp", [128, KH * TLOC], BF16, side="right"))
        xn_ctx = ExitStack()
        xnT = xn_ctx.enter_context(nc.sbuf_tensor("xnT", [128, KD * THAL], BF16, side="right"))
        s_mmA = sem("s_mmA"); s_evA = sem("s_evA"); s_dvA = sem("s_dvA")
        cmm, cev, cdv = Cnt(), Cnt(), Cnt()
        s_wA = [sem("s_wA0"), sem("s_wA1")]
        cw = [Cnt(), Cnt()]
        s_gA = [sem("s_gA0"), sem("s_gA1")]
        cg = [Cnt(), Cnt()]
        s_ab = [sem("s_ab0"), sem("s_ab1")]
        cab = [Cnt(), Cnt()]

        with ExitStack() as a1ctx:
            xbf = [a1ctx.enter_context(nc.sbuf_tensor("xbf0", [128, KD * 512], BF16, side="right"))] * 2
            xsq = [a1ctx.enter_context(nc.sbuf_tensor("xsq0", [128, KD * 512], BF16, side="right"))] * 2
            ltmp = a1ctx.enter_context(nc.sbuf_tensor("ltmp", [128, 512], F32, side="right"))
            pb = [a1ctx.enter_context(nc.psum_tensor(f"pbA1_{i}", [128, 512], F32)) for i in range(4)]
            xn_done = {}
            for i, (c0, w) in enumerate(A1_TILES):
                bi = i % 2

                @blk.vector
                def _(v):
                    if i == 0:
                        v.wait_ge(s_c, CONSTS_LD)
                        v.wait_ge(s_dv0, CCOL)
                        v.memset(z512[:], 0.0)
                        v.drain()
                    if i >= 1:
                        v.wait_ge(s_mmA, stats_of[i - 1])
                    for k in range(KD):
                        v.tensor_copy(out=xbf[bi][:, k * 512:k * 512 + w],
                                      in_=xres[:, k * THAL + c0:k * THAL + c0 + w])
                    v.drain().then_inc(s_dvA, 1)
                cast_done = cdv.inc()
                if i == 0:
                    stats_of = {}

                @blk.scalar
                def _(sc):
                    sc.wait_ge(s_dvA, cast_done)
                    sc.activation(out=xsq[bi][:].rearrange("p (k t) -> p k t", k=KD)[:, :, 0:w],
                                  in_=xbf[bi][:].rearrange("p (k t) -> p k t", k=KD)[:, :, 0:w],
                                  func=AF.Square)
                    sc.drain().then_inc(s_evA, 1)
                sq_done = cev.inc()

                @blk.tensor
                def _(t):
                    t.wait_ge(s_evA, sq_done)
                    if i >= 1:
                        t.wait_ge(s_dvA, xn_done[i - 1])
                    for k in range(KD):
                        t.matmul(pb[0][0:1, 0:w], ones_bf[:], xbf[bi][:, k * 512:k * 512 + w],
                                 start=(k == 0), stop=(k == KD - 1))
                    last = None
                    for k in range(KD):
                        last = t.matmul(pb[1][0:1, 0:w], ones_bf[:], xsq[bi][:, k * 512:k * 512 + w],
                                        start=(k == 0), stop=(k == KD - 1))
                    last.then_inc(s_mmA, 1)
                stats_of[i] = cmm.inc()

                @blk.vector
                def _(v):
                    v.wait_ge(s_mmA, stats_of[i])
                    v.tensor_single_scalar(m_row[:, 0:w], pb[0][0:1, 0:w], 1.0 / D, ALU.mult)
                    v.tensor_single_scalar(q_row[:, 0:w], pb[1][0:1, 0:w], 1.0 / D, ALU.mult)
                    v.drain()
                    v.tensor_mul(v_row[:, 0:w], m_row[:, 0:w], m_row[:, 0:w])
                    v.drain()
                    v.tensor_sub(v_row[:, 0:w], q_row[:, 0:w], v_row[:, 0:w])
                    v.drain()
                    v.tensor_single_scalar(v_row[:, 0:w], v_row[:, 0:w], EPS, ALU.add)
                    v.drain().then_inc(s_dvA, 1)
                var_done = cdv.inc()

                @blk.scalar
                def _(sc):
                    sc.wait_ge(s_dvA, var_done)
                    sc.activation(out=s_row[:, 0:w], in_=v_row[:, 0:w], func=AF.Sqrt)
                    sc.drain().then_inc(s_evA, 1)
                std_done = cev.inc()

                @blk.vector
                def _(v):
                    v.wait_ge(s_evA, std_done)
                    v.reciprocal(r_row[:, 0:w], s_row[:, 0:w])
                    v.drain().then_inc(s_dvA, 1)
                rinv_done = cdv.inc()

                @blk.tensor
                def _(t):
                    t.wait_ge(s_dvA, rinv_done)
                    t.matmul(pb[2][:, 0:w], ones_row[:], m_row[:, 0:w], start=True, stop=True)
                    last = t.matmul(pb[3][:, 0:w], ones_row[:], r_row[:, 0:w], start=True, stop=True)
                    last.then_inc(s_mmA, 1)
                bc_mm = cmm.inc()

                @blk.vector
                def _(v):
                    v.wait_ge(s_mmA, bc_mm)
                    for k in range(KD):
                        v.tensor_sub(ltmp[:, 0:w],
                                     xres[:, k * THAL + c0:k * THAL + c0 + w], pb[2][:, 0:w])
                        v.drain()
                        v.tensor_mul(xnT[:, k * THAL + c0:k * THAL + c0 + w],
                                     ltmp[:, 0:w], pb[3][:, 0:w])
                        v.drain()
                    v.drain().then_inc(s_dvA, 1)
                xn_done[i] = cdv.inc()
            a1_all_done = xn_done[len(A1_TILES) - 1]
            a1_mm = cmm.v
        pA_a1_done = (s_dvA, a1_all_done)

        with ExitStack() as a2ctx:
            wsl = [a2ctx.enter_context(nc.sbuf_tensor(f"wslA{i}", [128, KD * 128], BF16, side="right")) for i in range(2)]
            xc_f32 = a2ctx.enter_context(nc.sbuf_tensor("xc_f32", [128, THAL], F32, side="right"))
            acc = [a2ctx.enter_context(nc.sbuf_tensor(f"accA{i}", [128, TLOC // 2], F32, side="right")) for i in range(2)]
            gst = [a2ctx.enter_context(nc.sbuf_tensor(f"gstA{i}", [128, 512], BF16, side="right")) for i in range(2)]
            pb = [a2ctx.enter_context(nc.psum_tensor(f"pbA2_{i}", [128, 512], F32)) for i in range(2)]

            mt_order = list(range(16, 32)) + list(range(16))
            conv_done = {}
            slab_free = {}
            pb_last_ev = {0: 0, 1: 0}
            mmcnt = 0
            for oi, mt in enumerate(mt_order):
                wbi = oi % 2
                is_xc = mt >= 16
                ct = mt - 16
                tts = A1_TILES if is_xc else OWN_TILES

                @blk.gpsimd
                def _(g):
                    if oi >= 2:
                        g.wait_ge(s_mmA, slab_free[oi - 2])
                    else:
                        g.wait_ge(s_mmA, a1_mm)
                        g.wait_ge(s_dvA, a1_all_done)
                    g.dma_start(out=wsl[wbi][:].rearrange("p (k m) -> p k m", k=KD),
                                in_=_r_pk(winT)[:, :, mt * 128:(mt + 1) * 128]).then_inc(s_wA[wbi], 16)
                w_ld = cw[wbi].inc(16)

                for ti, (c0, w) in enumerate(tts):
                    par = mmcnt % 2
                    pbx = pb[par]
                    mmcnt += 1
                    psum_guard = pb_last_ev[par]

                    @blk.tensor
                    def _(t):
                        if ti == 0:
                            t.wait_ge(s_wA[wbi], w_ld)
                        if psum_guard:
                            t.wait_ge(s_evA, psum_guard)
                        last = None
                        for k in range(KD):
                            last = t.matmul(pbx[:, 0:w], wsl[wbi][:, k * 128:(k + 1) * 128],
                                            xnT[:, k * THAL + c0:k * THAL + c0 + w],
                                            start=(k == 0), stop=(k == KD - 1))
                        last.then_inc(s_mmA, 1)
                    mm_done = cmm.inc()

                    if is_xc:
                        @blk.scalar
                        def _(sc):
                            sc.wait_ge(s_mmA, mm_done)
                            if ti == 0 and oi >= 1:
                                sc.wait_ge(s_dvA, conv_done[ct - 1])
                            sc.copy(out=xc_f32[:, c0:c0 + w], in_=pbx[:, 0:w])
                            sc.drain().then_inc(s_evA, 1)
                        pb_last_ev[par] = cev.inc()
                    else:
                        stage = ti % 2
                        st_guard = cg[stage].v

                        @blk.scalar
                        def _(sc):
                            sc.wait_ge(s_mmA, mm_done)
                            if st_guard:
                                sc.wait_ge(s_gA[stage], st_guard)
                            sc.copy(out=gst[stage][:], in_=pbx[:, 0:w])
                            sc.drain().then_inc(s_evA, 1)
                        pb_last_ev[par] = cev.inc()
                        ev_here = pb_last_ev[par]

                        @blk.sync
                        def _(s):
                            s.wait_ge(s_evA, ev_here)
                            s.dma_start(out=gate_d[mt * 128:(mt + 1) * 128, c0 - 3:c0 - 3 + w],
                                        in_=gst[stage][:]).then_inc(s_gA[stage], 16)
                        cg[stage].inc(16)
                slab_free[oi] = cmm.v

                if is_xc:
                    my_ev = cev.v

                    @blk.vector
                    def _(v):
                        v.wait_ge(s_evA, my_ev)
                        HL = TLOC // 2
                        for hh in range(2):
                            o0 = hh * HL
                            v.tensor_single_scalar(acc[0][:], xc_f32[:, o0:o0 + HL],
                                                   cw_t[:, 4 * ct + 0:4 * ct + 1], ALU.mult)
                            v.drain()
                            v.scalar_tensor_tensor(acc[1][:], xc_f32[:, o0 + 1:o0 + 1 + HL],
                                                   cw_t[:, 4 * ct + 1:4 * ct + 2],
                                                   acc[0][:], ALU.mult, ALU.add)
                            v.drain()
                            v.scalar_tensor_tensor(acc[0][:], xc_f32[:, o0 + 2:o0 + 2 + HL],
                                                   cw_t[:, 4 * ct + 2:4 * ct + 3],
                                                   acc[1][:], ALU.mult, ALU.add)
                            v.drain()
                            v.scalar_tensor_tensor(acc[1][:], xc_f32[:, o0 + 3:o0 + 3 + HL],
                                                   cw_t[:, 4 * ct + 3:4 * ct + 4],
                                                   acc[0][:], ALU.mult, ALU.add)
                            v.drain()
                            v.tensor_single_scalar(xcp[:, ct * TLOC + o0:ct * TLOC + o0 + HL], acc[1][:],
                                                   cb_t[:, ct:ct + 1], ALU.add)
                            v.drain()
                        v.engine_nop().then_inc(s_dvA, 1)
                    conv_done[ct] = cdv.inc()
            a2_conv_all = cdv.v
            a2_end_mm = cmm.v
            a2_g0, a2_g1 = cg[0].v, cg[1].v
        xn_ctx.close()

        with ExitStack() as a3ctx:
            wf_sl = [a3ctx.enter_context(nc.sbuf_tensor(f"wfF{i}", [128, KH * 128], BF16, side="right")) for i in range(2)]
            wi_sl = [a3ctx.enter_context(nc.sbuf_tensor(f"wiF{i}", [128, KH * 128], BF16, side="right")) for i in range(2)]
            tFa = [a3ctx.enter_context(nc.sbuf_tensor(f"tFa{i}", [128, TLOC], F32, side="right")) for i in range(2)]
            tBb = [a3ctx.enter_context(nc.sbuf_tensor(f"tBb{i}", [128, TLOC], F32, side="right")) for i in range(2)]
            tI = a3ctx.enter_context(nc.sbuf_tensor("tIa", [128, TLOC], F32, side="right"))
            rsA = a3ctx.enter_context(nc.sbuf_tensor("rsA", [128, 1], F32, side="right"))
            pbF = [a3ctx.enter_context(nc.psum_tensor(f"pbA3_{i}", [128, 512], F32)) for i in range(8)]

            evF_done = {}
            evI_done = {}
            ladder_dv = {}
            tIfree_dv = {}
            slab_mm = {}
            for ctc in range(16):
                wbi = ctc % 2
                b = ctc % 2

                @blk.gpsimd
                def _(g):
                    if ctc >= 2:
                        g.wait_ge(s_mmA, slab_mm[ctc - 2])
                    else:
                        g.wait_ge(s_mmA, a2_end_mm)
                        g.wait_ge(s_dvA, a2_conv_all)
                        g.wait_ge(s_gA[0], a2_g0)
                        g.wait_ge(s_gA[1], a2_g1)
                    g.dma_start(out=wf_sl[wbi][:].rearrange("p (k m) -> p k m", k=KH),
                                in_=_r_pk(wgT)[:, :, ctc * 128:(ctc + 1) * 128]).then_inc(s_wA[wbi], 16)
                    g.dma_start(out=wi_sl[wbi][:].rearrange("p (k m) -> p k m", k=KH),
                                in_=_r_pk(wgT)[:, :, HID + ctc * 128:HID + (ctc + 1) * 128]).then_inc(s_wA[wbi], 16)
                w_ld = cw[wbi].inc(32)

                @blk.tensor
                def _(t):
                    t.wait_ge(s_wA[wbi], w_ld)
                    if ctc >= 1:
                        t.wait_ge(s_evA, evF_done[ctc - 1])
                    last = None
                    for k in range(KH):
                        for tti in range(4):
                            last = t.matmul(pbF[tti][:, :], wf_sl[wbi][:, k * 128:(k + 1) * 128],
                                            xcp[:, k * TLOC + tti * 512:k * TLOC + (tti + 1) * 512],
                                            start=(k == 0), stop=(k == KH - 1))
                    last.then_inc(s_mmA, 1)
                f_mm = cmm.inc()

                @blk.tensor
                def _(t):
                    if ctc >= 1:
                        t.wait_ge(s_evA, evI_done[ctc - 1])
                    last = None
                    for k in range(KH):
                        for tti in range(4):
                            last = t.matmul(pbF[4 + tti][:, :], wi_sl[wbi][:, k * 128:(k + 1) * 128],
                                            xcp[:, k * TLOC + tti * 512:k * TLOC + (tti + 1) * 512],
                                            start=(k == 0), stop=(k == KH - 1))
                    last.then_inc(s_mmA, 1)
                i_mm = cmm.inc()
                slab_mm[ctc] = cmm.v

                @blk.scalar
                def _(sc):
                    sc.wait_ge(s_mmA, f_mm)
                    if ctc >= 2:
                        sc.wait_ge(s_dvA, ladder_dv[ctc - 2])
                        sc.wait_ge(s_ab[b], cab[b].v)
                    for tti in range(4):
                        sc.activation(out=tFa[b][:, tti * 512:(tti + 1) * 512], in_=pbF[tti][:, :],
                                      func=AF.Sigmoid, bias=bg_t[:, ctc:ctc + 1])
                    sc.drain().then_inc(s_evA, 1)
                evF_done[ctc] = cev.inc()

                @blk.scalar
                def _(sc):
                    sc.wait_ge(s_mmA, i_mm)
                    if ctc >= 1:
                        sc.wait_ge(s_dvA, tIfree_dv[ctc - 1])
                    for tti in range(4):
                        sc.activation(out=tI[:, tti * 512:(tti + 1) * 512], in_=pbF[4 + tti][:, :],
                                      func=AF.Sigmoid, bias=bg_t[:, 16 + ctc:17 + ctc])
                    sc.drain().then_inc(s_evA, 1)
                evI_done[ctc] = cev.inc()

                @blk.vector
                def _(v):
                    v.wait_ge(s_evA, evF_done[ctc])
                    v.tensor_single_scalar(tFa[b][:], tFa[b][:], c_col[:, ctc:ctc + 1], ALU.mult)
                    v.drain()
                    v.reduce_sum(rsA[:], tFa[b][:], axis=mybir.AxisListType.X)
                    v.drain()
                    v.tensor_add(carA[:, ctc:ctc + 1], carA[:, ctc:ctc + 1], rsA[:])
                    v.drain().then_inc(s_dvA, 1)
                la_dv = cdv.inc()

                @blk.scalar
                def _(sc):
                    sc.wait_ge(s_dvA, la_dv)
                    sc.activation(out=tFa[b][:], in_=tFa[b][:], func=AF.Exp)
                    sc.drain().then_inc(s_evA, 1)
                a_ev = cev.inc()

                @blk.vector
                def _(v):
                    v.wait_ge(s_evA, a_ev)
                    if ctc >= 2:
                        v.wait_ge(s_ab[b], cab[b].v)
                    v.tensor_mul(tBb[b][:], tFa[b][:], tFa[b][:])
                    v.drain()
                    v.tensor_scalar(out=tBb[b][:], in0=tBb[b][:], scalar1=-1.0,
                                    scalar2=float(np.float32(1.0) + np.float32(1e-6)),
                                    op0=ALU.mult, op1=ALU.add)
                    v.drain().then_inc(s_dvA, 1)
                b2_dv = cdv.inc()

                @blk.scalar
                def _(sc):
                    sc.wait_ge(s_dvA, b2_dv)
                    sc.activation(out=tBb[b][:], in_=tBb[b][:], func=AF.Sqrt)
                    sc.drain().then_inc(s_evA, 1)
                beta_ev = cev.inc()

                @blk.vector
                def _(v):
                    v.wait_ge(s_evA, beta_ev)
                    v.wait_ge(s_evA, evI_done[ctc])
                    v.tensor_mul(tBb[b][:], tBb[b][:], tI[:])
                    v.tensor_mul(tBb[b][:], tBb[b][:], xcp[:, ctc * TLOC:(ctc + 1) * TLOC])
                    v.drain()
                    v.tensor_tensor_scan(out=tI[:], data0=tFa[b][:], data1=tBb[b][:],
                                         initial=carB[:, ctc:ctc + 1], op0=ALU.mult, op1=ALU.add)
                    v.drain()
                    v.tensor_copy(out=carB[:, ctc:ctc + 1], in_=tI[:, TLOC - 1:TLOC])
                    v.drain().then_inc(s_dvA, 1)
                ladder_dv[ctc] = cdv.inc()
                tIfree_dv[ctc] = ladder_dv[ctc]

                @blk.sync
                def _(s):
                    s.wait_ge(s_evA, a_ev)
                    s.wait_ge(s_dvA, ladder_dv[ctc])
                    s.dma_start(out=a_d[ctc * 128:(ctc + 1) * 128, :],
                                in_=tFa[b][:]).then_inc(s_ab[b], 16)
                    s.dma_start(out=b_d[ctc * 128:(ctc + 1) * 128, :],
                                in_=tBb[b][:]).then_inc(s_ab[b], 16)
                cab[b].inc(32)

            @blk.scalar
            def _(sc):
                sc.wait_ge(s_dvA, ladder_dv[15])
                sc.wait_ge(s_dvA, ladder_dv[14])
                sc.activation(out=carA[:], in_=carA[:], func=AF.Exp)
                sc.drain().then_inc(s_evA, 1)
            cev.inc()
            a3_dv = cdv.v
            a3_mm = cmm.v
            a3_ev = cev.v
            a3_ab0, a3_ab1 = cab[0].v, cab[1].v

        _scope_next(nc, "phAG1")
        @blk.sync
        def _(s):
            s.wait_ge(s_dvA, a3_dv)
            s.wait_ge(s_evA, a3_ev)
            s.dma_start(out=sum_in[:, 0:16], in_=carA[:]).then_inc(s_c, 16)
            s.dma_start(out=sum_in[:, 16:32], in_=carB[:]).then_inc(s_c, 16)
        CC.inc(32)
        SUMS_ST = CC.v

        @blk.gpsimd
        def _(g):
            g.wait_ge(s_c, SUMS_ST)
            g.collective_compute(
                "AllGather", ALU.bypass,
                replica_groups=[list(range(8))],
                ins=[sum_in.ap().opt()], outs=[sum_out.ap().opt()],
            ).then_inc(s_cc, 1)
        SUM_AG = CCC.inc()

        @blk.gpsimd
        def _(g):
            g.wait_ge(s_cc, SUM_AG)
            g.dma_start(out=sumsb[:].rearrange("p (r c) -> p r c", r=8),
                        in_=sum_out.rearrange("(r p) c -> p r c", p=128)).then_inc(s_c, 16)
        CC.inc(16)
        SUMS_LD = CC.v

        @blk.vector
        def _(v):
            v.wait_ge(s_c, SUMS_LD)
            v.memset(hin[:], 0.0)
            v.drain()
            for r in range(8):
                v.tensor_single_scalar(sp1[:], sumsb[:, r * 32:r * 32 + 16], -1.0, ALU.add)
                v.drain()
                v.tensor_single_scalar(sp1[:], sp1[:], mskp[:, r:r + 1], ALU.mult)
                v.drain()
                v.tensor_single_scalar(sp1[:], sp1[:], 1.0, ALU.add)
                v.drain()
                v.tensor_mul(hin[:], hin[:], sp1[:])
                v.drain()
                v.tensor_single_scalar(sp1[:], sumsb[:, r * 32 + 16:r * 32 + 32],
                                       mskp[:, r:r + 1], ALU.mult)
                v.drain()
                v.tensor_add(hin[:], hin[:], sp1[:])
                v.drain()
            v.engine_nop().then_inc(s_dvA, 1)
        HIN_DONE = cdv.inc()
        pA.close()

        _scope_next(nc, "phB")
        mh_ctx = ExitStack()
        mh = mh_ctx.enter_context(nc.sbuf_tensor("mhF", [128, KH * TLOC], BF16, side="right"))
        s_xB = [sem("s_xB0"), sem("s_xB1")]
        cxB = [Cnt(), Cnt()]
        with ExitStack() as pB:
            a_row = [pB.enter_context(nc.sbuf_tensor(f"a_row{i}", [128, TLOC], F32, side="right")) for i in range(2)]
            b_row = [pB.enter_context(nc.sbuf_tensor(f"b_row{i}", [128, TLOC], F32, side="right")) for i in range(2)]
            _g_row0 = pB.enter_context(nc.sbuf_tensor("g_row0", [128, TLOC], BF16, side="right"))
            g_row = [_g_row0, _g_row0]
            hF = pB.enter_context(nc.sbuf_tensor("hF", [128, TLOC], F32, side="right"))
            ggF = pB.enter_context(nc.sbuf_tensor("ggF", [128, TLOC], F32, side="right"))
            mh_done = {}
            for ctc in range(16):
                bi = ctc % 2

                @blk.sync
                def _(s):
                    if ctc < 2:
                        s.wait_ge(s_dvA, HIN_DONE)
                        s.wait_ge(s_ab[0], a3_ab0)
                        s.wait_ge(s_ab[1], a3_ab1)
                        s.wait_ge(s_evA, a3_ev)
                        s.wait_ge(s_mmA, a3_mm)
                    else:
                        s.wait_ge(s_dvA, mh_done[ctc - 2])
                    if ctc >= 1:
                        s.wait_ge(s_dvA, mh_done[ctc - 1])
                    s.dma_start(out=a_row[bi][:], in_=a_d[ctc * 128:(ctc + 1) * 128, :]).then_inc(s_xB[bi], 16)
                    s.dma_start(out=b_row[bi][:], in_=b_d[ctc * 128:(ctc + 1) * 128, :]).then_inc(s_xB[bi], 16)
                    s.dma_start(out=g_row[bi][:], in_=gate_d[ctc * 128:(ctc + 1) * 128, :]).then_inc(s_xB[bi], 16)
                ld = cxB[bi].inc(48)

                @blk.scalar
                def _(sc):
                    sc.wait_ge(s_xB[bi], ld)
                    if ctc >= 1:
                        sc.wait_ge(s_dvA, mh_done[ctc - 1])
                    sc.activation(out=ggF[:], in_=g_row[bi][:], func=AF.Gelu)
                    sc.drain().then_inc(s_evA, 1)
                gg_done = cev.inc()

                @blk.vector
                def _(v):
                    v.wait_ge(s_evA, gg_done)
                    v.tensor_tensor_scan(out=hF[:], data0=a_row[bi][:], data1=b_row[bi][:],
                                         initial=hin[:, ctc:ctc + 1], op0=ALU.mult, op1=ALU.add)
                    v.drain()
                    v.tensor_mul(mh[:, ctc * TLOC:(ctc + 1) * TLOC], hF[:], ggF[:])
                    v.drain().then_inc(s_dvA, 1)
                mh_done[ctc] = cdv.inc()
            b_dv = cdv.v
            b_ev = cev.v

        _scope_next(nc, "phC2")
        with ExitStack() as c2ctx:
            wslC = [c2ctx.enter_context(nc.sbuf_tensor(f"wslC{i}", [128, KH * 128], BF16, side="right")) for i in range(2)]
            pbC2 = [c2ctx.enter_context(nc.psum_tensor(f"pbC2_{i}", [128, 512], F32)) for i in range(2)]
            slab_mmC = {}
            pb_guard = {0: 0, 1: 0}
            for dt in range(KD):
                wbi = dt % 2
                slab_guard = slab_mmC.get(dt - 2, 0)

                @blk.gpsimd
                def _(g):
                    if dt >= 2:
                        g.wait_ge(s_mmA, slab_guard)
                    else:
                        g.wait_ge(s_dvA, b_dv)
                    g.dma_start(out=wslC[wbi][:].rearrange("p (k m) -> p k m", k=KH),
                                in_=_r_pk(woutT)[:, :, dt * 128:(dt + 1) * 128]).then_inc(s_wA[wbi], 16)
                w_ld = cw[wbi].inc(16)

                for ti, (c0, w) in enumerate(T_TILES):
                    par = (dt * 4 + ti) % 2
                    pbx = pbC2[par]
                    guard = pb_guard[par]

                    @blk.tensor
                    def _(t):
                        if ti == 0:
                            t.wait_ge(s_wA[wbi], w_ld)
                            if dt == 0:
                                t.wait_ge(s_dvA, b_dv)
                        if guard:
                            t.wait_ge(s_dvA, guard)
                        last = None
                        for k in range(KH):
                            last = t.matmul(pbx[:, 0:w], wslC[wbi][:, k * 128:(k + 1) * 128],
                                            mh[:, k * TLOC + c0:c0 + k * TLOC + w],
                                            start=(k == 0), stop=(k == KH - 1))
                        last.then_inc(s_mmA, 1)
                    mm_done = cmm.inc()

                    @blk.vector
                    def _(v):
                        v.wait_ge(s_mmA, mm_done)
                        v.tensor_add(xres[:, dt * THAL + 3 + c0:dt * THAL + 3 + c0 + w],
                                     xres[:, dt * THAL + 3 + c0:dt * THAL + 3 + c0 + w], pbx[:, 0:w])
                        v.drain().then_inc(s_dvA, 1)
                    pb_guard[par] = cdv.inc()
                slab_mmC[dt] = cmm.v
            c2_done = cdv.v
            c2_mm = cmm.v
        mh_ctx.close()

        _scope_next(nc, "phC3")
        ln2_ctx = ExitStack()
        ln2b = ln2_ctx.enter_context(nc.sbuf_tensor("ln2bF", [128, KD * TLOC], BF16, side="right"))
        with ExitStack() as c3ctx:
            _lf0 = c3ctx.enter_context(nc.sbuf_tensor("lfF0", [128, KD * 512], F32, side="right"))
            lf = [_lf0, _lf0]
            xb3 = c3ctx.enter_context(nc.sbuf_tensor("xb3", [128, KD * 512], BF16, side="right"))
            xq3 = c3ctx.enter_context(nc.sbuf_tensor("xq3", [128, KD * 512], BF16, side="right"))
            sTf = c3ctx.enter_context(nc.sbuf_tensor("sTfF", [32, 4 * 512], F32, side="right"))
            pbS = c3ctx.enter_context(nc.psum_tensor("pbSF", [128, 512], F32))
            pbQ = c3ctx.enter_context(nc.psum_tensor("pbQF", [128, 512], F32))
            pbM = c3ctx.enter_context(nc.psum_tensor("pbMF", [128, 512], F32))
            pbR = c3ctx.enter_context(nc.psum_tensor("pbRF", [128, 512], F32))
            pbL = c3ctx.enter_context(nc.psum_tensor("pbLF", [128, 512], F32))
            ln_step = {}
            prev_sT = 0
            for ti, (c0, w) in enumerate(T_TILES):
                bi = ti % 2

                @blk.vector
                def _(v):
                    if ti == 0:
                        v.wait_ge(s_dvA, c2_done)
                        v.wait_ge(s_mmA, c2_mm)
                    for k in range(KD):
                        v.tensor_copy(out=xb3[:, k * 512:k * 512 + w],
                                      in_=xres[:, k * THAL + 3 + c0:k * THAL + 3 + c0 + w])
                    v.drain().then_inc(s_dvA, 1)
                cast_done = cdv.inc()

                @blk.scalar
                def _(sc):
                    sc.wait_ge(s_dvA, cast_done)
                    sc.activation(out=xq3[:].rearrange("p (k t) -> p k t", k=KD)[:, :, 0:w],
                                  in_=xb3[:].rearrange("p (k t) -> p k t", k=KD)[:, :, 0:w], func=AF.Square)
                    sc.drain().then_inc(s_evA, 1)
                sq_done = cev.inc()

                @blk.tensor
                def _(t):
                    t.wait_ge(s_evA, sq_done)
                    if ti >= 1:
                        t.wait_ge(s_dvA, ln_step[ti - 1])
                    for k in range(KD):
                        t.matmul(pbS[0:1, 0:w], ones_bf[:], xb3[:, k * 512:k * 512 + w],
                                 start=(k == 0), stop=(k == KD - 1))
                    last = None
                    for k in range(KD):
                        last = t.matmul(pbQ[0:1, 0:w], ones_bf[:], xq3[:, k * 512:k * 512 + w],
                                        start=(k == 0), stop=(k == KD - 1))
                    last.then_inc(s_mmA, 1)
                stats_mm = cmm.inc()

                @blk.vector
                def _(v):
                    v.wait_ge(s_mmA, stats_mm)
                    v.tensor_single_scalar(m_row[:, 0:w], pbS[0:1, 0:w], 1.0 / D, ALU.mult)
                    v.tensor_single_scalar(q_row[:, 0:w], pbQ[0:1, 0:w], 1.0 / D, ALU.mult)
                    v.drain()
                    v.tensor_mul(v_row[:, 0:w], m_row[:, 0:w], m_row[:, 0:w])
                    v.drain()
                    v.tensor_sub(v_row[:, 0:w], q_row[:, 0:w], v_row[:, 0:w])
                    v.drain()
                    v.tensor_single_scalar(v_row[:, 0:w], v_row[:, 0:w], EPS, ALU.add)
                    v.drain().then_inc(s_dvA, 1)
                var_done = cdv.inc()

                @blk.scalar
                def _(sc):
                    sc.wait_ge(s_dvA, var_done)
                    sc.activation(out=s_row[:, 0:w], in_=v_row[:, 0:w], func=AF.Sqrt)
                    sc.drain().then_inc(s_evA, 1)
                std_done = cev.inc()

                @blk.vector
                def _(v):
                    v.wait_ge(s_evA, std_done)
                    v.reciprocal(r_row[:, 0:w], s_row[:, 0:w])
                    v.drain().then_inc(s_dvA, 1)
                rinv_done = cdv.inc()

                @blk.tensor
                def _(t):
                    t.wait_ge(s_dvA, rinv_done)
                    t.matmul(pbM[:, 0:w], ones_row[:], m_row[:, 0:w], start=True, stop=True)
                    last = t.matmul(pbR[:, 0:w], ones_row[:], r_row[:, 0:w], start=True, stop=True)
                    last.then_inc(s_mmA, 1)
                bc_mm = cmm.inc()

                @blk.vector
                def _(v):
                    v.wait_ge(s_mmA, bc_mm)
                    if ti >= 1:
                        v.wait_ge(s_evA, prev_sT)
                    for k in range(KD):
                        v.tensor_sub(lf[bi][:, k * 512:k * 512 + w],
                                     xres[:, k * THAL + 3 + c0:k * THAL + 3 + c0 + w], pbM[:, 0:w])
                        v.drain()
                        v.tensor_mul(lf[bi][:, k * 512:k * 512 + w],
                                     lf[bi][:, k * 512:k * 512 + w], pbR[:, 0:w])
                        v.drain()
                        v.tensor_copy(out=ln2b[:, k * TLOC + c0:k * TLOC + c0 + w],
                                      in_=lf[bi][:, k * 512:k * 512 + w])
                        v.drain()
                    v.drain().then_inc(s_dvA, 1)
                ln_f32 = cdv.inc()

                @blk.tensor
                def _(t):
                    t.wait_ge(s_dvA, ln_f32)
                    if prev_sT:
                        t.wait_ge(s_evA, prev_sT)
                    last = None
                    for k in range(KD):
                        last = t.matmul(pbL[0:32, 0:w], wg0[:, k * 32:(k + 1) * 32],
                                        lf[bi][:, k * 512:k * 512 + w],
                                        start=(k == 0), stop=(k == KD - 1))
                    last.then_inc(s_mmA, 1)
                sc_mm = cmm.inc()

                @blk.scalar
                def _(sc):
                    sc.wait_ge(s_mmA, sc_mm)
                    sc.copy(out=sTf[:, ti * 512:ti * 512 + w], in_=pbL[0:32, 0:w])
                    sc.drain().then_inc(s_evA, 1)
                ln_step[ti] = cdv.v
                prev_sT = cev.inc()
            c3_ev = cev.v
            c3_dv = cdv.v
            c3_mm = cmm.v

            IW_ctx = ExitStack()
            Ibuf = IW_ctx.enter_context(nc.sbuf_tensor("IbufF", [32, 4 * TLOC], BF16, side="left"))
            WIbuf = IW_ctx.enter_context(nc.sbuf_tensor("WIbufF", [32, 4 * TLOC], BF16, side="left"))
            with ExitStack() as c5ctx:
                lg16 = c5ctx.enter_context(nc.sbuf_tensor("lg16F", [128, 512], F32, side="right"))
                sr16 = c5ctx.enter_context(nc.sbuf_tensor("sr16F", [128, 16], F32, side="right"))
                top16 = c5ctx.enter_context(nc.sbuf_tensor("top16F", [128, 128], F32, side="right"))
                idxu16 = c5ctx.enter_context(nc.sbuf_tensor("idxu16F", [128, 128], U32, side="right"))
                idxf16 = c5ctx.enter_context(nc.sbuf_tensor("idxf16F", [128, 128], F32, side="right"))
                ohJf = c5ctx.enter_context(nc.sbuf_tensor("ohJfF", [128, 512], F32, side="right"))
                tmpJ = c5ctx.enter_context(nc.sbuf_tensor("tmpJF", [128, 512], F32, side="right"))
                ohb = c5ctx.enter_context(nc.sbuf_tensor("ohbF", [128, 512], BF16, side="right"))
                ohwb = c5ctx.enter_context(nc.sbuf_tensor("ohwbF", [128, 512], BF16, side="right"))
                wcol16 = c5ctx.enter_context(nc.sbuf_tensor("wcol16F", [128, 16], F32, side="right"))
                cnt_t = c5ctx.enter_context(nc.sbuf_tensor("cnt_tF", [32, 1], F32, side="right"))
                ptA16 = c5ctx.enter_context(nc.psum_tensor("ptA16", [32, 1024], BF16))
                ptB16 = c5ctx.enter_context(nc.psum_tensor("ptB16", [32, 1024], BF16))
                c3dv_pre = cdv.v

                @blk.tensor
                def _(t):
                    t.wait_ge(s_evA, c3_ev)
                    t.wait_ge(s_dvA, c3dv_pre)
                    t.wait_ge(s_cc, IOTA_DONE)
                    last = None
                    for tk in range(16):
                        ti5 = tk // 4
                        off = (tk % 4) * 128
                        last = t.transpose(pbM[:, tk * 32:(tk + 1) * 32],
                                           sTf[0:32, ti5 * 512 + off:ti5 * 512 + off + 128],
                                           ident[0:32, 0:32])
                    last.then_inc(s_mmA, 1)
                tr_mm = cmm.inc()

                @blk.scalar
                def _(sc):
                    sc.wait_ge(s_mmA, tr_mm)
                    sc.copy(out=lg16[:], in_=pbM[:, 0:512])
                    sc.drain().then_inc(s_evA, 1)
                lg_done = cev.inc()

                @blk.vector
                def _(v):
                    v.wait_ge(s_evA, lg_done)
                    for tk in range(16):
                        v.reduce_max(sr16[:, tk:tk + 1], lg16[:, tk * 32:(tk + 1) * 32],
                                     axis=mybir.AxisListType.X)
                    v.drain()
                    v.tensor_single_scalar(sr16[:], sr16[:], -1.0, ALU.mult)
                    v.drain().then_inc(s_dvA, 1)
                mx_done = cdv.inc()

                @blk.scalar
                def _(sc):
                    sc.wait_ge(s_dvA, mx_done)
                    for tk in range(16):
                        sc.activation(out=lg16[:, tk * 32:(tk + 1) * 32],
                                      in_=lg16[:, tk * 32:(tk + 1) * 32],
                                      func=AF.Exp, bias=sr16[:, tk:tk + 1])
                    sc.drain().then_inc(s_evA, 1)
                ex_done = cev.inc()

                @blk.vector
                def _(v):
                    v.wait_ge(s_evA, ex_done)
                    for tk in range(16):
                        v.reduce_sum(sr16[:, tk:tk + 1], lg16[:, tk * 32:(tk + 1) * 32],
                                     axis=mybir.AxisListType.X)
                    v.drain()
                    v.reciprocal(sr16[:], sr16[:])
                    v.drain()
                    for tk in range(16):
                        v.tensor_single_scalar(lg16[:, tk * 32:(tk + 1) * 32],
                                               lg16[:, tk * 32:(tk + 1) * 32],
                                               sr16[:, tk:tk + 1], ALU.mult)
                    v.drain()
                    for tk in range(16):
                        v.max(out=top16[:, tk * 8:(tk + 1) * 8], in_=lg16[:, tk * 32:(tk + 1) * 32])
                    v.drain()
                    for tk in range(16):
                        v.max_index(out=idxu16[:, tk * 8:(tk + 1) * 8],
                                    in_max=top16[:, tk * 8:(tk + 1) * 8],
                                    in_values=lg16[:, tk * 32:(tk + 1) * 32])
                    v.drain()
                    v.tensor_copy(out=idxf16[:], in_=idxu16[:])
                    v.drain().then_inc(s_dvA, 1)
                top_done = cdv.inc()

                tr2_mm = 0
                ev2 = 0
                for k4 in range(4):
                    @blk.vector
                    def _(v):
                        if k4 == 0:
                            v.wait_ge(s_dvA, top_done)
                        else:
                            v.wait_ge(s_mmA, tr2_mm)
                        for tk in range(16):
                            v.tensor_single_scalar(ohJf[:, tk * 32:(tk + 1) * 32], it32[:],
                                                   idxf16[:, tk * 8 + k4:tk * 8 + k4 + 1],
                                                   ALU.is_equal)
                        v.drain()
                        v.tensor_mul(tmpJ[:], lg16[:], ohJf[:])
                        v.drain()
                        for tk in range(16):
                            v.reduce_sum(wcol16[:, tk:tk + 1], tmpJ[:, tk * 32:(tk + 1) * 32],
                                         axis=mybir.AxisListType.X)
                        v.drain()
                        for tk in range(16):
                            v.tensor_single_scalar(ohwb[:, tk * 32:(tk + 1) * 32],
                                                   ohJf[:, tk * 32:(tk + 1) * 32],
                                                   wcol16[:, tk:tk + 1], ALU.mult)
                        v.tensor_copy(out=ohb[:], in_=ohJf[:])
                        v.drain().then_inc(s_dvA, 1)
                    oh_done = cdv.inc()

                    for jb in range(2):
                        tks = range(jb * 8, jb * 8 + 8)
                        my_ev2 = ev2

                        @blk.tensor
                        def _(t):
                            if jb == 0:
                                t.wait_ge(s_dvA, oh_done)
                            if my_ev2:
                                t.wait_ge(s_evA, my_ev2)
                            last = None
                            for j, tk in enumerate(tks):
                                last = t.transpose(ptA16[:, j * 128:(j + 1) * 128],
                                                   ohb[:, tk * 32:(tk + 1) * 32], ident_bf[:])
                                last = t.transpose(ptB16[:, j * 128:(j + 1) * 128],
                                                   ohwb[:, tk * 32:(tk + 1) * 32], ident_bf[:])
                            last.then_inc(s_mmA, 1)
                        tr2_mm = cmm.inc()

                        @blk.scalar
                        def _(sc):
                            sc.wait_ge(s_mmA, tr2_mm)
                            for j, tk in enumerate(tks):
                                sc.copy(out=Ibuf[:, tk * 512 + k4:tk * 512 + k4 + 509:4],
                                        in_=ptA16[:, j * 128:(j + 1) * 128])
                                sc.copy(out=WIbuf[:, tk * 512 + k4:tk * 512 + k4 + 509:4],
                                        in_=ptB16[:, j * 128:(j + 1) * 128])
                            sc.drain().then_inc(s_evA, 1)
                        ev2 = cev.inc()

                @blk.vector
                def _(v):
                    v.wait_ge(s_evA, cev.v)
                    v.reduce_sum(cnt_t[:], Ibuf[:], axis=mybir.AxisListType.X)
                    v.drain().then_inc(s_dvA, 1)
                cnt_done = cdv.inc()

                @blk.sync
                def _(s):
                    s.wait_ge(s_dvA, cnt_done)
                    s.dma_start(out=cnt_in.rearrange("a b -> b a"), in_=cnt_t[:]).then_inc(s_c, 16)
                CC.inc(16)
                CNT_ST = CC.v

                @blk.gpsimd
                def _(g):
                    g.wait_ge(s_c, CNT_ST)
                    g.collective_compute(
                        "AllGather", ALU.bypass,
                        replica_groups=[list(range(8))],
                        ins=[cnt_in.ap().opt()], outs=[cnt_out.ap().opt()],
                    ).then_inc(s_cc, 1)
                CNT_AG = CCC.inc()

        _scope_next(nc, "phD")
        ysb_ctx = ExitStack()
        ysb = ysb_ctx.enter_context(nc.sbuf_tensor("ysbF", [128, KD * TLOC], BF16, side="right"))
        s_wD = [sem("s_wD0"), sem("s_wD1")]
        cwD = [Cnt(), Cnt()]

        with ExitStack() as d1ctx:
            cnts8 = d1ctx.enter_context(nc.sbuf_tensor("cnts8", [8, 32], F32, side="right"))
            prow = d1ctx.enter_context(nc.sbuf_tensor("prowF", [1, 32], F32, side="right"))
            IfD = d1ctx.enter_context(nc.sbuf_tensor("IfD", [32, 512], F32, side="right"))
            zfD = d1ctx.enter_context(nc.sbuf_tensor("zfD", [32, 512], F32, side="right"))
            rkD = d1ctx.enter_context(nc.sbuf_tensor("rkD", [32, 512], F32, side="right"))
            tmpw = d1ctx.enter_context(nc.sbuf_tensor("tmpwF", [32, 128], F32, side="right"))
            carry = d1ctx.enter_context(nc.sbuf_tensor("carryF", [32, 1], F32, side="right"))
            cwf = d1ctx.enter_context(nc.sbuf_tensor("cwfF", [32, TLOC], F32, side="right"))
            pbD1 = d1ctx.enter_context(nc.psum_tensor("pbD1", [128, 512], F32))

            @blk.sync
            def _(s):
                s.wait_ge(s_cc, CNT_AG)
                s.dma_start(out=cnts8[:], in_=cnt_out[:, :]).then_inc(s_c, 16)
            CC.inc(16)
            CNT8_LD = CC.v

            @blk.tensor
            def _(t):
                t.wait_ge(s_c, CNT8_LD)
                t.matmul(pbD1[0:1, 0:32], maskb[:], cnts8[:], start=True, stop=True).then_inc(s_mmA, 1)
            pref_mm = cmm.inc()

            @blk.scalar
            def _(sc):
                sc.wait_ge(s_mmA, pref_mm)
                sc.copy(out=prow[:], in_=pbD1[0:1, 0:32])
                sc.drain().then_inc(s_evA, 1)
            prow_done = cev.inc()

            @blk.sync
            def _(s):
                s.wait_ge(s_evA, prow_done)
                s.dma_start(out=pref_d[:, :], in_=prow[:]).then_inc(s_c, 16)
                s.wait_ge(s_c, CNT8_LD + 16)
                s.dma_start(out=prefc[:], in_=pref_d.rearrange("a b -> b a")).then_inc(s_c, 16)
            CC.inc(32)
            PREF_LD = CC.v

            @blk.vector
            def _(v):
                v.memset(zfD[:], 0.0)
                v.wait_ge(s_c, PREF_LD)
                v.tensor_copy(out=carry[:], in_=prefc[:])
                v.drain()
                for q in range(16):
                    p0 = q * 512
                    v.tensor_copy(out=IfD[:], in_=Ibuf[:, p0:p0 + 512])
                    v.drain()
                    v.tensor_tensor_scan(out=rkD[:], data0=zfD[:], data1=IfD[:], initial=carry[:],
                                         op0=ALU.add, op1=ALU.add)
                    v.drain()
                    v.tensor_copy(out=carry[:], in_=rkD[:, 511:512])
                    v.drain()
                    v.tensor_single_scalar(rkD[:], rkD[:], 8192.0, ALU.is_le)
                    v.drain()
                    for k4 in range(4):
                        if k4 == 0:
                            v.tensor_mul(cwf[:, q * 128:(q + 1) * 128],
                                         WIbuf[:, p0 + k4:p0 + 509 + k4:4], rkD[:, k4:509 + k4:4])
                        else:
                            v.tensor_mul(tmpw[:], WIbuf[:, p0 + k4:p0 + 509 + k4:4], rkD[:, k4:509 + k4:4])
                            v.drain()
                            v.tensor_add(cwf[:, q * 128:(q + 1) * 128],
                                         cwf[:, q * 128:(q + 1) * 128], tmpw[:])
                        v.drain()
                v.engine_nop().then_inc(s_dvA, 1)
            cw_pre = cdv.inc()

            @blk.sync
            def _(s):
                s.wait_ge(s_dvA, cw_pre)
                s.dma_start(out=cw_d[:, :], in_=cwf[:]).then_inc(s_c, 16)
            CC.inc(16)
            CW_ST = CC.v

            @blk.vector
            def _(v):
                v.wait_ge(s_c, CW_ST)
                v.engine_nop().then_inc(s_dvA, 1)
            cw_done = cdv.inc()
        IW_ctx.close()

        _scope_next(nc, "phD2")
        with ExitStack() as d2ctx:
            ws1 = d2ctx.enter_context(nc.sbuf_tensor("ws1F", [128, KD * D], BF16, side="right"))
            ws3 = d2ctx.enter_context(nc.sbuf_tensor("ws3F", [128, KD * D], BF16, side="right"))
            ws2 = d2ctx.enter_context(nc.sbuf_tensor("ws2F", [128, KD * D], BF16, side="right"))
            h3b = d2ctx.enter_context(nc.sbuf_tensor("h3bF", [128, KD * 256], BF16, side="right"))
            asb = d2ctx.enter_context(nc.sbuf_tensor("asbF", [128, KD * 256], BF16, side="right"))
            pbA_ = d2ctx.enter_context(nc.psum_tensor("pbDA", [128, 512], F32))
            pbB_ = d2ctx.enter_context(nc.psum_tensor("pbDB", [128, 512], F32))
            pbC_ = d2ctx.enter_context(nc.psum_tensor("pbDC", [128, 512], F32))
            pbD_ = d2ctx.enter_context(nc.psum_tensor("pbDD", [128, 512], F32))
            pbY = [d2ctx.enter_context(nc.psum_tensor(f"pbDY{i}", [128, 512], F32)) for i in range(4)]
            DT8_ = [(i * 256, 256) for i in range(8)]

            @blk.gpsimd
            def _(g):
                g.wait_ge(s_dvA, cw_done)
                g.dma_start(out=ws1[:].rearrange("p (k m) -> p k m", k=KD),
                            in_=_r_pk(Ws1_i)).then_inc(s_wD[0], 16)
                g.dma_start(out=ws3[:].rearrange("p (k m) -> p k m", k=KD),
                            in_=_r_pk(Ws3_i)).then_inc(s_wD[0], 16)
                g.dma_start(out=ws2[:].rearrange("p (k m) -> p k m", k=KD),
                            in_=_r_pk(Ws2_i)).then_inc(s_wD[0], 16)
            ws_ld = cwD[0].inc(48)

            PB4 = [pbA_, pbB_, pbC_, pbD_]
            pb_guard = {0: None, 1: None, 2: None, 3: None}
            prev_ymm = 0
            prev_yev = None
            for ti, (c0, w) in enumerate(DT8_):
                for phase in range(2):
                    for m in range(KD):
                        par = m % 4
                        pbx = PB4[par]
                        guard = pb_guard[par]

                        @blk.tensor
                        def _(t):
                            if ti == 0 and phase == 0 and m == 0:
                                t.wait_ge(s_wD[0], ws_ld)
                                t.wait_ge(s_dvA, cw_done)
                            if phase == 0 and m == 0 and ti > 0:
                                t.wait_ge(s_mmA, prev_ymm)
                            if guard:
                                t.wait_ge(s_dvA if guard[0] == "dv" else s_evA, guard[1])
                            wsl = ws3 if phase == 0 else ws1
                            last = None
                            for k in range(KD):
                                last = t.matmul(pbx[:, 0:w], wsl[:, k * D + m * 128:k * D + (m + 1) * 128],
                                                ln2b[:, k * TLOC + c0:k * TLOC + c0 + w],
                                                start=(k == 0), stop=(k == KD - 1))
                            last.then_inc(s_mmA, 1)
                        mm_done = cmm.inc()

                        if phase == 0:
                            @blk.scalar
                            def _(sc):
                                sc.wait_ge(s_mmA, mm_done)
                                sc.copy(out=h3b[:, m * 256:m * 256 + w], in_=pbx[:, 0:w])
                                if m % 2 == 1:
                                    sc.drain().then_inc(s_evA, 2)
                            if m % 2 == 1:
                                ev = cev.inc(2)
                                pb_guard[(m - 1) % 4] = ("ev", ev)
                                pb_guard[m % 4] = ("ev", ev)
                        else:
                            @blk.scalar
                            def _(sc):
                                sc.wait_ge(s_mmA, mm_done)
                                sc.activation(out=asb[:, m * 256:m * 256 + w], in_=pbx[:, 0:w],
                                              func=AF.Silu)
                                if m % 2 == 1:
                                    sc.drain().then_inc(s_evA, 2)
                            if m % 2 == 1:
                                ev = cev.inc(2)
                                pb_guard[(m - 1) % 4] = ("ev", ev)
                                pb_guard[m % 4] = ("ev", ev)
                last_he_ev = cev.v

                @blk.vector
                def _(v):
                    v.wait_ge(s_evA, last_he_ev)
                    v.tensor_mul(asb[:], asb[:], h3b[:])
                    v.drain().then_inc(s_dvA, 1)
                as_done = cdv.inc()

                @blk.tensor
                def _(t):
                    t.wait_ge(s_dvA, as_done)
                    if prev_yev:
                        t.wait_ge(s_dvA, prev_yev[1])
                    last = None
                    for mo in range(KD):
                        pby = pbY[mo // 2]
                        of = (mo % 2) * 256
                        for k in range(KD):
                            last = t.matmul(pby[:, of:of + w], ws2[:, k * D + mo * 128:k * D + (mo + 1) * 128],
                                            asb[:, k * 256:k * 256 + w],
                                            start=(k == 0), stop=(k == KD - 1))
                    last.then_inc(s_mmA, 1)
                prev_ymm = cmm.inc()

                @blk.vector
                def _(v):
                    v.wait_ge(s_mmA, prev_ymm)
                    for mo in range(KD):
                        pby = pbY[mo // 2]
                        of = (mo % 2) * 256
                        v.tensor_copy(out=ysb[:, mo * TLOC + c0:mo * TLOC + c0 + w], in_=pby[:, of:of + w])
                    v.drain().then_inc(s_dvA, 1)
                prev_yev = ("dv", cdv.inc())
            d2_ev = cev.v
            d2_mm = cmm.v
            d2_dv = cdv.v

        _scope_next(nc, "phD3")
        with ExitStack() as d3ctx:
            e1sl = [d3ctx.enter_context(nc.sbuf_tensor(f"e1slF{i}", [128, KD * 512], BF16, side="right")) for i in range(2)]
            e3sl = [d3ctx.enter_context(nc.sbuf_tensor(f"e3slF{i}", [128, KD * 512], BF16, side="right")) for i in range(2)]
            e2sl = d3ctx.enter_context(nc.sbuf_tensor("e2slF", [128, 4 * D], BF16, side="right"))
            aeb = d3ctx.enter_context(nc.sbuf_tensor("aebF", [128, 16 * 512], BF16, side="right"))
            s_wE = sem("s_wE"); cwE = Cnt()
            cweD = d3ctx.enter_context(nc.sbuf_tensor("cweD", [128, 4 * 512], BF16, side="right"))
            cwrow = d3ctx.enter_context(nc.sbuf_tensor("cwrowF0", [1, TLOC], BF16, side="right"))
            s_cwr = sem("s_cwr"); ccwr = Cnt()
            grpA = [d3ctx.enter_context(nc.psum_tensor(f"pbGA{i}", [128, 512], F32)) for i in range(4)]
            grpB = [d3ctx.enter_context(nc.psum_tensor(f"pbGB{i}", [128, 512], F32)) for i in range(4)]
            GRP = {0: grpA, 1: grpB}
            SEMMAP = {"dv": s_dvA, "ev": s_evA, "mm": s_mmA}
            grp_guard = {0: [("dv", d2_dv), ("ev", d2_ev)], 1: [("dv", d2_dv)]}

            def twait(t, gl):
                for kind, val in gl:
                    if val:
                        t.wait_ge(SEMMAP[kind], val)

            slab_mm = {}
            @blk.gpsimd
            def _(g):
                g.dma_start(out=cwrow[:], in_=cw_d[0:1, :]).then_inc(s_cwr, 16)
            cwr_ld = ccwr.inc(16)

            g0 = grp_guard[0]

            @blk.tensor
            def _(t):
                t.wait_ge(s_mmA, d2_mm)
                twait(t, g0)
                t.wait_ge(s_cwr, cwr_ld)
                last = None
                for ti in range(4):
                    last = t.matmul(grpA[ti][:, :], ones_row_bf[:], cwrow[0:1, ti * 512:(ti + 1) * 512],
                                    start=True, stop=True)
                last.then_inc(s_mmA, 1)
            cwb_mm = cmm.inc()

            @blk.scalar
            def _(sc):
                sc.wait_ge(s_mmA, cwb_mm)
                for ti in range(4):
                    sc.copy(out=cweD[:, ti * 512:(ti + 1) * 512], in_=grpA[ti][:, :])
                sc.drain().then_inc(s_evA, 1)
            cwe_ev = cev.inc()
            grp_guard[0] = [("ev", cwe_ev)]

            for e in range(32):
                wbi = e % 2

                @blk.gpsimd
                def _(g):
                    if e >= 2:
                        g.wait_ge(s_mmA, slab_mm[e - 2])
                    elif e == 0:
                        g.wait_ge(s_evA, d2_ev)
                        g.wait_ge(s_mmA, d2_mm)
                        g.wait_ge(s_dvA, d2_dv)
                    g.dma_start(out=e1sl[wbi][:].rearrange("p (k m) -> p k m", k=KD),
                                in_=_r_pk(W1_i[e])).then_inc(s_wD[wbi], 16)
                    g.dma_start(out=e3sl[wbi][:].rearrange("p (k m) -> p k m", k=KD),
                                in_=_r_pk(W3_i[e])).then_inc(s_wD[wbi], 16)
                w_ld = cwD[wbi].inc(32)

                @blk.gpsimd
                def _(g):
                    if e >= 1:
                        g.wait_ge(s_mmA, slab_mm[e - 1])
                    g.dma_start(out=e2sl[:].rearrange("p (k m) -> p k m", k=4),
                                in_=_r_pk(W2_i[e])).then_inc(s_wE, 16)
                e2_ld = cwE.inc(16)

                if e < 31:
                    my_cwb = cwb_mm

                    @blk.gpsimd
                    def _(g):
                        g.wait_ge(s_mmA, my_cwb)
                        g.dma_start(out=cwrow[:], in_=cw_d[e + 1:e + 2, :]).then_inc(s_cwr, 16)
                    cwr_ld = ccwr.inc(16)

                PAR = (1, 0, 1, 0)
                w1mm = {}
                sil_ev = {}
                for m in range(4):
                    g = PAR[m]; gl = grp_guard[g]; grp = GRP[g]

                    @blk.tensor
                    def _(t):
                        if m == 0:
                            t.wait_ge(s_wD[wbi], w_ld)
                        twait(t, gl)
                        last = None
                        for k in range(KD):
                            for ti in range(4):
                                last = t.matmul(grp[ti][:, :],
                                                e1sl[wbi][:, k * 512 + m * 128:k * 512 + (m + 1) * 128],
                                                ln2b[:, k * TLOC + ti * 512:k * TLOC + (ti + 1) * 512],
                                                start=(k == 0), stop=(k == KD - 1))
                        last.then_inc(s_mmA, 1)
                    w1mm[m] = cmm.inc()

                    @blk.scalar
                    def _(sc):
                        sc.wait_ge(s_mmA, w1mm[m])
                        for ti in range(4):
                            sc.activation(out=aeb[:, (m * 4 + ti) * 512:(m * 4 + ti + 1) * 512],
                                          in_=grp[ti][:, :], func=AF.Silu)
                        sc.drain().then_inc(s_evA, 1)
                    sil_ev[m] = cev.inc()
                    grp_guard[g] = [("ev", sil_ev[m])]

                w3mm = {}
                ae_dv = {}
                for m in range(4):
                    g = PAR[m]; gl = grp_guard[g]; grp = GRP[g]

                    @blk.tensor
                    def _(t):
                        twait(t, gl)
                        last = None
                        for k in range(KD):
                            for ti in range(4):
                                last = t.matmul(grp[ti][:, :],
                                                e3sl[wbi][:, k * 512 + m * 128:k * 512 + (m + 1) * 128],
                                                ln2b[:, k * TLOC + ti * 512:k * TLOC + (ti + 1) * 512],
                                                start=(k == 0), stop=(k == KD - 1))
                        last.then_inc(s_mmA, 1)
                    w3mm[m] = cmm.inc()

                    @blk.vector
                    def _(v):
                        v.wait_ge(s_mmA, w3mm[m])
                        v.wait_ge(s_evA, sil_ev[m])
                        if m == 0:
                            v.wait_ge(s_evA, cwe_ev)
                        for ti in range(4):
                            c = (m * 4 + ti) * 512
                            v.tensor_mul(aeb[:, c:c + 512], aeb[:, c:c + 512], grp[ti][:, :])
                            v.tensor_mul(aeb[:, c:c + 512], aeb[:, c:c + 512],
                                         cweD[:, ti * 512:(ti + 1) * 512])
                        v.drain().then_inc(s_dvA, 1)
                    ae_dv[m] = cdv.inc()
                    grp_guard[g] = [("dv", ae_dv[m])]

                if e < 31:
                    gl2 = grp_guard[0]

                    @blk.tensor
                    def _(t):
                        twait(t, gl2)
                        t.wait_ge(s_cwr, cwr_ld)
                        last = None
                        for ti in range(4):
                            last = t.matmul(grpA[ti][:, :], ones_row_bf[:],
                                            cwrow[0:1, ti * 512:(ti + 1) * 512],
                                            start=True, stop=True)
                        last.then_inc(s_mmA, 1)
                    cwb_mm = cmm.inc()
                    my_ae3 = ae_dv[3]

                    @blk.scalar
                    def _(sc):
                        sc.wait_ge(s_mmA, cwb_mm)
                        sc.wait_ge(s_dvA, my_ae3)
                        for ti in range(4):
                            sc.copy(out=cweD[:, ti * 512:(ti + 1) * 512], in_=grpA[ti][:, :])
                        sc.drain().then_inc(s_evA, 1)
                    cwe_ev = cev.inc()
                    grp_guard[0] = [("ev", cwe_ev)]

                w2mm = {}
                ydv = {}
                for mo in range(KD):
                    g = (1, 0)[mo % 2]; gl = grp_guard[g]; grp = GRP[g]

                    if mo == 0:
                        @blk.tensor
                        def _(t):
                            t.wait_ge(s_dvA, ae_dv[2])
                            t.wait_ge(s_wE, e2_ld)
                            twait(t, gl)
                            last = None
                            for k in range(3):
                                for ti in range(4):
                                    last = t.matmul(grp[ti][:, :],
                                                    e2sl[:, k * D + mo * 128:k * D + (mo + 1) * 128],
                                                    aeb[:, (k * 4 + ti) * 512:(k * 4 + ti + 1) * 512],
                                                    start=(k == 0), stop=False)
                            last.then_inc(s_mmA, 1)
                        cmm.inc()

                        @blk.tensor
                        def _(t):
                            t.wait_ge(s_dvA, ae_dv[3])
                            last = None
                            for ti in range(4):
                                last = t.matmul(grp[ti][:, :],
                                                e2sl[:, 3 * D + mo * 128:3 * D + (mo + 1) * 128],
                                                aeb[:, (3 * 4 + ti) * 512:(3 * 4 + ti + 1) * 512],
                                                start=False, stop=True)
                            last.then_inc(s_mmA, 1)
                        w2mm[mo] = cmm.inc()
                    else:
                        @blk.tensor
                        def _(t):
                            twait(t, gl)
                            last = None
                            for k in range(4):
                                for ti in range(4):
                                    last = t.matmul(grp[ti][:, :],
                                                    e2sl[:, k * D + mo * 128:k * D + (mo + 1) * 128],
                                                    aeb[:, (k * 4 + ti) * 512:(k * 4 + ti + 1) * 512],
                                                    start=(k == 0), stop=(k == 3))
                            last.then_inc(s_mmA, 1)
                        w2mm[mo] = cmm.inc()

                    @blk.vector
                    def _(v):
                        v.wait_ge(s_mmA, w2mm[mo])
                        for ti in range(4):
                            v.tensor_add(ysb[:, mo * TLOC + ti * 512:mo * TLOC + (ti + 1) * 512],
                                         ysb[:, mo * TLOC + ti * 512:mo * TLOC + (ti + 1) * 512],
                                         grp[ti][:, :])
                        v.drain().then_inc(s_dvA, 1)
                    ydv[mo] = cdv.inc()
                    grp_guard[g] = [("dv", ydv[mo])]
                slab_mm[e] = cmm.v
            d3_dv = cdv.v
            d3_mm = cmm.v
            d3_ev = cev.v

        _scope_next(nc, "phD4")
        with ExitStack() as d4ctx:
            yb4 = d4ctx.enter_context(nc.sbuf_tensor("yb4", [128, KD * 512], F32, side="right"))
            lnb4 = d4ctx.enter_context(nc.sbuf_tensor("lnb4", [128, KD * 512], BF16, side="right"))
            lnq4 = d4ctx.enter_context(nc.sbuf_tensor("lnq4", [128, KD * 512], BF16, side="right"))
            lnf4 = d4ctx.enter_context(nc.sbuf_tensor("lnf4", [128, KD * 512], F32, side="right"))
            pbS4 = d4ctx.enter_context(nc.psum_tensor("pbS4", [128, 512], F32))
            pbQ4 = d4ctx.enter_context(nc.psum_tensor("pbQ4", [128, 512], F32))
            pbM4 = d4ctx.enter_context(nc.psum_tensor("pbM4", [128, 512], F32))
            pbR4 = d4ctx.enter_context(nc.psum_tensor("pbR4", [128, 512], F32))
            s_sl = sem("s_slF"); csl = Cnt()
            ln_prev4 = {}
            for ti, (c0, w) in enumerate(T_TILES):
                @blk.vector
                def _(v):
                    if ti == 0:
                        v.wait_ge(s_dvA, d3_dv)
                        v.wait_ge(s_evA, d3_ev)
                        v.wait_ge(s_mmA, d3_mm)
                    for k in range(KD):
                        v.tensor_copy(out=yb4[:, k * 512:k * 512 + w],
                                      in_=ysb[:, k * TLOC + c0:k * TLOC + c0 + w])
                        v.drain()
                        v.tensor_add(xres[:, k * THAL + 3 + c0:k * THAL + 3 + c0 + w],
                                     xres[:, k * THAL + 3 + c0:k * THAL + 3 + c0 + w],
                                     yb4[:, k * 512:k * 512 + w])
                        v.drain()
                        v.tensor_copy(out=lnb4[:, k * 512:k * 512 + w],
                                      in_=xres[:, k * THAL + 3 + c0:k * THAL + 3 + c0 + w])
                        v.drain()
                    v.engine_nop().then_inc(s_dvA, 1)
                x2_done = cdv.inc()

                @blk.scalar
                def _(sc):
                    sc.wait_ge(s_dvA, x2_done)
                    if csl.v:
                        sc.wait_ge(s_sl, csl.v)
                    sc.activation(out=lnq4[:].rearrange("p (k t) -> p k t", k=KD)[:, :, 0:w],
                                  in_=lnb4[:].rearrange("p (k t) -> p k t", k=KD)[:, :, 0:w], func=AF.Square)
                    sc.drain().then_inc(s_evA, 1)
                sq_done = cev.inc()

                @blk.tensor
                def _(t):
                    t.wait_ge(s_evA, sq_done)
                    if ti >= 1:
                        t.wait_ge(s_dvA, ln_prev4[ti - 1])
                    for k in range(KD):
                        t.matmul(pbS4[0:1, 0:w], ones_bf[:], lnb4[:, k * 512:k * 512 + w],
                                 start=(k == 0), stop=(k == KD - 1))
                    last = None
                    for k in range(KD):
                        last = t.matmul(pbQ4[0:1, 0:w], ones_bf[:], lnq4[:, k * 512:k * 512 + w],
                                        start=(k == 0), stop=(k == KD - 1))
                    last.then_inc(s_mmA, 1)
                stats_mm = cmm.inc()

                @blk.vector
                def _(v):
                    v.wait_ge(s_mmA, stats_mm)
                    v.tensor_single_scalar(m_row[:, 0:w], pbS4[0:1, 0:w], 1.0 / D, ALU.mult)
                    v.tensor_single_scalar(q_row[:, 0:w], pbQ4[0:1, 0:w], 1.0 / D, ALU.mult)
                    v.drain()
                    v.tensor_mul(v_row[:, 0:w], m_row[:, 0:w], m_row[:, 0:w])
                    v.drain()
                    v.tensor_sub(v_row[:, 0:w], q_row[:, 0:w], v_row[:, 0:w])
                    v.drain()
                    v.tensor_single_scalar(v_row[:, 0:w], v_row[:, 0:w], EPS, ALU.add)
                    v.drain().then_inc(s_dvA, 1)
                var_done = cdv.inc()

                @blk.scalar
                def _(sc):
                    sc.wait_ge(s_dvA, var_done)
                    sc.activation(out=s_row[:, 0:w], in_=v_row[:, 0:w], func=AF.Sqrt)
                    sc.drain().then_inc(s_evA, 1)
                std_done = cev.inc()

                @blk.vector
                def _(v):
                    v.wait_ge(s_evA, std_done)
                    v.reciprocal(r_row[:, 0:w], s_row[:, 0:w])
                    v.drain().then_inc(s_dvA, 1)
                rinv_done = cdv.inc()

                @blk.tensor
                def _(t):
                    t.wait_ge(s_dvA, rinv_done)
                    t.matmul(pbM4[:, 0:w], ones_row[:], m_row[:, 0:w], start=True, stop=True)
                    last = t.matmul(pbR4[:, 0:w], ones_row[:], r_row[:, 0:w], start=True, stop=True)
                    last.then_inc(s_mmA, 1)
                bc_mm = cmm.inc()

                @blk.vector
                def _(v):
                    v.wait_ge(s_mmA, bc_mm)
                    for k in range(KD):
                        v.tensor_sub(lnf4[:, k * 512:k * 512 + w],
                                     xres[:, k * THAL + 3 + c0:k * THAL + 3 + c0 + w], pbM4[:, 0:w])
                        v.drain()
                        v.tensor_mul(lnq4[:, k * 512:k * 512 + w],
                                     lnf4[:, k * 512:k * 512 + w], pbR4[:, 0:w])
                        v.drain()
                    v.engine_nop().then_inc(s_dvA, 1)
                ln_done = cdv.inc()

                @blk.sync
                def _(s):
                    s.wait_ge(s_dvA, ln_done)
                    s.dma_start(out=ln3_in.ap()[:, c0:c0 + w].rearrange("(k p) t -> p k t", p=128),
                                in_=lnq4[:].rearrange("p (k t) -> p k t", k=KD)[:, :, 0:w]).then_inc(s_sl, 16)
                csl.inc(16)
                ln_prev4[ti] = cdv.v
            d4_dv = cdv.v
            d4_mm = cmm.v
            d4_ev = cev.v
            d4_sl = csl.v
        ysb_ctx.close()
        ln2_ctx.close()

        @blk.gpsimd
        def _(g):
            g.wait_ge(s_sl, d4_sl)
            g.collective_compute(
                "AllGather", ALU.bypass,
                replica_groups=[list(range(8))],
                ins=[ln3_in.ap().opt()], outs=[ln3_out.ap().opt()],
            ).then_inc(s_cc, 1)
        LN3_AG = CCC.inc()

        _scope_next(nc, "phE")
        lns_ctx = ExitStack()
        ln3s = lns_ctx.enter_context(nc.sbuf_tensor("ln3sF", [128, KD * TLOC], BF16, side="right"))
        r_off = ctx.enter_context(nc.gpsimd.register("r_off"))

        @blk.sync
        def _(s):
            s.dma_start(out=eoff[:], in_=eoff_i).then_inc(s_c, 16)
            s.dma_start(out=goff[:], in_=goff_i).then_inc(s_c, 16)
        CC.inc(32)
        OFFS_LD = CC.v

        @blk.gpsimd
        def _(g):
            g.wait_ge(s_cc, LN3_AG)
            g.wait_ge(s_c, OFFS_LD)
            g.wait_ge(s_dvA, d4_dv)
            g.wait_ge(s_mmA, d4_mm)
            g.wait_ge(s_evA, d4_ev)
            for a in range(4):
                g.reg_load(r_off, eoff[0:1, a:a + 1])
                g.dma_start(
                    out=lnrb_d[a * 1024:(a + 1) * 1024, :],
                    in_=bass.AP(ln3_out, r_off, [[TLOC, 1024], [1, 1664]])
                    ).then_inc(s_c, 16)
            g.wait_ge(s_c, CC.v + 64)
            for i in range(16):
                a, r = i // 4, i % 4
                g.dma_start(
                    out=ln3s[:].rearrange("p (k t) -> p k t", k=KD)[:, :, i * 128:(i + 1) * 128],
                    in_=lnrb_d.ap()[a * 1024:(a + 1) * 1024, 512 * r:512 * r + 128]
                        .rearrange("(k p) t -> p k t", p=128)).then_inc(s_c, 16)
            g.wait_ge(s_c, CC.v + 64 + 256)
            g.dma_start(out=_r_pk(ln3s_d), in_=ln3s[:].rearrange("p (k t) -> p k t", k=KD)
                        ).then_inc(s_c, 16)
        CC.inc(64 + 256 + 16)
        LN3S_LD = CC.v

        with ExitStack() as pE:
            wslE = [pE.enter_context(nc.sbuf_tensor(f"wslE{i}", [128, KD * 128], BF16, side="right")) for i in range(4)]
            stgE = [pE.enter_context(nc.sbuf_tensor(f"stgE{i}", [128, TLOC], BF16, side="right")) for i in range(2)]
            pbE = [pE.enter_context(nc.psum_tensor(f"pbE{i}", [128, 512], F32)) for i in range(8)]
            s_wE = [sem(f"s_wE{i}") for i in range(4)]
            cwE = [Cnt() for _ in range(4)]
            s_gE = [sem(f"s_gE{i}") for i in range(2)]
            cgE = [Cnt() for _ in range(2)]

            slab_mmE = {}
            dv_guardE = {}
            for mt in range(32):
                for ui, (w_in, out_d) in enumerate([(w0_i, uT_d), (w2a_i, vT_d)]):
                    it = mt * 2 + ui
                    sl = it % 4
                    par = it % 2

                    @blk.gpsimd
                    def _(g):
                        if it >= 4:
                            g.wait_ge(s_mmA, slab_mmE[it - 4])
                        g.dma_start(out=wslE[sl][:].rearrange("p (k m) -> p k m", k=KD),
                                    in_=_r_pk(w_in)[:, :, mt * 128:(mt + 1) * 128]).then_inc(s_wE[sl], 16)
                    w_ld = cwE[sl].inc(16)

                    mm_dones = {}
                    for ti, (c0, w) in enumerate(T_TILES):
                        pbi = par * 4 + ti

                        @blk.tensor
                        def _(t):
                            if ti == 0:
                                t.wait_ge(s_wE[sl], w_ld)
                                if it == 0:
                                    t.wait_ge(s_c, LN3S_LD)
                                if dv_guardE.get(par):
                                    t.wait_ge(s_dvA, dv_guardE[par])
                            last = None
                            for k in range(KD):
                                last = t.matmul(pbE[pbi][:, 0:w], wslE[sl][:, k * 128:(k + 1) * 128],
                                                ln3s[:, k * TLOC + c0:k * TLOC + c0 + w],
                                                start=(k == 0), stop=(k == KD - 1))
                            last.then_inc(s_mmA, 1)
                        mm_dones[ti] = cmm.inc()
                    slab_mmE[it] = cmm.v
                    st_guard = cgE[par].v

                    @blk.vector
                    def _(v):
                        if st_guard:
                            v.wait_ge(s_gE[par], st_guard)
                        for ti, (c0, w) in enumerate(T_TILES):
                            v.wait_ge(s_mmA, mm_dones[ti])
                            v.tensor_copy(out=stgE[par][:, c0:c0 + w], in_=pbE[par * 4 + ti][:, 0:w])
                        v.drain().then_inc(s_dvA, 1)
                    dv_guardE[par] = cdv.inc()
                    ev_here = dv_guardE[par]

                    @blk.sync
                    def _(s):
                        s.wait_ge(s_dvA, ev_here)
                        s.dma_start(out=out_d[mt * 128:(mt + 1) * 128, :],
                                    in_=stgE[par][:]).then_inc(s_gE[par], 16)
                    cgE[par].inc(16)
            e_mm = cmm.v
            e_ev = cev.v
            e_dv = cdv.v
            e_g = [cgE[i].v for i in range(2)]
            e_gsem = list(s_gE)
        lns_ctx.close()

        _scope_next(nc, "phF")
        with ExitStack() as pF:
            w1b = pF.enter_context(nc.sbuf_tensor("w1bF", [128, KF * D], BF16, side="right"))
            usl = [pF.enter_context(nc.sbuf_tensor(f"uslF{i}", [128, KF * 128], BF16, side="right")) for i in range(2)]
            vsl = [pF.enter_context(nc.sbuf_tensor(f"vslF{i}", [128, KF * 128], BF16, side="right")) for i in range(2)]
            subF = pF.enter_context(nc.sbuf_tensor("subF", [128, KF * 128], BF16, side="right"))
            hhb = pF.enter_context(nc.sbuf_tensor("hhbF", [128, KF * 128], BF16, side="right"))
            o_f = pF.enter_context(nc.sbuf_tensor("o_fF", [128, KD * 128], F32, side="right"))
            xi_b = pF.enter_context(nc.sbuf_tensor("xi_bF", [128, KD * 128], BF16, side="right"))
            xi_f = pF.enter_context(nc.sbuf_tensor("xi_fF", [128, KD * 128], F32, side="right"))
            dcol = pF.enter_context(nc.sbuf_tensor("dcolF", [128, 1], F32, side="right"))
            d2col = pF.enter_context(nc.sbuf_tensor("d2colF", [128, 1], F32, side="right"))
            rs_col = pF.enter_context(nc.sbuf_tensor("rs_colF", [128, 1], F32, side="right"))
            ls_s = pF.enter_context(nc.sbuf_tensor("ls_sF", [1, 1], F32, side="right"))
            ls_all = pF.enter_context(nc.sbuf_tensor("ls_allF", [1, 8], F32, side="right"))
            lt_col = pF.enter_context(nc.sbuf_tensor("lt_colF", [128, 1], F32, side="right"))
            pbO = [pF.enter_context(nc.psum_tensor(f"pbOF{i}", [128, 512], F32)) for i in range(2)]
            pbL2 = pF.enter_context(nc.psum_tensor("pbL2F", [128, 512], F32))
            s_uF = [sem("s_uF0"), sem("s_uF1")]
            cuF = [Cnt(), Cnt()]
            s_stF = sem("s_stF"); cstF = Cnt()

            @blk.gpsimd
            def _(g):
                g.wait_ge(s_mmA, e_mm)
                g.wait_ge(s_evA, e_ev)
                g.dma_start(out=w1b[:].rearrange("p (k m) -> p k m", k=KF),
                            in_=_r_pk(w1a_i)).then_inc(s_c, 16)
            CC.inc(16)
            W1B_LD = CC.v

            @blk.vector
            def _(v):
                v.wait_ge(s_mmA, e_mm)
                v.memset(dcol[:], 1.0)
                v.drain().then_inc(s_dvA, 1)
            init_dv = cdv.inc()

            hh_doneF = {}
            rs_doneF = {}
            for i in range(16):
                sl = i % 2

                @blk.gpsimd
                def _(g):
                    if i >= 2:
                        g.wait_ge(s_dvA, hh_doneF[i - 2])
                    else:
                        for qi in range(2):
                            g.wait_ge(e_gsem[qi], e_g[qi])
                        g.wait_ge(s_cc, LN3_AG)
                    g.dma_start(out=usl[sl][:].rearrange("p (k m) -> p k m", k=KF),
                                in_=_r_pk(uT_d)[:, :, i * 128:(i + 1) * 128]).then_inc(s_uF[sl], 16)
                    g.dma_start(out=vsl[sl][:].rearrange("p (k m) -> p k m", k=KF),
                                in_=_r_pk(vT_d)[:, :, i * 128:(i + 1) * 128]).then_inc(s_uF[sl], 16)
                    if i >= 1:
                        g.wait_ge(s_dvA, rs_doneF[i - 1])
                    g.dma_start(out=xi_b[:].rearrange("p (k t) -> p k t", k=KD),
                                in_=_r_pk(ln3s_d)[:, :, i * 128:(i + 1) * 128]).then_inc(s_uF[sl], 16)
                u_ld = cuF[sl].inc(48)

                @blk.vector
                def _(v):
                    if i == 0:
                        v.wait_ge(s_dvA, init_dv)
                    v.tensor_mul(d2col[:], dcol[:], dcol[:])
                    v.drain().then_inc(s_dvA, 1)
                d2_done = cdv.inc()

                @blk.scalar
                def _(sc):
                    sc.wait_ge(s_uF[sl], u_ld)
                    sc.wait_ge(s_dvA, d2_done)
                    if i == 0:
                        sc.wait_ge(s_c, W1B_LD)
                    sc.activation(out=subF[:], in_=usl[sl][:], func=AF.Silu, scale=dcol[:])
                    sc.drain().then_inc(s_evA, 1)
                silu_done = cev.inc()

                @blk.vector
                def _(v):
                    v.wait_ge(s_evA, silu_done)
                    v.tensor_mul(hhb[:], subF[:], vsl[sl][:])
                    v.drain().then_inc(s_dvA, 1)
                hh_doneF[i] = cdv.inc()

                @blk.tensor
                def _(t):
                    t.wait_ge(s_dvA, hh_doneF[i])
                    if i >= 1:
                        t.wait_ge(s_evA, o_ev)
                    last = None
                    for mo in range(KD):
                        pby = pbO[mo // 4]
                        of = (mo % 4) * 128
                        for k in range(KF):
                            last = t.matmul(pby[:, of:of + 128],
                                            w1b[:, k * D + mo * 128:k * D + (mo + 1) * 128],
                                            hhb[:, k * 128:(k + 1) * 128],
                                            start=(k == 0), stop=(k == KF - 1))
                    last.then_inc(s_mmA, 1)
                o_mm = cmm.inc()

                @blk.scalar
                def _(sc):
                    sc.wait_ge(s_mmA, o_mm)
                    sc.wait_ge(s_uF[sl], u_ld)
                    if i >= 1:
                        sc.wait_ge(s_stF, ostF)
                    for mo in range(KD):
                        pby = pbO[mo // 4]
                        of = (mo % 4) * 128
                        sc.activation(out=o_f[:, mo * 128:(mo + 1) * 128], in_=pby[:, of:of + 128],
                                      func=AF.Copy, scale=d2col[:])
                    sc.drain().then_inc(s_evA, 1)
                o_ev = cev.inc()

                @blk.gpsimd
                def _(g):
                    g.wait_ge(s_evA, o_ev)
                    g.dma_start(out=ato_in.ap()[:, i * 128:(i + 1) * 128].rearrange("(k p) t -> p k t", p=128),
                                in_=o_f[:].rearrange("p (k t) -> p k t", k=KD)).then_inc(s_stF, 16)
                ostF = cstF.inc(16)

                @blk.vector
                def _(v):
                    v.wait_ge(s_evA, o_ev)
                    v.tensor_sub(xi_f[:], o_f[:], xi_b[:])
                    v.drain()
                    v.tensor_mul(xi_f[:], xi_f[:], xi_f[:])
                    v.drain()
                    v.reduce_sum(rs_col[:], xi_f[:], axis=mybir.AxisListType.X)
                    v.drain().then_inc(s_dvA, 1)
                rs_done = cdv.inc()
                rs_doneF[i] = rs_done

                @blk.tensor
                def _(t):
                    t.wait_ge(s_dvA, rs_done)
                    if i >= 1:
                        t.wait_ge(s_evA, ls_ev)
                    t.matmul(pbL2[0:1, 0:1], ones_col[:], rs_col[:], start=True, stop=True).then_inc(s_mmA, 1)
                ls_mm = cmm.inc()

                @blk.scalar
                def _(sc):
                    sc.wait_ge(s_mmA, ls_mm)
                    sc.copy(out=ls_s[:], in_=pbL2[0:1, 0:1])
                    sc.drain().then_inc(s_evA, 1)
                ls_ev = cev.inc()

                @blk.gpsimd
                def _(g):
                    g.wait_ge(s_evA, ls_ev)
                    g.dma_start(out=agl_in[:, :], in_=ls_s[:]).then_inc(s_cc, 16)
                    g.wait_ge(s_cc, CCC.v + 16)
                    g.collective_compute(
                        "AllGather", ALU.bypass,
                        replica_groups=[list(range(8))],
                        ins=[agl_in.ap().opt()], outs=[agl_out.ap().opt()],
                    ).then_inc(s_cc, 1)
                    g.wait_ge(s_cc, CCC.v + 17)
                    g.dma_start(out=ls_all[:], in_=agl_out.rearrange("a b -> b a")).then_inc(s_cc, 16)
                    g.wait_ge(s_cc, CCC.v + 33)
                    g.engine_nop().then_inc(s_cc, 1)
                CCC.inc(34)
                ag_done = CCC.v

                @blk.vector
                def _(v):
                    v.wait_ge(s_cc, ag_done)
                    v.reduce_sum(ls_s[:], ls_all[:], axis=mybir.AxisListType.X)
                    v.drain().then_inc(s_dvA, 1)
                tot_done = cdv.inc()

                @blk.tensor
                def _(t):
                    t.wait_ge(s_dvA, tot_done)
                    t.matmul(pbL2[:, 0:1], ones_row[:], ls_s[:], start=True, stop=True).then_inc(s_mmA, 1)
                bcF_mm = cmm.inc()

                @blk.vector
                def _(v):
                    v.wait_ge(s_mmA, bcF_mm)
                    v.tensor_single_scalar(lt_col[:], pbL2[:, 0:1], 1.0 / (2 * 512 * 1024), ALU.mult)
                    v.drain()
                    v.tensor_scalar(out=lt_col[:], in0=lt_col[:], scalar1=-LR, scalar2=1.0,
                                    op0=ALU.mult, op1=ALU.add)
                    v.drain()
                    v.tensor_mul(dcol[:], dcol[:], lt_col[:])
                    v.drain().then_inc(s_dvA, 1)
                cdv.inc()
            f_dv = cdv.v
            f_mm = cmm.v
            f_ev = cev.v
            f_st = cstF.v

            @blk.gpsimd
            def _(g):
                g.wait_ge(s_stF, f_st)
                g.collective_compute(
                    "AllGather", ALU.bypass,
                    replica_groups=[list(range(8))],
                    ins=[ato_in.ap().opt()], outs=[ato_out.ap().opt()],
                ).then_inc(s_cc, 1)
            ATO_AG = CCC.inc()

        _scope_next(nc, "phG")
        gR_ctx = ExitStack()
        ln4b = gR_ctx.enter_context(nc.sbuf_tensor("ln4bF", [128, KD * TLOC], BF16, side="right"))
        ysbG = gR_ctx.enter_context(nc.sbuf_tensor("ysbG", [128, KD * TLOC], BF16, side="right"))
        s_sfG = [sem("s_sfG0"), sem("s_sfG1")]
        csfG = [Cnt(), Cnt()]
        s_wG = [sem("s_wG0"), sem("s_wG1")]
        cwG = [Cnt(), Cnt()]

        g12_ctx = ExitStack()
        s8f = g12_ctx.enter_context(nc.sbuf_tensor("s8fF", [8, 4 * 512], F32, side="right"))
        with ExitStack() as g1ctx:
            abf = g1ctx.enter_context(nc.sbuf_tensor("abfF", [128, KD * 512], BF16, side="right"))
            acf = g1ctx.enter_context(nc.sbuf_tensor("acfF", [128, 512], F32, side="right"))
            lnqG = g1ctx.enter_context(nc.sbuf_tensor("lnqG", [128, KD * 512], BF16, side="right"))
            lsqG = g1ctx.enter_context(nc.sbuf_tensor("lsqG", [128, KD * 512], BF16, side="right"))
            lnfG = g1ctx.enter_context(nc.sbuf_tensor("lnfG", [128, KD * 512], F32, side="right"))
            pbS5 = g1ctx.enter_context(nc.psum_tensor("pbS5", [128, 512], F32))
            pbQ5 = g1ctx.enter_context(nc.psum_tensor("pbQ5", [128, 512], F32))
            pbM5 = g1ctx.enter_context(nc.psum_tensor("pbM5", [128, 512], F32))
            pbR5 = g1ctx.enter_context(nc.psum_tensor("pbR5", [128, 512], F32))
            pbL5 = g1ctx.enter_context(nc.psum_tensor("pbL5", [128, 512], F32))
            s_at = sem("s_at"); cat_ = Cnt()
            lg_ev5 = 0
            ln_prev5 = {}
            @blk.gpsimd
            def _(g):
                g.wait_ge(s_cc, ATO_AG)
                g.wait_ge(s_c, OFFS_LD)
                for jm in range(4):
                    g.reg_load(r_off, goff[0:1, jm:jm + 1])
                    g.dma_start(
                        out=atrb_d[jm * 1024:(jm + 1) * 1024, :],
                        in_=bass.AP(ato_out, r_off, [[TLOC, 1024], [1, 512]])
                        ).then_inc(s_at, 16)
            cat_.inc(64)
            ATL_ST = cat_.v

            for ti, (c0, w) in enumerate(T_TILES):
                @blk.gpsimd
                def _(g):
                    if ti == 0:
                        g.wait_ge(s_at, ATL_ST)
                        g.wait_ge(s_dvA, f_dv)
                        g.wait_ge(s_mmA, f_mm)
                        g.wait_ge(s_evA, f_ev)
                    else:
                        g.wait_ge(s_dvA, x3_done)
                    for jj in range(4):
                        g.dma_start(out=abf[:].rearrange("p (k t) -> p k t", k=KD)[:, :, jj * 128:(jj + 1) * 128],
                                    in_=atrb_d.ap()[jj * 1024:(jj + 1) * 1024, 128 * ti:128 * ti + 128]
                                        .rearrange("(k p) t -> p k t", p=128)).then_inc(s_at, 16)
                at_ld = cat_.inc(64)

                @blk.vector
                def _(v):
                    v.wait_ge(s_at, at_ld)
                    for k in range(KD):
                        v.tensor_copy(out=acf[:, 0:w], in_=abf[:, k * 512:k * 512 + w])
                        v.drain()
                        v.tensor_add(xres[:, k * THAL + 3 + c0:k * THAL + 3 + c0 + w],
                                     xres[:, k * THAL + 3 + c0:k * THAL + 3 + c0 + w],
                                     acf[:, 0:w])
                        v.drain()
                        v.tensor_copy(out=lnqG[:, k * 512:k * 512 + w],
                                      in_=xres[:, k * THAL + 3 + c0:k * THAL + 3 + c0 + w])
                        v.drain()
                    v.engine_nop().then_inc(s_dvA, 1)
                x3_done = cdv.inc()

                @blk.sync
                def _(s):
                    s.wait_ge(s_dvA, x3_done)
                    s.dma_start(out=x3_d[:, c0:c0 + w].rearrange("(k p) t -> p k t", p=128),
                                in_=xres[:].rearrange("p (k t) -> p k t", k=KD)[:, :, 3 + c0:3 + c0 + w]
                                ).then_inc(s_sfG[ti % 2], 16)
                csfG[ti % 2].inc(16)

                @blk.scalar
                def _(sc):
                    sc.wait_ge(s_dvA, x3_done)
                    sc.activation(out=lsqG[:].rearrange("p (k t) -> p k t", k=KD)[:, :, 0:w],
                                  in_=lnqG[:].rearrange("p (k t) -> p k t", k=KD)[:, :, 0:w], func=AF.Square)
                    sc.drain().then_inc(s_evA, 1)
                sq_done = cev.inc()

                @blk.tensor
                def _(t):
                    t.wait_ge(s_evA, sq_done)
                    if ti >= 1:
                        t.wait_ge(s_dvA, ln_prev5[ti - 1])
                    for k in range(KD):
                        t.matmul(pbS5[0:1, 0:w], ones_bf[:], lnqG[:, k * 512:k * 512 + w],
                                 start=(k == 0), stop=(k == KD - 1))
                    last = None
                    for k in range(KD):
                        last = t.matmul(pbQ5[0:1, 0:w], ones_bf[:], lsqG[:, k * 512:k * 512 + w],
                                        start=(k == 0), stop=(k == KD - 1))
                    last.then_inc(s_mmA, 1)
                stats_mm = cmm.inc()

                @blk.vector
                def _(v):
                    v.wait_ge(s_mmA, stats_mm)
                    v.tensor_single_scalar(m_row[:, 0:w], pbS5[0:1, 0:w], 1.0 / D, ALU.mult)
                    v.tensor_single_scalar(q_row[:, 0:w], pbQ5[0:1, 0:w], 1.0 / D, ALU.mult)
                    v.drain()
                    v.tensor_mul(v_row[:, 0:w], m_row[:, 0:w], m_row[:, 0:w])
                    v.drain()
                    v.tensor_sub(v_row[:, 0:w], q_row[:, 0:w], v_row[:, 0:w])
                    v.drain()
                    v.tensor_single_scalar(v_row[:, 0:w], v_row[:, 0:w], EPS, ALU.add)
                    v.drain().then_inc(s_dvA, 1)
                var_done = cdv.inc()

                @blk.scalar
                def _(sc):
                    sc.wait_ge(s_dvA, var_done)
                    sc.activation(out=s_row[:, 0:w], in_=v_row[:, 0:w], func=AF.Sqrt)
                    sc.drain().then_inc(s_evA, 1)
                std_done = cev.inc()

                @blk.vector
                def _(v):
                    v.wait_ge(s_evA, std_done)
                    v.reciprocal(r_row[:, 0:w], s_row[:, 0:w])
                    v.drain().then_inc(s_dvA, 1)
                rinv_done = cdv.inc()

                @blk.tensor
                def _(t):
                    t.wait_ge(s_dvA, rinv_done)
                    t.matmul(pbM5[:, 0:w], ones_row[:], m_row[:, 0:w], start=True, stop=True)
                    last = t.matmul(pbR5[:, 0:w], ones_row[:], r_row[:, 0:w], start=True, stop=True)
                    last.then_inc(s_mmA, 1)
                bc_mm = cmm.inc()

                @blk.vector
                def _(v):
                    v.wait_ge(s_mmA, bc_mm)
                    for k in range(KD):
                        v.tensor_sub(lnfG[:, k * 512:k * 512 + w],
                                     xres[:, k * THAL + 3 + c0:k * THAL + 3 + c0 + w], pbM5[:, 0:w])
                        v.drain()
                        v.tensor_mul(lnfG[:, k * 512:k * 512 + w],
                                     lnfG[:, k * 512:k * 512 + w], pbR5[:, 0:w])
                        v.drain()
                        v.tensor_copy(out=ln4b[:, k * TLOC + c0:k * TLOC + c0 + w],
                                      in_=lnfG[:, k * 512:k * 512 + w])
                        v.drain()
                    v.engine_nop().then_inc(s_dvA, 1)
                lnf_done = cdv.inc()

                @blk.tensor
                def _(t):
                    t.wait_ge(s_dvA, lnf_done)
                    if lg_ev5:
                        t.wait_ge(s_evA, lg_ev5)
                    last = None
                    for k in range(KD):
                        last = t.matmul(pbL5[0:8, 0:w], gwb[:, k * 8:(k + 1) * 8],
                                        lnfG[:, k * 512:k * 512 + w],
                                        start=(k == 0), stop=(k == KD - 1))
                    last.then_inc(s_mmA, 1)
                lg_mm = cmm.inc()

                @blk.scalar
                def _(sc):
                    sc.wait_ge(s_mmA, lg_mm)
                    sc.copy(out=s8f[:, ti * 512:ti * 512 + w], in_=pbL5[0:8, 0:w])
                    sc.drain().then_inc(s_evA, 1)
                lg_ev5 = cev.inc()
                ln_prev5[ti] = cdv.v
            g1_dv = cdv.v
            g1_ev = cev.v
            g1_mm = cmm.v
            g1_sf = [csfG[0].v, csfG[1].v]
        xres_ctx.close()

        with ExitStack() as g2ctx:
            lg2 = g2ctx.enter_context(nc.sbuf_tensor("lg2", [128, 8], F32, side="right"))
            bs2 = g2ctx.enter_context(nc.sbuf_tensor("bs2", [128, 8], F32, side="right"))
            srow2 = g2ctx.enter_context(nc.sbuf_tensor("srow2", [128, 1], F32, side="right"))
            top82 = g2ctx.enter_context(nc.sbuf_tensor("top82", [128, 8], F32, side="right"))
            idxu2 = g2ctx.enter_context(nc.sbuf_tensor("idxu2", [128, 8], U32, side="right"))
            idxf2 = g2ctx.enter_context(nc.sbuf_tensor("idxf2", [128, 8], F32, side="right"))
            oh2 = g2ctx.enter_context(nc.sbuf_tensor("oh2", [128, 8], F32, side="right"))
            wcol2 = g2ctx.enter_context(nc.sbuf_tensor("wcol2", [128, 2], F32, side="right"))
            cwtm2 = g2ctx.enter_context(nc.sbuf_tensor("cwtm2", [128, 8], F32, side="right"))
            cw8f2 = g2ctx.enter_context(nc.sbuf_tensor("cw8f2", [8, TLOC], F32, side="right"))
            pbT2 = g2ctx.enter_context(nc.psum_tensor("pbT2", [128, 512], F32))
            pbU2 = g2ctx.enter_context(nc.psum_tensor("pbU2", [128, 512], F32))

            @blk.tensor
            def _(t):
                t.wait_ge(s_evA, g1_ev)
                t.wait_ge(s_mmA, g1_mm)
                t.matmul(pbU2[:, 0:8], ones_row[:], gbrow[:], start=True, stop=True).then_inc(s_mmA, 1)
            gbmm = cmm.inc()

            @blk.scalar
            def _(sc):
                sc.wait_ge(s_mmA, gbmm)
                sc.copy(out=gb_b[:], in_=pbU2[:, 0:8])
                sc.drain().then_inc(s_evA, 1)
            gb_done = cev.inc()

            lgB = g2ctx.enter_context(nc.sbuf_tensor("lgB2", [128, 128], F32, side="right"))
            srB = g2ctx.enter_context(nc.sbuf_tensor("srB2", [128, 16], F32, side="right"))
            topB = g2ctx.enter_context(nc.sbuf_tensor("topB2", [128, 128], F32, side="right"))
            idxuB = g2ctx.enter_context(nc.sbuf_tensor("idxuB2", [128, 128], U32, side="right"))
            idxfB = g2ctx.enter_context(nc.sbuf_tensor("idxfB2", [128, 128], F32, side="right"))
            ohB = g2ctx.enter_context(nc.sbuf_tensor("ohB2", [128, 128], F32, side="right"))
            tmpB = g2ctx.enter_context(nc.sbuf_tensor("tmpB2", [128, 128], F32, side="right"))
            wcB = g2ctx.enter_context(nc.sbuf_tensor("wcB2", [128, 16], F32, side="right"))
            cwtmB = g2ctx.enter_context(nc.sbuf_tensor("cwtmB2", [128, 128], F32, side="right"))

            @blk.tensor
            def _(t):
                t.wait_ge(s_evA, gb_done)
                t.wait_ge(s_cc, IOTA_DONE)
                last = None
                for tk in range(16):
                    last = t.transpose(pbT2[:, tk * 8:(tk + 1) * 8],
                                       s8f[0:8, tk * 128:(tk + 1) * 128], ident[0:8, 0:8])
                last.then_inc(s_mmA, 1)
            trB_mm = cmm.inc()

            @blk.scalar
            def _(sc):
                sc.wait_ge(s_mmA, trB_mm)
                sc.copy(out=lgB[:], in_=pbT2[:, 0:128])
                sc.drain().then_inc(s_evA, 1)
            lgB_done = cev.inc()

            @blk.vector
            def _(v):
                v.wait_ge(s_evA, lgB_done)
                for tk in range(16):
                    v.reduce_max(srB[:, tk:tk + 1], lgB[:, tk * 8:(tk + 1) * 8],
                                 axis=mybir.AxisListType.X)
                v.drain()
                v.tensor_single_scalar(srB[:], srB[:], -1.0, ALU.mult)
                v.drain().then_inc(s_dvA, 1)
            mxB_done = cdv.inc()

            @blk.scalar
            def _(sc):
                sc.wait_ge(s_dvA, mxB_done)
                for tk in range(16):
                    sc.activation(out=lgB[:, tk * 8:(tk + 1) * 8], in_=lgB[:, tk * 8:(tk + 1) * 8],
                                  func=AF.Exp, bias=srB[:, tk:tk + 1])
                sc.drain().then_inc(s_evA, 1)
            exB_done = cev.inc()

            @blk.vector
            def _(v):
                v.wait_ge(s_evA, exB_done)
                for tk in range(16):
                    v.reduce_sum(srB[:, tk:tk + 1], lgB[:, tk * 8:(tk + 1) * 8],
                                 axis=mybir.AxisListType.X)
                v.drain()
                v.reciprocal(srB[:], srB[:])
                v.drain()
                for tk in range(16):
                    v.tensor_single_scalar(lgB[:, tk * 8:(tk + 1) * 8],
                                           lgB[:, tk * 8:(tk + 1) * 8],
                                           srB[:, tk:tk + 1], ALU.mult)
                v.drain()
                for tk in range(16):
                    v.tensor_add(tmpB[:, tk * 8:(tk + 1) * 8], lgB[:, tk * 8:(tk + 1) * 8],
                                 gb_b[:])
                v.drain()
                for tk in range(16):
                    v.max(out=topB[:, tk * 8:(tk + 1) * 8], in_=tmpB[:, tk * 8:(tk + 1) * 8])
                v.drain()
                for tk in range(16):
                    v.max_index(out=idxuB[:, tk * 8:(tk + 1) * 8],
                                in_max=topB[:, tk * 8:(tk + 1) * 8],
                                in_values=tmpB[:, tk * 8:(tk + 1) * 8])
                v.drain()
                v.tensor_copy(out=idxfB[:], in_=idxuB[:])
                v.drain().then_inc(s_dvA, 1)
            topB_done = cdv.inc()

            @blk.vector
            def _(v):
                v.wait_ge(s_dvA, topB_done)
                for k2 in range(2):
                    for tk in range(16):
                        v.tensor_single_scalar(ohB[:, tk * 8:(tk + 1) * 8], it8[:],
                                               idxfB[:, tk * 8 + k2:tk * 8 + k2 + 1],
                                               ALU.is_equal)
                    v.drain()
                    v.tensor_mul(tmpB[:], lgB[:], ohB[:])
                    v.drain()
                    for tk in range(16):
                        v.reduce_sum(wcB[:, tk:tk + 1], tmpB[:, tk * 8:(tk + 1) * 8],
                                     axis=mybir.AxisListType.X)
                    v.drain()
                    for tk in range(16):
                        if k2 == 0:
                            v.tensor_single_scalar(cwtmB[:, tk * 8:(tk + 1) * 8],
                                                   ohB[:, tk * 8:(tk + 1) * 8],
                                                   wcB[:, tk:tk + 1], ALU.mult)
                        else:
                            v.scalar_tensor_tensor(cwtmB[:, tk * 8:(tk + 1) * 8],
                                                   ohB[:, tk * 8:(tk + 1) * 8],
                                                   wcB[:, tk:tk + 1],
                                                   cwtmB[:, tk * 8:(tk + 1) * 8],
                                                   ALU.mult, ALU.add)
                    v.drain()
                v.engine_nop().then_inc(s_dvA, 1)
            cwB_done = cdv.inc()

            cwT_ev2 = 0
            for qb in range(4):
                my_ev = cwT_ev2

                @blk.tensor
                def _(t):
                    if qb == 0:
                        t.wait_ge(s_dvA, cwB_done)
                    if my_ev:
                        t.wait_ge(s_evA, my_ev)
                    last = None
                    for j in range(4):
                        tk = qb * 4 + j
                        last = t.transpose(pbU2[0:8, j * 128:(j + 1) * 128],
                                           cwtmB[:, tk * 8:(tk + 1) * 8], ident[:])
                    last.then_inc(s_mmA, 1)
                cwT_mm = cmm.inc()

                @blk.scalar
                def _(sc):
                    sc.wait_ge(s_mmA, cwT_mm)
                    for j in range(4):
                        tk = qb * 4 + j
                        sc.copy(out=cw8f2[:, tk * 128:(tk + 1) * 128],
                                in_=pbU2[0:8, j * 128:(j + 1) * 128])
                    sc.drain().then_inc(s_evA, 1)
                cwT_ev2 = cev.inc()

            @blk.sync
            def _(s):
                s.wait_ge(s_evA, cwT_ev2)
                s.dma_start(out=cw8_d[:, :], in_=cw8f2[:]).then_inc(s_c, 16)
            CC.inc(16)
            CW8_ST = CC.v

            g2_ev = cev.v
            g2_dv = cdv.v
            g2_mm = cmm.v
        g12_ctx.close()

        _scope_next(nc, "phG34")
        with ExitStack() as g34ctx:
            wslG = [g34ctx.enter_context(nc.sbuf_tensor(f"wslG{i}", [128, KD * 1024], BF16, side="right")) for i in range(4)]
            w2sG = g34ctx.enter_context(nc.sbuf_tensor("w2sG", [128, KD * D], BF16, side="right"))
            h1eG = g34ctx.enter_context(nc.sbuf_tensor("h1eG", [128, 32 * 512], BF16, side="right"))
            asvG = g34ctx.enter_context(nc.sbuf_tensor("asvG", [128, 16 * 512], BF16, side="right"))
            cweG = g34ctx.enter_context(nc.sbuf_tensor("cweG", [128, 2 * 512], BF16, side="right"))
            ywG = g34ctx.enter_context(nc.sbuf_tensor("ywG", [128, 512], BF16, side="right"))
            b2rG = g34ctx.enter_context(nc.sbuf_tensor("b2rG", [1, D], BF16, side="right"))
            cwrG = g34ctx.enter_context(nc.sbuf_tensor("cwrG", [1, TLOC], BF16, side="right"))
            b1cG = g34ctx.enter_context(nc.sbuf_tensor("b1cG", [128, 32], F32, side="right"))
            b3cG = g34ctx.enter_context(nc.sbuf_tensor("b3cG", [128, 32], F32, side="right"))
            b1r16 = g34ctx.enter_context(nc.sbuf_tensor("b1r16", [16, 128], F32, side="right"))
            b3r16 = g34ctx.enter_context(nc.sbuf_tensor("b3r16", [16, 128], F32, side="right"))

            GB = [[g34ctx.enter_context(nc.psum_tensor(f"pbG{gi}{j}", [128, 512], F32)) for j in range(2)]
                  for gi in range(4)]
            s_wGb = sem("s_wGb"); cwGb = Cnt()
            SEMMAP = {"dv": s_dvA, "ev": s_evA, "mm": s_mmA}
            grp_guard = {gi: [("dv", g2_dv), ("ev", g2_ev)] for gi in range(4)}

            def twait(t, gl):
                for kind, val in gl:
                    if val:
                        t.wait_ge(SEMMAP[kind], val)

            rot = [0]

            def nextgrp():
                gi = rot[0] % 4
                rot[0] += 1
                return gi

            w1_freeG = {}
            w3_freeG = {}
            w2_freeG = {}
            EXP = [-1] + list(range(8))
            for ei, e in enumerate(EXP):
                bi = ei % 2

                @blk.gpsimd
                def _(g):
                    if ei >= 1:
                        g.wait_ge(s_mmA, w1_freeG[ei - 1])
                    else:
                        g.wait_ge(s_dvA, g2_dv)
                        g.wait_ge(s_evA, g2_ev)
                        g.wait_ge(s_mmA, g2_mm)
                        g.wait_ge(s_sfG[0], g1_sf[0])
                        g.wait_ge(s_sfG[1], g1_sf[1])
                    for half in range(2):
                        src = _r_pk(sw1p_i if e < 0 else w1p_i[e])[:, :, half * 1024:half * 1024 + 1024]
                        g.dma_start(out=wslG[2 + half][:].rearrange("p (k m) -> p k m", k=KD),
                                    in_=src).then_inc(s_wG[0], 16)
                    b1src = sb1p_i if e < 0 else b1p_i[e:e + 1, :]
                    g.dma_start(out=b1r16[:],
                                in_=b1src.rearrange("a (m p) -> (a m) p", p=128)).then_inc(s_wG[0], 16)
                w1_ld = cwG[0].inc(48)

                @blk.gpsimd
                def _(g):
                    if ei >= 1:
                        g.wait_ge(s_mmA, w3_freeG[ei - 1])
                    for half in range(2):
                        src = _r_pk(sw3p_i if e < 0 else w3p_i[e])[:, :, half * 1024:half * 1024 + 1024]
                        g.dma_start(out=wslG[half][:].rearrange("p (k m) -> p k m", k=KD),
                                    in_=src).then_inc(s_wGb, 16)
                    b3src = sb3p_i if e < 0 else b3p_i[e:e + 1, :]
                    g.dma_start(out=b3r16[:],
                                in_=b3src.rearrange("a (m p) -> (a m) p", p=128)).then_inc(s_wGb, 16)
                w3_ld = cwGb.inc(48)

                @blk.gpsimd
                def _(g):
                    g.wait_ge(s_c, CW8_ST)
                    if ei >= 1:
                        g.wait_ge(s_mmA, w2_freeG[ei - 1])
                    g.dma_start(out=w2sG[:].rearrange("p (k m) -> p k m", k=KD),
                                in_=_r_pk(sw2t_i if e < 0 else w2t_i[e])).then_inc(s_wG[1], 16)
                    if e < 0:
                        g.dma_start(out=b2rG[:], in_=sb2_i).then_inc(s_wG[1], 16)
                        g.dma_start(out=cwrG[:], in_=cw8_d[0:1, :]).then_inc(s_wG[1], 16)
                    else:
                        g.dma_start(out=cwrG[:], in_=cw8_d[e:e + 1, :]).then_inc(s_wG[1], 16)
                        g.dma_start(out=b2rG[:], in_=b2_i[e:e + 1, :]).then_inc(s_wG[1], 16)
                w2_ld = cwG[1].inc(48)

                gi = nextgrp(); glb = grp_guard[gi]; my_gi = gi

                @blk.tensor
                def _(t):
                    t.wait_ge(s_wG[0], w1_ld)
                    t.wait_ge(s_wGb, w3_ld)
                    twait(t, glb)
                    t.transpose(GB[my_gi][0][:, 0:16], b1r16[:], ident[0:16, 0:16])
                    last = t.transpose(GB[my_gi][1][:, 0:16], b3r16[:], ident[0:16, 0:16])
                    last.then_inc(s_mmA, 1)
                bc_mm = cmm.inc()

                @blk.scalar
                def _(sc):
                    sc.wait_ge(s_mmA, bc_mm)
                    sc.copy(out=b1cG[:, bi * 16:bi * 16 + 16], in_=GB[my_gi][0][:, 0:16])
                    sc.copy(out=b3cG[:, bi * 16:bi * 16 + 16], in_=GB[my_gi][1][:, 0:16])
                    sc.drain().then_inc(s_evA, 1)
                bc_ev = cev.inc()
                grp_guard[gi] = [("ev", bc_ev)]

                w1mm_last = 0
                w3mm_last = 0
                for pp in range(2):
                    PTI = (2 * pp, 2 * pp + 1)
                    w1mm = {}
                    ev_batch = []
                    for m in range(16):
                        ci = 2 + m // 8
                        mm_ = m % 8
                        gi = nextgrp(); gl = grp_guard[gi]; grp = GB[gi]

                        @blk.tensor
                        def _(t):
                            if pp == 0 and m == 0:
                                t.wait_ge(s_wG[0], w1_ld)
                            twait(t, gl)
                            last = None
                            for k in range(KD):
                                for tj in range(2):
                                    last = t.matmul(grp[tj][:, :],
                                                    wslG[ci][:, k * 1024 + mm_ * 128:k * 1024 + (mm_ + 1) * 128],
                                                    ln4b[:, k * TLOC + PTI[tj] * 512:k * TLOC + (PTI[tj] + 1) * 512],
                                                    start=(k == 0), stop=(k == KD - 1))
                            last.then_inc(s_mmA, 1)
                        w1mm[m] = cmm.inc()
                        my_m, my_gi = m, gi

                        @blk.vector
                        def _(v):
                            v.wait_ge(s_mmA, w1mm[my_m])
                            if pp == 0 and my_m == 0:
                                v.wait_ge(s_evA, bc_ev)
                            for tj in range(2):
                                v.tensor_single_scalar(h1eG[:, (my_m * 2 + tj) * 512:(my_m * 2 + tj + 1) * 512],
                                                       GB[my_gi][tj][:, :],
                                                       b1cG[:, bi * 16 + my_m:bi * 16 + my_m + 1], ALU.add)
                            if my_m % 2 == 1:
                                v.drain().then_inc(s_dvA, 2)
                        ev_batch.append(gi)
                        if m % 2 == 1:
                            dv = cdv.inc(2)
                            for gg in ev_batch:
                                grp_guard[gg] = [("dv", dv)]
                            ev_batch = []
                    w1mm_last = cmm.v

                    if e >= 0:
                        gi = nextgrp(); gl = grp_guard[gi]; grp = GB[gi]

                        @blk.tensor
                        def _(t):
                            twait(t, gl)
                            t.wait_ge(s_wG[1], w2_ld)
                            last = None
                            for tj in range(2):
                                last = t.matmul(grp[tj][:, :], ones_row_bf[:],
                                                cwrG[0:1, PTI[tj] * 512:(PTI[tj] + 1) * 512],
                                                start=True, stop=True)
                            last.then_inc(s_mmA, 1)
                        cwb_mm1 = cmm.inc()
                        my_g1 = gi

                        @blk.vector
                        def _(v):
                            v.wait_ge(s_mmA, cwb_mm1)
                            for tj in range(2):
                                v.tensor_copy(out=cweG[:, tj * 512:(tj + 1) * 512],
                                              in_=GB[my_g1][tj][:, :])
                            v.drain().then_inc(s_dvA, 1)
                        cwb_dv = cdv.inc()
                        grp_guard[gi] = [("dv", cwb_dv)]

                    w3mm = {}
                    ev_batch = []
                    for m in range(16):
                        ci = m // 8
                        mm_ = m % 8
                        gi = nextgrp(); gl = grp_guard[gi]; grp = GB[gi]

                        @blk.tensor
                        def _(t):
                            if pp == 0 and m == 0:
                                t.wait_ge(s_wGb, w3_ld)
                            twait(t, gl)
                            last = None
                            for k in range(KD):
                                for tj in range(2):
                                    last = t.matmul(grp[tj][:, :],
                                                    wslG[ci][:, k * 1024 + mm_ * 128:k * 1024 + (mm_ + 1) * 128],
                                                    ln4b[:, k * TLOC + PTI[tj] * 512:k * TLOC + (PTI[tj] + 1) * 512],
                                                    start=(k == 0), stop=(k == KD - 1))
                            last.then_inc(s_mmA, 1)
                        w3mm[m] = cmm.inc()
                        my_m, my_gi = m, gi

                        @blk.vector
                        def _(v):
                            v.wait_ge(s_mmA, w3mm[my_m])
                            for tj in range(2):
                                c = (my_m * 2 + tj) * 512
                                v.scalar_tensor_tensor(h1eG[:, c:c + 512], GB[my_gi][tj][:, :],
                                                       b3cG[:, bi * 16 + my_m:bi * 16 + my_m + 1],
                                                       h1eG[:, c:c + 512], ALU.add, ALU.mult)
                            if my_m % 2 == 1:
                                v.drain().then_inc(s_dvA, 2)
                        ev_batch.append(gi)
                        if m % 2 == 1:
                            dv = cdv.inc(2)
                            for gg in ev_batch:
                                grp_guard[gg] = [("dv", dv)]
                            ev_batch = []
                        if m == 7:
                            @blk.vector
                            def _(v):
                                v.tensor_single_scalar(h1eG[:, 0:16 * 512], h1eG[:, 0:16 * 512],
                                                       LIM, ALU.min)
                                v.drain().then_inc(s_dvA, 1)
                            glv_dv = cdv.inc()

                            @blk.scalar
                            def _(sc):
                                sc.wait_ge(s_dvA, glv_dv)
                                sc.activation(out=asvG[:], in_=h1eG[:, 0:16 * 512],
                                              func=AF.Sigmoid, scale=1.702)
                                sc.drain().then_inc(s_evA, 1)
                            sg_ev = cev.inc()
                    w3mm_last = cmm.v

                    @blk.vector
                    def _(v):
                        v.tensor_scalar(out=h1eG[:, 16 * 512:32 * 512], in0=h1eG[:, 16 * 512:32 * 512],
                                        scalar1=LIM, scalar2=-LIM, op0=ALU.min, op1=ALU.max)
                        v.drain()
                        v.tensor_single_scalar(h1eG[:, 16 * 512:32 * 512], h1eG[:, 16 * 512:32 * 512],
                                               1.0, ALU.add)
                        v.drain()
                        v.wait_ge(s_evA, sg_ev)
                        v.tensor_mul(asvG[:], asvG[:], h1eG[:, 0:16 * 512])
                        v.drain()
                        v.tensor_mul(asvG[:, 0:8 * 512], asvG[:, 0:8 * 512],
                                     h1eG[:, 16 * 512:24 * 512])
                        v.drain().then_inc(s_dvA, 1)
                        v.tensor_mul(asvG[:, 8 * 512:16 * 512], asvG[:, 8 * 512:16 * 512],
                                     h1eG[:, 24 * 512:32 * 512])
                        v.drain().then_inc(s_dvA, 1)
                    ae_dv1 = cdv.inc()
                    ae_dv = cdv.inc()

                    ev_batch = []
                    sc_batch = []
                    for mo in range(KD):
                        gi = nextgrp(); gl = grp_guard[gi]; grp = GB[gi]

                        @blk.tensor
                        def _(t):
                            if mo == 0:
                                t.wait_ge(s_dvA, ae_dv1)
                                if pp == 0:
                                    t.wait_ge(s_wG[1], w2_ld)
                            twait(t, gl)
                            last = None
                            for k in range(4):
                                for tj in range(2):
                                    last = t.matmul(grp[tj][:, :],
                                                    w2sG[:, k * D + mo * 128:k * D + (mo + 1) * 128],
                                                    asvG[:, (k * 2 + tj) * 512:(k * 2 + tj + 1) * 512],
                                                    start=(k == 0), stop=False)
                            if mo == 0:
                                last.then_inc(s_mmA, 1)
                        if mo == 0:
                            cmm.inc()

                        @blk.tensor
                        def _(t):
                            if mo == 0:
                                t.wait_ge(s_dvA, ae_dv)
                            last = None
                            for k in range(4, KD):
                                for tj in range(2):
                                    last = t.matmul(grp[tj][:, :],
                                                    w2sG[:, k * D + mo * 128:k * D + (mo + 1) * 128],
                                                    asvG[:, (k * 2 + tj) * 512:(k * 2 + tj + 1) * 512],
                                                    start=False, stop=False)
                            for tj in range(2):
                                last = t.matmul(grp[tj][:, :], b2rG[0:1, mo * 128:(mo + 1) * 128],
                                                ones5[0:1, 0:512], start=False, stop=True)
                            last.then_inc(s_mmA, 1)
                        ymm = cmm.inc()
                        my_mo, my_gi = mo, gi

                        if e < 0:
                            @blk.scalar
                            def _(sc):
                                sc.wait_ge(s_mmA, ymm)
                                for tj in range(2):
                                    sc.copy(out=ysbG[:, my_mo * TLOC + PTI[tj] * 512:my_mo * TLOC + (PTI[tj] + 1) * 512],
                                            in_=GB[my_gi][tj][:, :])
                                if my_mo % 2 == 1:
                                    sc.drain().then_inc(s_evA, 2)
                            sc_batch.append(gi)
                            if mo % 2 == 1:
                                ev = cev.inc(2)
                                for gg in sc_batch:
                                    grp_guard[gg] = [("ev", ev)]
                                sc_batch = []
                        else:
                            @blk.vector
                            def _(v):
                                v.wait_ge(s_mmA, ymm)
                                for tj in range(2):
                                    v.tensor_mul(ywG[:], GB[my_gi][tj][:, :],
                                                 cweG[:, tj * 512:(tj + 1) * 512])
                                    v.drain()
                                    v.tensor_add(ysbG[:, my_mo * TLOC + PTI[tj] * 512:my_mo * TLOC + (PTI[tj] + 1) * 512],
                                                 ysbG[:, my_mo * TLOC + PTI[tj] * 512:my_mo * TLOC + (PTI[tj] + 1) * 512],
                                                 ywG[:])
                                if my_mo % 2 == 1:
                                    v.drain().then_inc(s_dvA, 2)
                            ev_batch.append(gi)
                            if mo % 2 == 1:
                                dv = cdv.inc(2)
                                for gg in ev_batch:
                                    grp_guard[gg] = [("dv", dv)]
                                ev_batch = []
                w1_freeG[ei] = w1mm_last
                w3_freeG[ei] = w3mm_last
                w2_freeG[ei] = cmm.v

            g4_dv = cdv.v
            g4_mm = cmm.v
            g4_ev = cev.v

        _scope_next(nc, "phG5")
        with ExitStack() as g5ctx:
            xb5 = [g5ctx.enter_context(nc.sbuf_tensor(f"xb5F{i}", [128, KD * 512], F32, side="right")) for i in range(2)]
            yb5 = g5ctx.enter_context(nc.sbuf_tensor("yb5F", [128, KD * 512], F32, side="right"))
            s_x5 = [sem("s_x50"), sem("s_x51")]
            cx5 = [Cnt(), Cnt()]
            s_f5 = [sem("s_f50"), sem("s_f51")]
            cf5 = [Cnt(), Cnt()]
            fin_prev = {}
            for ti, (c0, w) in enumerate(T_TILES):
                bi = ti % 2
                st_guard = cf5[bi].v

                @blk.sync
                def _(s):
                    if ti == 0:
                        s.wait_ge(s_dvA, g4_dv)
                        s.wait_ge(s_mmA, g4_mm)
                        s.wait_ge(s_evA, g4_ev)
                    s.wait_ge(s_sfG[0], g1_sf[0])
                    s.wait_ge(s_sfG[1], g1_sf[1])
                    if ti >= 2:
                        s.wait_ge(s_f5[bi], st_guard)
                        s.wait_ge(s_dvA, fin_prev[ti - 2])
                    s.dma_start(out=xb5[bi][:].rearrange("p (k t) -> p k t", k=KD),
                                in_=_r_pk(x3_d)[:, :, c0:c0 + w]).then_inc(s_x5[bi], 16)
                x_ld = cx5[bi].inc(16)

                @blk.vector
                def _(v):
                    v.wait_ge(s_x5[bi], x_ld)
                    if ti >= 1:
                        v.wait_ge(s_dvA, fin_prev[ti - 1])
                    for k in range(KD):
                        v.tensor_copy(out=yb5[:, k * 512:k * 512 + w],
                                      in_=ysbG[:, k * TLOC + c0:k * TLOC + c0 + w])
                        v.drain()
                        v.tensor_add(xb5[bi][:, k * 512:k * 512 + w],
                                     xb5[bi][:, k * 512:k * 512 + w], yb5[:, k * 512:k * 512 + w])
                        v.drain()
                    v.engine_nop().then_inc(s_dvA, 1)
                fin_done = cdv.inc()

                @blk.sync
                def _(s):
                    s.wait_ge(s_dvA, fin_done)
                    s.dma_start(out=fT[:, c0:c0 + w].rearrange("(k p) t -> p k t", p=128),
                                in_=xb5[bi][:].rearrange("p (k t) -> p k t", k=KD)[:, :, 0:w]
                                ).then_inc(s_f5[bi], 16)
                cf5[bi].inc(16)
                fin_prev[ti] = cdv.v

            @blk.sync
            def _(s):
                s.wait_ge(s_f5[0], cf5[0].v)
                s.wait_ge(s_f5[1], cf5[1].v)
        gR_ctx.close()
        _scope_next(nc, None)

    return nc


def prep_FUSED_inmaps(inputs):
    x = np.asarray(inputs["L0_attn_x"])
    winT = np.ascontiguousarray(inputs["L0_attn_W_in"].T)
    wgT = np.ascontiguousarray(inputs["L0_attn_W_g"].T)
    cwr = np.ascontiguousarray(
        inputs["L0_attn_conv_w"].reshape(2048, 4).reshape(16, 128, 4).transpose(1, 0, 2).reshape(128, 64))
    cbr = _part_cols(inputs["L0_attn_conv_b"], 16)
    bgr = _part_cols(inputs["L0_attn_b_g"], 32)
    fbr = _part_cols(inputs["L0_attn_forget_base"], 16)
    woutT = np.ascontiguousarray(inputs["L0_attn_W_out"].T)
    wg0 = np.ascontiguousarray(inputs["L0_ffn_Wg"])
    ident = np.eye(128, dtype=np.float32)
    w0 = np.ascontiguousarray(inputs["L1_attn_w0"])
    w2a = np.ascontiguousarray(inputs["L1_attn_w2"])
    w1a = np.ascontiguousarray(inputs["L1_attn_w1"])
    perm = np.concatenate([np.arange(0, 2048, 2), np.arange(1, 2048, 2)])
    w1 = inputs["L1_ffn_w1"]; w3 = inputs["L1_ffn_w3"]; w2 = inputs["L1_ffn_w2"]
    w1pT = np.ascontiguousarray(np.stack([w1[e].T[:, perm] for e in range(8)]))
    w3pT = np.ascontiguousarray(np.stack([w3[e].T[:, perm] for e in range(8)]))
    w2T = np.ascontiguousarray(np.stack([w2[e].T for e in range(8)]))
    b1p = np.ascontiguousarray(inputs["L1_ffn_b1"][:, perm])
    b3p = np.ascontiguousarray(inputs["L1_ffn_b3"][:, perm])
    sw1pT = np.ascontiguousarray(inputs["L1_ffn_sw1"].T[:, perm])
    sw3pT = np.ascontiguousarray(inputs["L1_ffn_sw3"].T[:, perm])
    sw2T = np.ascontiguousarray(inputs["L1_ffn_sw2"].T)
    sb1p = np.ascontiguousarray(inputs["L1_ffn_sb1"][perm][None, :])
    sb3p = np.ascontiguousarray(inputs["L1_ffn_sb3"][perm][None, :])
    sb2r = np.ascontiguousarray(inputs["L1_ffn_sb2"][None, :])
    gwT = np.ascontiguousarray(inputs["L1_ffn_gate_w"].T)
    gbr = np.ascontiguousarray(inputs["L1_ffn_gate_b"][None, :])

    in_maps = []
    for c in range(NCORES):
        b, q = c // 4, c % 4
        t0 = q * TLOC
        xh = np.zeros((THAL, D), dtype=np.float32)
        lo = max(t0 - 3, 0)
        xh[3 - (t0 - lo):] = x[b, lo:t0 + TLOC]
        mask = np.zeros((8, 1), dtype=np.float32)
        mask[:c] = 1.0
        mskp = np.zeros((128, 8), dtype=np.float32)
        for r in range(b * 4, c):
            mskp[:, r] = 1.0
        eoff = np.array([[(b * 4 + a) * 1024 * TLOC + 128 * q for a in range(4)]], dtype=np.int32)
        goff = np.array([[(4 * b + jm) * 1024 * TLOC + 512 * q for jm in range(4)]], dtype=np.int32)
        in_maps.append({
            "xT": np.ascontiguousarray(xh.T),
            "winT": winT, "wgT": wgT, "conv_w_r": cwr, "conv_b_r": cbr, "bg_r": bgr, "fb_r": fbr,
            "woutT": woutT, "wg0_i": wg0, "ident_i": ident, "mask_i": mask, "mskp_i": mskp,
            "W1_i": inputs["L0_ffn_W1"], "W3_i": inputs["L0_ffn_W3"], "W2_i": inputs["L0_ffn_W2"],
            "Ws1_i": inputs["L0_ffn_Ws1"], "Ws3_i": inputs["L0_ffn_Ws3"], "Ws2_i": inputs["L0_ffn_Ws2"],
            "w0_i": w0, "w2a_i": w2a, "w1a_i": w1a,
            "gwT_i": gwT, "gb_i": gbr,
            "w1p_i": w1pT, "b1p_i": b1p, "w3p_i": w3pT, "b3p_i": b3p,
            "w2t_i": w2T, "b2_i": inputs["L1_ffn_b2"],
            "sw1p_i": sw1pT, "sb1p_i": sb1p, "sw3p_i": sw3pT, "sb3p_i": sb3p,
            "sw2t_i": sw2T, "sb2_i": sb2r,
            "eoff_i": eoff, "goff_i": goff,
        })
    return in_maps


def kernel_fused(**inputs):
    inputs = {k: np.asarray(v) for k, v in inputs.items()}
    nc = build_FUSED()
    res = run_bass_kernel_spmd(nc, prep_FUSED_inmaps(inputs), core_ids=list(range(NCORES))).results
    out = np.empty((2, 8192, D), dtype=np.float32)
    for c in range(NCORES):
        b, q = c // 4, c % 4
        out[b, q * TLOC:(q + 1) * TLOC, :] = np.asarray(res[c]["fT"]).T
    return out


def kernel(**inputs):
    return kernel_fused(**inputs)
